# version 09
# baseline (speedup 1.0000x reference)
import math

import numpy as np
import ml_dtypes
import concourse.bass as bass
import concourse.mybir as mybir
from concourse.bass_utils import run_bass_kernel_spmd

F32 = mybir.dt.float32
BF16 = mybir.dt.bfloat16
I32 = mybir.dt.int32
AF = mybir.ActivationFunctionType
ALU = mybir.AluOpType
AX = mybir.AxisListType

B, S, D, H, DH = 2, 4096, 2048, 16, 128
QKVC = 3 * D
HID, FF = 5632, 11264
TOK, NT = 1024, 8
NCH = D // 128
NHCH = HID // 128
QSCALE = 1.0 / math.sqrt(DH)
MAGIC = float(np.float32(12582912.0))
SHIFT = 12.0
EPS = 1e-5
GROUPS = [[0, 1, 2, 3], [4, 5, 6, 7]]

TAPS = []
BUILD_UPTO = 99
DO_COLLECTIVES = True


def _crh(ck):
    return (ck, 0) if ck < 4 else (7 - ck, 1)



def _pck(dram_ap, p=128):
    return dram_ap.rearrange("(c p) k -> p c k", p=p)


def _sck(sbuf_ap, c):
    return sbuf_ap.rearrange("p (c k) -> p c k", c=c)



def _ap_span(ap):
    try:
        name = ap.tensor.name
    except Exception:
        name = str(ap.tensor)
    off = ap.offset
    reach = 1
    for step, cnt in ap.ap:
        reach += abs(step) * (cnt - 1)
    return (name, off, off + reach)


class VDrain:

    WINDOW = 5

    def __init__(self, ve):
        self._ve = ve
        self._hist = []

    def _chk(self, ins, outs):
        spans = [_ap_span(a) for a in ins + outs if a is not None]
        hit = False
        for (n, lo, hi) in spans:
            for (n2, lo2, hi2) in self._hist:
                if n == n2 and lo < hi2 and lo2 < hi:
                    hit = True
                    break
            if hit:
                break
        if hit:
            self._ve.drain()
            self._hist = []

    def _rec(self, outs):
        self._hist.extend(_ap_span(a) for a in outs if a is not None)
        self._hist = self._hist[-self.WINDOW:]

    def _emit(self, fn, ins, outs, *args, **kw):
        self._chk(ins, outs)
        r = fn(*args, **kw)
        self._rec(outs)
        return r

    def tensor_copy(self, out, in_):
        return self._emit(self._ve.tensor_copy, [in_], [out], out, in_)

    def tensor_add(self, out, a, b):
        return self._emit(self._ve.tensor_add, [a, b], [out], out, a, b)

    def tensor_sub(self, out, a, b):
        return self._emit(self._ve.tensor_sub, [a, b], [out], out, a, b)

    def tensor_mul(self, out, a, b):
        return self._emit(self._ve.tensor_mul, [a, b], [out], out, a, b)

    def tensor_scalar_mul(self, out, a, sc):
        ins = [a] + ([sc] if not isinstance(sc, float) else [])
        return self._emit(self._ve.tensor_scalar_mul, ins, [out], out, a, sc)

    def tensor_scalar_add(self, out, a, sc):
        ins = [a] + ([sc] if not isinstance(sc, float) else [])
        return self._emit(self._ve.tensor_scalar_add, ins, [out], out, a, sc)

    def tensor_scalar_max(self, out, a, sc):
        ins = [a] + ([sc] if not isinstance(sc, float) else [])
        return self._emit(self._ve.tensor_scalar_max, ins, [out], out, a, sc)

    def reciprocal(self, out, in_):
        return self._emit(self._ve.reciprocal, [in_], [out], out, in_)

    def reduce_sum(self, out, in_, axis, **kw):
        return self._emit(self._ve.reduce_sum, [in_], [out], out, in_, axis=axis, **kw)

    def reduce_max(self, out, in_, axis, **kw):
        return self._emit(self._ve.reduce_max, [in_], [out], out, in_, axis=axis, **kw)

    def scalar_tensor_tensor(self, out, in0, sc, in1, op0, op1):
        ins = [in0, in1] + ([sc] if not isinstance(sc, float) else [])
        return self._emit(self._ve.scalar_tensor_tensor, ins, [out],
                          out, in0, sc, in1, op0, op1)

    def memset(self, ap, v):
        return self._emit(self._ve.memset, [], [ap], ap, v)

    def wait_ge(self, sem, v):
        return self._ve.wait_ge(sem, v)

    def drain(self):
        self._hist = []
        return self._ve.drain()

    @property
    def engine(self):
        return self._ve.engine


class Ctr:
    __slots__ = ("v",)

    def __init__(self):
        self.v = 0

    def inc(self, n=1):
        self.v += n
        return self.v


class MK:
    def __init__(self):
        nc = bass.Bass("TRN2", target_bir_lowering=False,
                       dynamic_dma_scratch_size=16384)
        self.nc = nc
        self.stack = []
        self.taps = {}
        self.rid = nc.dram_tensor("rid", [1, 1], I32, kind="ExternalInput").ap()
        self.x_d = nc.dram_tensor("x", [TOK, D], F32, kind="ExternalInput").ap()
        self.wqkp_d = nc.dram_tensor("wqkp", [8, 128, 4096], F32, kind="ExternalInput").ap()
        self.wqvp_d = nc.dram_tensor("wqvp", [4, 128, 8192], F32, kind="ExternalInput").ap()
        self.wqqp_d = nc.dram_tensor("wqqp", [8, 128, 4096], F32, kind="ExternalInput").ap()
        self.bqkv_d = nc.dram_tensor("bqkvT", [128, 48], F32, kind="ExternalInput").ap()
        self.wo_d = nc.dram_tensor("wo", [D, D], F32, kind="ExternalInput").ap()
        self.bo_d = nc.dram_tensor("bo_rep", [128, D], F32, kind="ExternalInput").ap()
        self.w13p_d = nc.dram_tensor("w13p", [22, 2, 128, 4096], F32, kind="ExternalInput").ap()
        self.w2p_d = nc.dram_tensor("w2p", [4, 6, 128, 4096], F32, kind="ExternalInput").ap()
        self.nw_d = nc.dram_tensor("nw_rep", [128, HID], F32, kind="ExternalInput").ap()
        self.alibi_d = nc.dram_tensor("alibi", [128, 512], F32, kind="ExternalInput").ap()
        self.masks_d = nc.dram_tensor("masks", [128, 2048], F32, kind="ExternalInput").ap()
        self.id_d = nc.dram_tensor("ident", [128, 128], BF16, kind="ExternalInput").ap()
        self.aq_d = nc.dram_tensor("aq", [32, 512], BF16, kind="ExternalInput").ap()
        self.out_d = nc.dram_tensor("out", [TOK, D], F32, kind="ExternalOutput").ap()

        self.kvin = nc.dram_tensor("kvin", [2 * D, TOK], BF16, kind="Internal").ap()
        self.kvout = nc.dram_tensor("kvout", [8 * D, TOK], BF16, kind="Internal").ap()
        self.kin = self.kvin[0:D, :]
        self.vin = self.kvin[D:2 * D, :].rearrange("(t two) k -> t (two k)", two=2)
        self.qts = nc.dram_tensor("qts", [D, TOK], BF16, kind="Internal").ap()
        self.xqs = nc.dram_tensor("xqs", [D, TOK], BF16, kind="Internal").ap()
        self.gsp = nc.dram_tensor("gsp", [TOK, D], F32, kind="Internal").ap()
        for name, shape, dt in TAPS:
            self.taps[name.replace("tap_", "")] = nc.dram_tensor(
                name, shape, dt, kind="ExternalOutput").ap()

        self.te, self.ve, self.se = nc.tensor, VDrain(nc.vector), nc.scalar
        self.sy, self.gp = nc.sync, nc.gpsimd

        self.sSY = nc.semaphore("sSY").__enter__()
        self.sGP = nc.semaphore("sGP").__enter__()
        self.sCC = nc.semaphore("sCC").__enter__()
        self.sTE = nc.semaphore("sTE").__enter__()
        self.sVE = nc.semaphore("sVE").__enter__()
        self.sSE = nc.semaphore("sSE").__enter__()
        self.cSY, self.cGP, self.cCC = Ctr(), Ctr(), Ctr()
        self.cTE, self.cVE, self.cSE = Ctr(), Ctr(), Ctr()

    def sb(self, name, shape, dtype):
        g = self.nc.sbuf_tensor(name, shape, dtype)
        t = g.__enter__()
        self.stack.append((g, name))
        return t

    def ps(self, name, shape, dtype):
        g = self.nc.psum_tensor(name, shape, dtype)
        t = g.__enter__()
        self.stack.append((g, name))
        return t

    def free(self, *names):
        for name in names:
            g, n = self.stack.pop()
            assert n == name, f"free order: got {n}, want {name}"
            g.__exit__(None, None, None)

    def phase_barrier(self):
        s = self
        targets = [(s.sSY, s.cSY.v), (s.sTE, s.cTE.v),
                   (s.sVE, s.cVE.v), (s.sSE, s.cSE.v)]
        for eng in (s.te, s.ve, s.se, s.sy, s.gp):
            for sem, v in targets:
                if v:
                    eng.wait_ge(sem, v)

    def dma(self, eng, out, in_):
        if eng is self.sy:
            eng.dma_start(out=out, in_=in_).then_inc(self.sSY, 16)
            return self.cSY.inc(16)
        eng.dma_start(out=out, in_=in_).then_inc(self.sGP, 16)
        return self.cGP.inc(16)

    def build(self):
        self.p0_consts()
        if BUILD_UPTO >= 1:
            self.p1_layernorm()
        if BUILD_UPTO >= 2:
            self.p2_qkv()
        if BUILD_UPTO >= 4:
            self.p4_ffn()
        if BUILD_UPTO >= 5:
            self.p5_attention()
        if BUILD_UPTO >= 6:
            self.p6_combine()
        self.finish()
        return self.nc

    def p0_consts(self):
        s = self
        s.c_nw = s.sb("c_nw", [128, HID], F32)
        s.c_id = s.sb("c_id", [128, 128], BF16)
        s.c_bqkv = s.sb("c_bqkv", [128, 48], F32)
        s.c_bqs = s.sb("c_bqs", [128, 16], F32)
        s.c_ones = s.sb("c_ones", [128, 1], BF16)
        s.rid_s = s.sb("rid_s", [1, 1], I32)
        s.sent_s = s.sb("sent_s", [1, 1], I32)
        s.sxr = s.sb("sxr", [128, NT], F32)
        s.s2r = s.sb("s2r", [128, NT], F32)
        s.stats = s.sb("stats", [128, 96], F32)

        s.dma(s.sy, s.c_nw[:], s.nw_d[:, :])
        s.dma(s.sy, s.c_id[:], s.id_d[:, :])
        s.dma(s.sy, s.c_bqkv[:], s.bqkv_d[:, :])
        s.rid_rec = s.dma(s.sy, s.rid_s[:], s.rid[:, :])
        s.p0_loads = s.cSY.v

        s.pc_w13 = {}

        s.ve.wait_ge(s.sSY, s.p0_loads)
        s.ve.memset(s.c_ones[:], 1.0)
        s.ve.tensor_scalar_mul(s.c_bqs[:], s.c_bqkv[:, 0:16], QSCALE).then_inc(s.sVE, 1)
        s.p0_ve = s.cVE.inc()

    def p1_layernorm(self):
        s = self
        te, ve, se, sy = s.te, s.ve, s.se, s.sy
        sTE, sVE, sSE, sSY = s.sTE, s.sVE, s.sSE, s.sSY
        cTE, cVE, cSE = s.cTE, s.cVE, s.cSE
        st = s.stats

        s.xnT = s.sb("xnT", [128, NCH * TOK], BF16)
        x_all = s.sb("x_all", [128, NT * D], F32)
        dump = s.sb("dump", [128, D], F32)
        xnf = [s.sb(f"xnf{i}", [128, D], F32) for i in range(2)]
        xnb = [s.sb(f"xnb{i}", [128, D], BF16) for i in range(2)]
        xqb = [s.sb(f"xqb{i}", [128, D], BF16) for i in range(2)]
        xqstg = [s.sb(f"xqstg{i}", [128, 512], BF16) for i in range(2)]
        tpb = [s.ps(f"tpb{i}", [128, 1024], BF16) for i in range(2)]

        x_recs = [s.dma(sy, x_all[:, t * D:(t + 1) * D],
                        s.x_d[t * 128:(t + 1) * 128, :]) for t in range(NT)]
        lastve = lastse = None
        for t in range(NT):
            xt = x_all[:, t * D:(t + 1) * D]
            ve.wait_ge(sSY, x_recs[t])
            lastve = ve.reduce_sum(st[:, t:t + 1], xt, axis=AX.X)
            se.wait_ge(sSY, x_recs[t])
            lastse = se.activation(out=dump[:], in_=xt, func=AF.Square,
                                   accum_out=st[:, 8 + t:9 + t])
        lastve.then_inc(sVE, 1)
        v_sums = cVE.inc()
        lastse.then_inc(sSE, 1)
        s_sq = cSE.inc()

        SU, SQ, ME, VA, RV, IV, NB = 0, 8, 16, 24, 32, 40, 48
        SM, SC, RS = 56, 64, 72
        ve.wait_ge(sSE, s_sq)
        ve.tensor_scalar_mul(st[:, ME:ME + 8], st[:, SU:SU + 8], 1.0 / D)
        ve.tensor_scalar_mul(st[:, VA:VA + 8], st[:, SQ:SQ + 8], 1.0 / D)
        ve.tensor_mul(st[:, NB:NB + 8], st[:, ME:ME + 8], st[:, ME:ME + 8])
        ve.tensor_sub(st[:, VA:VA + 8], st[:, VA:VA + 8], st[:, NB:NB + 8])
        ve.tensor_scalar_add(st[:, VA:VA + 8], st[:, VA:VA + 8], EPS)
        ve.reciprocal(st[:, RV:RV + 8], st[:, VA:VA + 8]).then_inc(sVE, 1)
        v_rv = cVE.inc()
        se.wait_ge(sVE, v_rv)
        se.activation(out=st[:, IV:IV + 8], in_=st[:, RV:RV + 8],
                      func=AF.Sqrt).then_inc(sSE, 1)
        s_inv = cSE.inc()
        ve.wait_ge(sSE, s_inv)
        ve.tensor_scalar_mul(st[:, NB:NB + 8], st[:, ME:ME + 8], -1.0)
        ve.tensor_mul(st[:, NB:NB + 8], st[:, NB:NB + 8],
                      st[:, IV:IV + 8]).then_inc(sVE, 1)
        v_nb = cVE.inc()

        stg_free = [None, None]
        tp_free = [None, None]
        dump_free = None
        batch_i = 0
        xnf_free = [None, None]
        xnb_free = [None, None]
        xqb_free = [None, None]
        smax_rec = {}
        rnd_rec = {}
        for t in range(NT):
            p = t % 2
            se.wait_ge(sVE, v_nb)
            if xnf_free[p] is not None:
                se.wait_ge(sVE, xnf_free[p])
            se.activation(out=xnf[p][:], in_=x_all[:, t * D:(t + 1) * D],
                          func=AF.Identity, scale=st[:, IV + t:IV + t + 1],
                          bias=st[:, NB + t:NB + t + 1]).then_inc(sSE, 1)
            s_xnf = cSE.inc()
            ve.wait_ge(sSE, s_xnf)
            if xnb_free[p] is not None:
                ve.wait_ge(sTE, xnb_free[p])
            ve.tensor_copy(xnb[p][:], xnf[p][:])
            ve.reduce_max(st[:, SM + t:SM + t + 1], xnf[p][:], axis=AX.X,
                          apply_absolute_value=True)
            ve.tensor_scalar_max(st[:, SC + t:SC + t + 1],
                                 st[:, SM + t:SM + t + 1], 1e-5)
            ve.reciprocal(st[:, RS + t:RS + t + 1], st[:, SC + t:SC + t + 1])
            ve.tensor_scalar_mul(st[:, RS + t:RS + t + 1],
                                 st[:, RS + t:RS + t + 1], 127.0)
            ve.tensor_scalar_mul(s.sxr[:, t:t + 1], st[:, SC + t:SC + t + 1],
                                 1.0 / 127.0).then_inc(sVE, 1)
            v_qs = cVE.inc()
            smax_rec[t] = v_qs
            se.wait_ge(sVE, v_qs)
            if dump_free is not None:
                se.wait_ge(sVE, dump_free)
            se.activation(out=dump[:], in_=xnf[p][:], func=AF.Copy,
                          scale=st[:, RS + t:RS + t + 1], bias=MAGIC).then_inc(sSE, 1)
            s_rnd = cSE.inc()
            ve.wait_ge(sSE, s_rnd)
            if xqb_free[p] is not None:
                ve.wait_ge(sTE, xqb_free[p])
            ve.tensor_scalar_add(xqb[p][:], dump[:], -MAGIC).then_inc(sVE, 1)
            v_xq = cVE.inc()
            dump_free = v_xq
            xnf_free[p] = v_xq

            te.wait_ge(sVE, v_xq)
            for src_buf, is_xq in ((xnb[p], False), (xqb[p], True)):
                for b0 in range(0, NCH, 4):
                    bank = batch_i % 2
                    batch_i += 1
                    if tp_free[bank] is not None:
                        te.wait_ge(sVE, tp_free[bank])
                    last = None
                    for j in range(4):
                        cix = b0 + j
                        last = te.transpose(tpb[bank][:, j * 128:(j + 1) * 128],
                                            src_buf[:, cix * 128:(cix + 1) * 128],
                                            s.c_id[:])
                    last.then_inc(sTE, 1)
                    t_b = cTE.inc()
                    ve.wait_ge(sTE, t_b)
                    if is_xq:
                        par = bank
                        if stg_free[par] is not None:
                            ve.wait_ge(sSY, stg_free[par])
                        ve.tensor_copy(xqstg[par][:], tpb[bank][:, 0:512]).then_inc(sVE, 1)
                        v_e = cVE.inc()
                        tp_free[bank] = v_e
                        sy.wait_ge(sVE, v_e)
                        dst = _pck(s.xqs[b0 * 128:(b0 + 4) * 128,
                                         t * 128:(t + 1) * 128])
                        stg_free[par] = s.dma(sy, dst, _sck(xqstg[par][:], 4))
                    else:
                        last = None
                        for j in range(4):
                            cix = b0 + j
                            last = ve.tensor_copy(
                                s.xnT[:, cix * TOK + t * 128: cix * TOK + (t + 1) * 128],
                                tpb[bank][:, j * 128:(j + 1) * 128])
                        last.then_inc(sVE, 1)
                        tp_free[bank] = cVE.inc()
                xnb_free[p] = cTE.v if not is_xq else xnb_free[p]
                xqb_free[p] = cTE.v if is_xq else xqb_free[p]

        if "xnT" in s.taps:
            sy.wait_ge(sVE, cVE.v)
            s.dma(sy, s.taps["xnT"][:, :], s.xnT[:])
        if "sxr" in s.taps:
            sy.wait_ge(sVE, cVE.v)
            s.dma(sy, s.taps["sxr"][:, :], s.sxr[:])
        if "xqs" in s.taps:
            sy.wait_ge(sSY, s.cSY.v)
            s.dma(sy, s.taps["xqs"][:, :], s.xqs[:, :])
        s.p1 = dict(te=cTE.v, ve=cVE.v, se=cSE.v, sy=s.cSY.v)
        s.free("tpb1", "tpb0", "xqstg1", "xqstg0", "xqb1", "xqb0", "xnb1", "xnb0",
               "xnf1", "xnf0", "dump", "x_all")

    def p2_qkv(self):
        s = self
        s.phase_barrier()
        te, ve, se, sy, gp = s.te, s.ve, s.se, s.sy, s.gp
        sTE, sVE, sSE, sSY, sGP = s.sTE, s.sVE, s.sSE, s.sSY, s.sGP
        cTE, cVE, cSE = s.cTE, s.cVE, s.cSE

        wbuf = [s.sb(f"wbuf{i}", [128, NCH * 512], BF16) for i in range(2)]
        wf32 = [s.sb(f"wf32{i}", [128, 4096], F32) for i in range(2)]
        kstg = [s.sb(f"kstg{i}", [128, TOK], BF16) for i in range(2)]
        vstg = [s.sb(f"vstg{i}", [128, 512], BF16) for i in range(2)]
        pb = [s.ps(f"pb{i}", [128, 512], F32) for i in range(8)]

        te.wait_ge(sVE, s.p1["ve"])

        wbuf_free = [None, None]
        kstg_free = [None, None]
        vstg_free = [None, None]
        bank_free = [None] * 8
        kin_last = vin_last = 0

        wf32_free = [None, None]

        def stream_w(src_f32_blk, wdst, par, ncols_half):
            rec = None
            for hh, blk in enumerate(src_f32_blk):
                fpar = (par * 2 + hh) % 2
                if wf32_free[fpar] is not None:
                    sy.wait_ge(sVE, wf32_free[fpar])
                d_rec = s.dma(sy, wf32[fpar][:], blk)
                ve.wait_ge(sSY, d_rec)
                if wbuf_free[par] is not None:
                    ve.wait_ge(sTE, wbuf_free[par])
                ve.tensor_copy(wdst[:, hh * 4096:(hh + 1) * 4096],
                               wf32[fpar][:]).then_inc(sVE, 1)
                rec = cVE.inc()
                wf32_free[fpar] = rec
            return rec

        for grp in range(8):
            par = grp % 2
            w_rec = stream_w([s.wqkp_d[grp]], wbuf[par], par, 1)
            te.wait_ge(sVE, w_rec)
            banks = [pb[4 * par + i] for i in range(4)]
            for i in range(4):
                if bank_free[4 * par + i] is not None:
                    sem, v = bank_free[4 * par + i]
                    te.wait_ge(sem, v)
            last = None
            for c in range(NCH):
                for m in range(2):
                    w_ap = wbuf[par][:, c * 256 + m * 128: c * 256 + (m + 1) * 128]
                    for sp in range(2):
                        last = te.matmul(
                            banks[2 * m + sp][:], w_ap,
                            s.xnT[:, c * TOK + sp * 512: c * TOK + (sp + 1) * 512],
                            start=(c == 0), stop=(c == NCH - 1))
            last.then_inc(sTE, 1)
            t_grp = cTE.inc()
            wbuf_free[par] = t_grp
            for m in range(2):
                mk = 2 * grp + m
                se.wait_ge(sTE, t_grp)
                if kstg_free[m] is not None:
                    se.wait_ge(sSY, kstg_free[m])
                lastse = None
                for sp in range(2):
                    lastse = se.activation(
                        out=kstg[m][:, sp * 512:(sp + 1) * 512],
                        in_=banks[2 * m + sp][:], func=AF.Identity,
                        bias=s.c_bqkv[:, 16 + mk:17 + mk], scale=1.0)
                lastse.then_inc(sSE, 1)
                s_ev = cSE.inc()
                for sp in range(2):
                    bank_free[4 * par + 2 * m + sp] = (sSE, s_ev)
                sy.wait_ge(sSE, s_ev)
                rec = s.dma(sy, s.kin[mk * 128:(mk + 1) * 128, :], kstg[m][:])
                kstg_free[m] = rec
                kin_last = rec

        for vs in range(4):
            par = vs % 2
            w_rec = stream_w([s.wqvp_d[vs, :, 0:4096], s.wqvp_d[vs, :, 4096:8192]],
                             wbuf[par], par, 2)
            te.wait_ge(sVE, w_rec)
            for tg in range(2):
                banks = [pb[4 * tg + i] for i in range(4)]
                for i in range(4):
                    if bank_free[4 * tg + i] is not None:
                        sem, v = bank_free[4 * tg + i]
                        te.wait_ge(sem, v)
                last = None
                for c in range(NCH):
                    for ti in range(4):
                        t = 4 * tg + ti
                        last = te.matmul(
                            banks[ti][:],
                            s.xnT[:, c * TOK + t * 128: c * TOK + (t + 1) * 128],
                            wbuf[par][:, c * 512:(c + 1) * 512],
                            start=(c == 0), stop=(c == NCH - 1))
                last.then_inc(sTE, 1)
                t_grp = cTE.inc()
                if tg == 1:
                    wbuf_free[par] = t_grp
                for ti in range(4):
                    t = 4 * tg + ti
                    sp = t % 2
                    ve.wait_ge(sTE, t_grp)
                    if vstg_free[sp] is not None:
                        ve.wait_ge(sSY, vstg_free[sp])
                    ve.tensor_copy(vstg[sp][:], banks[ti][:]).then_inc(sVE, 1)
                    v_ev = cVE.inc()
                    bank_free[4 * tg + ti] = (sVE, v_ev)
                    sy.wait_ge(sVE, v_ev)
                    rec = s.dma(sy, s.vin[t * 128:(t + 1) * 128, vs * 512:(vs + 1) * 512],
                                vstg[sp][:])
                    vstg_free[sp] = rec
                    vin_last = rec

        if DO_COLLECTIVES:
            gp.wait_ge(sSY, s.cSY.v)
            gp.wait_ge(sVE, s.cVE.v)
            gp.collective_compute("AllGather", ALU.bypass, replica_groups=GROUPS,
                                  ins=[s.kvin[:, :]], outs=[s.kvout[:, :]]).then_inc(s.sCC, 1)
            s.cc_k = s.cc_v = s.cCC.inc()

        if DO_COLLECTIVES:
            sy.wait_ge(s.sCC, s.cc_v)
        se.wait_ge(sVE, s.p0_ve)
        for grp in range(8):
            par = grp % 2
            w_rec = stream_w([s.wqqp_d[grp]], wbuf[par], par, 1)
            te.wait_ge(sVE, w_rec)
            banks = [pb[4 * par + i] for i in range(4)]
            for i in range(4):
                if bank_free[4 * par + i] is not None:
                    sem, v = bank_free[4 * par + i]
                    te.wait_ge(sem, v)
            last = None
            for c in range(NCH):
                for m in range(2):
                    w_ap = wbuf[par][:, c * 256 + m * 128: c * 256 + (m + 1) * 128]
                    for sp in range(2):
                        last = te.matmul(
                            banks[2 * m + sp][:], w_ap,
                            s.xnT[:, c * TOK + sp * 512: c * TOK + (sp + 1) * 512],
                            start=(c == 0), stop=(c == NCH - 1))
            last.then_inc(sTE, 1)
            t_grp = cTE.inc()
            wbuf_free[par] = t_grp
            for m in range(2):
                mq = 2 * grp + m
                se.wait_ge(sTE, t_grp)
                if kstg_free[m] is not None:
                    se.wait_ge(sSY, kstg_free[m])
                lastse = None
                for sp in range(2):
                    lastse = se.activation(
                        out=kstg[m][:, sp * 512:(sp + 1) * 512],
                        in_=banks[2 * m + sp][:], func=AF.Identity,
                        bias=s.c_bqs[:, mq:mq + 1], scale=QSCALE)
                lastse.then_inc(sSE, 1)
                s_ev = cSE.inc()
                for sp in range(2):
                    bank_free[4 * par + 2 * m + sp] = (sSE, s_ev)
                sy.wait_ge(sSE, s_ev)
                rec = s.dma(sy, s.qts[mq * 128:(mq + 1) * 128, :], kstg[m][:])
                kstg_free[m] = rec

        for nm, buf in (("kin", s.kin), ("vin", s.vin), ("qts", s.qts)):
            if nm in s.taps:
                sy.wait_ge(sSY, s.cSY.v)
                s.dma(sy, s.taps[nm][:, :], buf[:, :])

        s.p2 = dict(te=cTE.v, ve=cVE.v, se=cSE.v, sy=s.cSY.v, gp=s.cGP.v)
        s.free(*[f"pb{i}" for i in reversed(range(8))])
        s.free("vstg1", "vstg0", "kstg1", "kstg0",
               "wf321", "wf320", "wbuf1", "wbuf0", "xnT")

    def p4_ffn(self):
        s = self
        s.phase_barrier()
        te, ve, se, sy, gp = s.te, s.ve, s.se, s.sy, s.gp
        sTE, sVE, sSE, sSY, sGP = s.sTE, s.sVE, s.sSE, s.sSY, s.sGP
        cTE, cVE, cSE = s.cTE, s.cVE, s.cSE

        wsb = [s.sb(f"wsb{i}", [128, 8 * 512], BF16) for i in range(2)]
        wq32 = [s.sb(f"wq32{i}", [128, 640], F32) for i in range(3)]
        xqG = s.sb("xqG", [128, NCH * 512], BF16)
        x1h = s.sb("x1h", [128, 4 * 512], F32)
        tmp2 = [s.sb(f"tmp2{i}", [128, 512], F32) for i in range(2)]
        gstg = tmp2

        wsb_free = [None, None]
        gstg_free = [None, None]
        x1h_free = None
        half_i = 0

        wq32_free = [None, None, None]

        def cast_block(dst_bf, src_f32_blk, ncols, guard):
            rec = None
            CW = 640
            nq = (ncols + CW - 1) // CW
            base = half_i_cast[0]
            half_i_cast[0] += nq
            d_recs = []

            def issue(q):
                w = min(CW, ncols - q * CW)
                fp = (base + q) % 3
                if wq32_free[fp] is not None:
                    sy.wait_ge(sSE, wq32_free[fp])
                d_recs.append(s.dma(sy, wq32[fp][:, 0:w],
                                    src_f32_blk[:, q * CW:q * CW + w]))

            issue(0)
            if nq > 1:
                issue(1)
            rec = None
            for q in range(nq):
                if q + 2 < nq:
                    issue(q + 2)
                if q + 1 < nq:
                    se.wait_ge(sSY, d_recs[q + 1])
                else:
                    sent = s.dma(sy, s.sent_s[:], s.rid[:, :])
                    se.wait_ge(sSY, sent)
                if q == 0 and guard is not None:
                    se.wait_ge(guard[0], guard[1])
                w = min(CW, ncols - q * CW)
                fp = (base + q) % 3
                se.activation(out=dst_bf[:, q * CW:q * CW + w],
                              in_=wq32[fp][:, 0:w],
                              func=AF.Copy).then_inc(sSE, 1)
                rec = cSE.inc()
                wq32_free[fp] = rec
            return rec

        half_i_cast = [0]

        def stream_w13(pass_i, hh):
            nonlocal half_i
            par = half_i % 2
            half_i += 1
            guard = (sTE, wsb_free[par]) if wsb_free[par] is not None else None
            rec = cast_block(wsb[par][:, 0:8 * 512], s.w13p_d[pass_i, hh],
                             8 * 512, guard)
            return par, rec

        for G in range(2):
            iqT = s.sb(f"iqT_{G}", [128, 4 * HID], BF16)
            gated = s.sb(f"gated_{G}", [128, 4 * HID], F32)
            iq_tok = s.sb(f"iq_tok_{G}", [128, 512], BF16)

            sy.wait_ge(s.sTE, s.cTE.v)
            src = _pck(s.xqs[:, G * 512:(G + 1) * 512])
            xq_rec = s.dma(sy, _sck(xqG[:], NCH), src)

            pbF = [s.ps(f"pbF{G}_{i}", [128, 512], F32) for i in range(8)]
            bankF_free = [None] * 8
            te.wait_ge(sSY, xq_rec)
            te.wait_ge(sSE, cSE.v)
            te.wait_ge(sVE, cVE.v)
            se.wait_ge(sSY, s.cSY.v)
            ve.wait_ge(sSY, s.cSY.v)
            for j in range(11):
                for part in range(2):
                    pass_i = 2 * j + part
                    halves = [stream_w13(pass_i, hh) for hh in range(2)]
                    banks = [pbF[4 * part + i] for i in range(4)]
                    for i in range(4):
                        if bankF_free[4 * part + i] is not None:
                            sem, v = bankF_free[4 * part + i]
                            te.wait_ge(sem, v)
                    last = None
                    for c in range(NCH):
                        hh = c // 8
                        par, rec = halves[hh]
                        if c % 8 == 0:
                            te.wait_ge(sSE, rec)
                        w_ap = wsb[par][:, (c % 8) * 512:(c % 8 + 1) * 512]
                        for ti in range(4):
                            last = te.matmul(
                                banks[ti][:],
                                xqG[:, c * 512 + ti * 128: c * 512 + (ti + 1) * 128],
                                w_ap, start=(c == 0), stop=(c == NCH - 1))
                        if c == 7:
                            last.then_inc(sTE, 1)
                            wsb_free[halves[0][0]] = cTE.inc()
                    last.then_inc(sTE, 1)
                    t_pass = cTE.inc()
                    wsb_free[halves[1][0]] = t_pass
                    if part == 0:
                        se.wait_ge(sTE, t_pass)
                        if x1h_free is not None:
                            se.wait_ge(sVE, x1h_free)
                        lastse = None
                        for ti in range(4):
                            lastse = se.activation(out=x1h[:, ti * 512:(ti + 1) * 512],
                                                   in_=banks[ti][:], func=AF.Relu)
                        lastse.then_inc(sSE, 1)
                        s_x1 = cSE.inc()
                        for ti in range(4):
                            bankF_free[ti] = (sSE, s_x1)
                    else:
                        ve.wait_ge(sTE, t_pass)
                        ve.wait_ge(sSE, s_x1)
                        lastve = None
                        for ti in range(4):
                            tb = tmp2[ti % 2]
                            ve.tensor_mul(tb[:], x1h[:, ti * 512:(ti + 1) * 512],
                                          x1h[:, ti * 512:(ti + 1) * 512])
                            lastve = ve.tensor_mul(
                                gated[:, ti * HID + j * 512: ti * HID + (j + 1) * 512],
                                tb[:], banks[ti][:])
                        lastve.then_inc(sVE, 1)
                        v_gat = cVE.inc()
                        x1h_free = v_gat
                        for ti in range(4):
                            bankF_free[4 + ti] = (sVE, v_gat)
            f1a_ve = cVE.v
            s.free(*[f"pbF{G}_{i}" for i in reversed(range(8))])

            tpb = [s.ps(f"tpbF{G}_{i}", [128, 1024], BF16) for i in range(2)]
            tp_free = [None, None]
            batch_i = 0
            tmp2_safe = f1a_ve
            for ti in range(4):
                t = 4 * G + ti
                gt = gated[:, ti * HID:(ti + 1) * HID]
                se.wait_ge(sVE, tmp2_safe)
                lastse = None
                for sl in range(11):
                    lastse = se.activation(out=tmp2[sl % 2][:],
                                           in_=gt[:, sl * 512:(sl + 1) * 512],
                                           func=AF.Square,
                                           accum_out=s.stats[:, 12 + sl:13 + sl])
                lastse.then_inc(sSE, 1)
                s_ssq = cSE.inc()
                ve.wait_ge(sSE, s_ssq)
                ve.reduce_sum(s.stats[:, 24:25], s.stats[:, 12:23], axis=AX.X)
                ve.tensor_mul(gt[:], gt[:], s.c_nw[:])
                ve.reduce_max(s.stats[:, 25:26], gt[:], axis=AX.X,
                              apply_absolute_value=True)
                ve.tensor_mul(s.stats[:, 26:27], s.sxr[:, t:t + 1], s.sxr[:, t:t + 1])
                ve.tensor_mul(s.stats[:, 26:27], s.stats[:, 26:27], s.sxr[:, t:t + 1])
                ve.tensor_mul(s.stats[:, 27:28], s.stats[:, 26:27], s.stats[:, 26:27])
                ve.tensor_mul(s.stats[:, 27:28], s.stats[:, 27:28], s.stats[:, 24:25])
                ve.tensor_scalar_mul(s.stats[:, 27:28], s.stats[:, 27:28], 1.0 / HID)
                ve.tensor_scalar_add(s.stats[:, 27:28], s.stats[:, 27:28], EPS)
                ve.reciprocal(s.stats[:, 28:29], s.stats[:, 27:28]).then_inc(sVE, 1)
                v_pre = cVE.inc()
                se.wait_ge(sVE, v_pre)
                se.activation(out=s.stats[:, 29:30], in_=s.stats[:, 28:29],
                              func=AF.Sqrt).then_inc(sSE, 1)
                s_rs = cSE.inc()
                ve.wait_ge(sSE, s_rs)
                ve.tensor_mul(s.stats[:, 29:30], s.stats[:, 29:30], s.stats[:, 26:27])
                ve.tensor_mul(s.stats[:, 30:31], s.stats[:, 29:30], s.stats[:, 25:26])
                ve.tensor_scalar_max(s.stats[:, 30:31], s.stats[:, 30:31], 1e-5)
                ve.reciprocal(s.stats[:, 31:32], s.stats[:, 30:31])
                ve.tensor_mul(s.stats[:, 31:32], s.stats[:, 31:32], s.stats[:, 29:30])
                ve.tensor_scalar_mul(s.stats[:, 31:32], s.stats[:, 31:32], 127.0)
                ve.tensor_scalar_mul(s.s2r[:, t:t + 1], s.stats[:, 30:31],
                                     1.0 / 127.0).then_inc(sVE, 1)
                v_cs = cVE.inc()
                prev_sub = None
                iq_free = None
                for sl in range(11):
                    tb = tmp2[sl % 2]
                    se.wait_ge(sVE, v_cs if prev_sub is None else prev_sub)
                    se.activation(out=tb[:], in_=gt[:, sl * 512:(sl + 1) * 512],
                                  func=AF.Copy, scale=s.stats[:, 31:32],
                                  bias=MAGIC).then_inc(sSE, 1)
                    s_r = cSE.inc()
                    ve.wait_ge(sSE, s_r)
                    if iq_free is not None:
                        ve.wait_ge(sTE, iq_free)
                    ve.tensor_scalar_add(iq_tok[:], tb[:], -MAGIC).then_inc(sVE, 1)
                    prev_sub = cVE.inc()
                    bank = batch_i % 2
                    batch_i += 1
                    if tp_free[bank] is not None:
                        te.wait_ge(sVE, tp_free[bank])
                    te.wait_ge(sVE, prev_sub)
                    last = None
                    for jj in range(4):
                        last = te.transpose(tpb[bank][:, jj * 128:(jj + 1) * 128],
                                            iq_tok[:, jj * 128:(jj + 1) * 128],
                                            s.c_id[:])
                    last.then_inc(sTE, 1)
                    t_b = cTE.inc()
                    iq_free = t_b
                    ve.wait_ge(sTE, t_b)
                    lastve = None
                    for jj in range(4):
                        hc = sl * 4 + jj
                        lastve = ve.tensor_copy(
                            iqT[:, ti * HID + hc * 128: ti * HID + (hc + 1) * 128],
                            tpb[bank][:, jj * 128:(jj + 1) * 128])
                    lastve.then_inc(sVE, 1)
                    tp_free[bank] = cVE.inc()
                tmp2_safe = prev_sub
            f1b_ve = cVE.v
            s.free(f"tpbF{G}_1", f"tpbF{G}_0")

            pb2 = [s.ps(f"pb2{G}_{i}", [128, 512], F32) for i in range(8)]
            bank2_free = [None] * 8
            te.wait_ge(sVE, f1b_ve)
            hgs = [(hg * 8, min(8, NHCH - hg * 8)) for hg in range(6)]
            for ds in range(4):
                banks = [pb2[4 * (ds % 2) + i] for i in range(4)]
                for i in range(4):
                    if bank2_free[4 * (ds % 2) + i] is not None:
                        sem, v = bank2_free[4 * (ds % 2) + i]
                        te.wait_ge(sem, v)
                last = None
                for hgi, (hc0, nhc) in enumerate(hgs):
                    par = half_i % 2
                    half_i += 1
                    guard = (sTE, wsb_free[par]) if wsb_free[par] is not None else None
                    w_rec = cast_block(wsb[par][:, 0:nhc * 512],
                                       s.w2p_d[ds, hgi, :, 0:nhc * 512],
                                       nhc * 512, guard)
                    te.wait_ge(sSE, w_rec)
                    for hcj in range(nhc):
                        hc = hc0 + hcj
                        for ti in range(4):
                            last = te.matmul(
                                banks[ti][:],
                                iqT[:, ti * HID + hc * 128: ti * HID + (hc + 1) * 128],
                                wsb[par][:, hcj * 512:(hcj + 1) * 512],
                                start=(hc == 0), stop=(hc == NHCH - 1))
                    last.then_inc(sTE, 1)
                    wsb_free[par] = cTE.inc()
                t_ds = cTE.v
                for ti in range(4):
                    t = 4 * G + ti
                    pp = ti % 2
                    se.wait_ge(sTE, t_ds)
                    if gstg_free[pp] is not None:
                        se.wait_ge(sSY, gstg_free[pp])
                    se.activation(out=gstg[pp][:], in_=banks[ti][:], func=AF.Silu,
                                  scale=s.s2r[:, t:t + 1]).then_inc(sSE, 1)
                    s_g = cSE.inc()
                    bank2_free[4 * (ds % 2) + ti] = (sSE, s_g)
                    sy.wait_ge(sSE, s_g)
                    rec = s.dma(sy, s.gsp[t * 128:(t + 1) * 128, ds * 512:(ds + 1) * 512],
                                gstg[pp][:])
                    gstg_free[pp] = rec
            s.free(*[f"pb2{G}_{i}" for i in reversed(range(8))])
            s.free(f"iq_tok_{G}", f"gated_{G}", f"iqT_{G}")

        if "gsp" in s.taps:
            sy.wait_ge(sSY, s.cSY.v)
            s.dma(sy, s.taps["gsp"][:, :], s.gsp[:, :])
        s.p4 = dict(te=cTE.v, ve=cVE.v, se=cSE.v, sy=s.cSY.v, gp=s.cGP.v)
        s.free("tmp21", "tmp20", "x1h", "xqG",
               "wq322", "wq321", "wq320", "wsb1", "wsb0")

    def p5_attention(self):
        s = self
        s.phase_barrier()
        te, ve, se, sy, gp = s.te, s.ve, s.se, s.sy, s.gp
        sTE, sVE, sSE, sSY, sGP, sCC = s.sTE, s.sVE, s.sSE, s.sSY, s.sGP, s.sCC
        cTE, cVE, cSE = s.cTE, s.cVE, s.cSE

        s.c_onesr = s.sb("c_onesr", [1, 128], F32)
        s.c_onesr16 = s.sb("c_onesr16", [1, 128], BF16)
        ve.memset(s.c_onesr[:], 1.0)
        ve.memset(s.c_onesr16[:], 1.0).then_inc(sVE, 1)
        s.onesr_rec = cVE.inc()
        s.wo_s = s.sb("wo_s", [128, NCH * D], BF16)
        s.c_bo = s.sb("c_bo", [128, D], F32)
        s.attnT = s.sb("attnT", [128, NCH * TOK], BF16)
        s.c_masks = s.sb("c_masks", [128, 2048], F32)
        s.c_alibi = s.sb("c_alibi", [128, 512], F32)
        s.dma(sy, s.c_bo[:], s.bo_d[:, :])
        s.dma(sy, s.c_alibi[:], s.alibi_d[:, :])
        mask_rec = s.dma(sy, s.c_masks[:], s.masks_d[:, :])

        wo32 = [s.sb(f"wo32{i}", [128, D], F32) for i in range(2)]
        wo_rec = 0
        wo32_free = [None, None]
        for c in range(NCH):
            fp = c % 2
            if wo32_free[fp] is not None:
                sy.wait_ge(sSE, wo32_free[fp])
            sy.dma_start(out=wo32[fp][:],
                         in_=s.wo_d[c * 128:(c + 1) * 128, :]).then_inc(s.sGP, 16)
            s.cGP.inc(16)
            se.wait_ge(s.sGP, s.cGP.v)
            se.activation(out=s.wo_s[:, c * D:(c + 1) * D], in_=wo32[fp][:],
                          func=AF.Copy).then_inc(sSE, 1)
            wo_rec = cSE.inc()
            wo32_free[fp] = wo_rec
        s.wo_rec = wo_rec

        aqrow = [s.sb(f"aqrow{i}", [1, 512], BF16) for i in range(4)]
        kbuf = [s.sb(f"kbuf{i}", [128, 4096], BF16) for i in range(2)]
        vbuf = [s.sb(f"vbuf{i}", [128, 4096], BF16) for i in range(2)]
        qbuf = [s.sb(f"qbuf{i}", [128, TOK], BF16) for i in range(2)]
        phat = s.sb("phat", [128, 4 * 512], BF16)
        Lrow = s.sb("Lrow", [128, 512], F32)
        bcs = s.sb("bcs", [128, 512], F32)
        atmp = s.sb("atmp", [128, 512], F32)
        zb = [s.ps(f"zb{i}", [128, 512], F32) for i in range(2)]
        pv = [s.ps(f"pv{i}", [128, 512], F32) for i in range(2)]
        lb = [s.ps(f"lb{i}", [128, 512], F32) for i in range(2)]
        bcb = s.ps("bcb", [128, 512], F32)
        te.wait_ge(s.sSE, s.cSE.v)
        te.wait_ge(s.sVE, s.cVE.v)

        kv_rec = []
        self._h_done_target = [None] * H
        te_base = cTE.v
        per_h_te = 2 * 36 + 2
        for h in range(H):
            if h >= 2:
                sy.wait_ge(sTE, te_base + (h - 1) * per_h_te)
            if h == 0:
                sy.wait_ge(sCC, s.cc_v)
            for qs01 in range(2):
                s.dma(sy, aqrow[(2 * h + qs01) % 4][:],
                      s.aq_d[2 * h + qs01:2 * h + qs01 + 1, :])
            rec = s.dma(sy, qbuf[h % 2][:], s.qts[h * 128:(h + 1) * 128, :])
            for ck in range(8):
                rank, half = _crh(ck)
                slab = 4096 * rank
                rec = s.dma(sy, kbuf[h % 2][:, ck * 512:(ck + 1) * 512],
                            s.kvout[slab + 128 * h: slab + 128 * (h + 1),
                                    512 * half: 512 * (half + 1)])
                vout_r = s.kvout[slab + 2048: slab + 4096, :].rearrange(
                    "(t two) k -> t (two k)", two=2)
                vsrc = _pck(vout_r[512 * half: 512 * (half + 1),
                                   h * 128:(h + 1) * 128])
                rec = s.dma(sy, _sck(vbuf[h % 2][:, ck * 512:(ck + 1) * 512], 4), vsrc)
            kv_rec.append(rec)

        ve.wait_ge(sSY, mask_rec)
        for eng in (te, se, ve):
            eng.wait_ge(sSY, s.rid_rec)
        idx = s.nc.values_load(s.rid_s[0:1, 0:1],
                               engines=[te.engine, se.engine, ve.engine],
                               min_val=0, max_val=3, skip_runtime_bounds_check=True)
        snap = (cTE.v, cVE.v, cSE.v)
        ends = []
        for r in s.nc.Switch(engines=[te, se, ve._ve], index=[idx, idx, idx], n=4):
            cTE.v, cVE.v, cSE.v = snap
            self._emit_attn_variant(r, kbuf, vbuf, qbuf, phat, Lrow, bcs, atmp,
                                    zb, pv, lb, bcb, kv_rec, aqrow)
            ends.append((cTE.v, cVE.v, cSE.v))
        assert all(e == ends[0] for e in ends), f"variant counts differ: {ends}"
        cTE.v, cVE.v, cSE.v = ends[0]
        assert cTE.v - te_base == H * per_h_te, (cTE.v - te_base, H * per_h_te)

        if "attnT" in s.taps:
            sy.wait_ge(sSE, cSE.v)
            s.dma(sy, s.taps["attnT"][:, :], s.attnT[:])
        if "wos" in s.taps:
            sy.wait_ge(sSE, s.wo_rec)
            s.dma(sy, s.taps["wos"][:, :], s.wo_s[:])
        for nm, buf in (("kbuf1", kbuf[1]), ("vbuf1", vbuf[1]), ("qbuf1", qbuf[1]),
                        ("lrow", Lrow), ("bcs", bcs), ("atmp", atmp),
                        ("phat", phat)):
            if nm in s.taps:
                sy.wait_ge(sSE, cSE.v)
                sy.wait_ge(sVE, cVE.v)
                s.dma(sy, s.taps[nm][:, :], buf[:])
        s.p5 = dict(te=cTE.v, ve=cVE.v, se=cSE.v, sy=s.cSY.v, gp=s.cGP.v)
        s.free("bcb", "lb1", "lb0", "pv1", "pv0", "zb1", "zb0", "atmp", "bcs",
               "Lrow", "phat", "qbuf1", "qbuf0", "vbuf1", "vbuf0", "kbuf1", "kbuf0",
               "aqrow3", "aqrow2", "aqrow1", "aqrow0", "wo321", "wo320")

    def _emit_attn_variant(self, r, kbuf, vbuf, qbuf, phat, Lrow, bcs, atmp,
                           zb, pv, lb, bcb, kv_rec, aqrow):
        s = self
        te, ve, se = s.te, s.ve, s.se
        sTE, sVE, sSE, sSY = s.sTE, s.sVE, s.sSE, s.sSY
        cTE, cVE, cSE = s.cTE, s.cVE, s.cSE

        pvl_free = [None, None]
        bcb_free = None
        atmp_free = None
        prev_exp_last = None
        for h in range(H):
            te.wait_ge(sSY, kv_rec[h])
            for qs01 in range(2):
                A = 4 * r + 4 if qs01 == 0 else 32 - 4 * r
                g2 = (2 * h + qs01) % 2
                exp_rec = {}
                pvl_rec = {}
                phat_rec = {}

                def emit_z(kb):
                    zi = kb % 2
                    if kb >= 2:
                        te.wait_ge(sSE, exp_rec[zi])
                    elif prev_exp_last is not None:
                        te.wait_ge(sSE, prev_exp_last)
                    te.matmul(zb[zi][:], s.c_onesr16[0:1, 0:128],
                              aqrow[(2 * h + qs01) % 4][0:1, :],
                              start=True, stop=False)
                    te.matmul(zb[zi][:], kbuf[h % 2][:, kb * 128:(kb + 1) * 128],
                              qbuf[h % 2][:, qs01 * 512:(qs01 + 1) * 512],
                              start=False, stop=True).then_inc(sTE, 1)
                    z_rec = cTE.inc()
                    if kb >= A - 4:
                        v = kb - (A - 4)
                        ve.wait_ge(sTE, z_rec)
                        ve.tensor_add(zb[zi][:], zb[zi][:],
                                      s.c_masks[:, v * 512:(v + 1) * 512]).then_inc(sVE, 1)
                        z_gate = (sVE, cVE.inc())
                    else:
                        z_gate = (sTE, z_rec)
                    se.wait_ge(z_gate[0], z_gate[1])
                    if kb - 4 in pvl_rec:
                        se.wait_ge(sTE, pvl_rec[kb - 4])
                    elif kb < 4 and pvl_free[1 - g2] is not None:
                        se.wait_ge(sVE, pvl_free[1 - g2])
                    se.activation(out=phat[:, (kb % 4) * 512:(kb % 4 + 1) * 512],
                                  in_=zb[zi][:], func=AF.Exp,
                                  bias=s.c_alibi[:, h * 32 + kb: h * 32 + kb + 1],
                                  scale=1.0).then_inc(sSE, 1)
                    exp_rec[zi] = cSE.inc()
                    phat_rec[kb] = (sSE, exp_rec[zi])

                emit_z(0)
                for kb in range(A):
                    if kb + 1 < A:
                        emit_z(kb + 1)
                    sem, rec = phat_rec[kb]
                    te.wait_ge(sem, rec)
                    if kb == 0 and pvl_free[g2] is not None:
                        te.wait_ge(sVE, pvl_free[g2])
                    te.matmul(pv[g2][:], vbuf[h % 2][:, kb * 128:(kb + 1) * 128],
                              phat[:, (kb % 4) * 512:(kb % 4 + 1) * 512],
                              start=(kb == 0), stop=(kb == A - 1))
                    te.matmul(lb[g2][0:1, :], s.c_ones[:, 0:1],
                              phat[:, (kb % 4) * 512:(kb % 4 + 1) * 512],
                              start=(kb == 0), stop=(kb == A - 1)).then_inc(sTE, 1)
                    pvl_rec[kb] = cTE.inc()
                ve.wait_ge(sTE, pvl_rec[A - 1])
                if atmp_free is not None:
                    ve.wait_ge(sSE, atmp_free)
                ve.tensor_copy(Lrow[0:1, 0:512], lb[g2][0:1, :])
                ve.reciprocal(Lrow[0:1, 0:512], Lrow[0:1, 0:512]).then_inc(sVE, 1)
                l_rec = cVE.inc()
                te.wait_ge(sVE, l_rec)
                te.matmul(bcb[:], s.c_onesr[0:1, :], Lrow[0:1, 0:512],
                          start=True, stop=True).then_inc(sTE, 1)
                bc_rec = cTE.inc()
                ve.wait_ge(sTE, bc_rec)
                ve.tensor_copy(bcs[:], bcb[:]).then_inc(sVE, 1)
                bcb_free = cVE.inc()
                ve.scalar_tensor_tensor(atmp[:], pv[g2][:], 1.0, bcs[:],
                                        ALU.bypass, ALU.mult).then_inc(sVE, 1)
                ev_rec = cVE.inc()
                pvl_free[g2] = ev_rec
                se.wait_ge(sVE, ev_rec)
                se.activation(out=s.attnT[:, h * TOK + qs01 * 512: h * TOK + (qs01 + 1) * 512],
                              in_=atmp[:], func=AF.Identity,
                              bias=s.c_bqkv[:, 32 + h:33 + h], scale=1.0).then_inc(sSE, 1)
                atmp_free = cSE.inc()
                prev_exp_last = exp_rec[(A - 1) % 2]

    def p6_combine(self):
        s = self
        s.phase_barrier()
        te, ve, se, sy = s.te, s.ve, s.se, s.sy
        sTE, sVE, sSE, sSY, sGP = s.sTE, s.sVE, s.sSE, s.sSY, s.sGP
        cTE, cVE, cSE = s.cTE, s.cVE, s.cSE

        xrl = [s.sb(f"xrl{i}", [128, D], F32) for i in range(2)]
        grl = [s.sb(f"grl{i}", [128, D], F32) for i in range(2)]
        ostg = [s.sb(f"ostg{i}", [128, D], F32) for i in range(2)]
        p6 = [s.ps(f"p6_{i}", [128, 512], F32) for i in range(8)]

        te.wait_ge(sSE, s.p5["se"])
        te.wait_ge(sVE, s.p5["ve"])
        te.wait_ge(sSE, s.wo_rec)
        rl_free = [None, None]
        o_free = [None, None]
        bank6_free = [None] * 8
        for t in range(NT):
            pp = t % 2
            if rl_free[pp] is not None:
                sy.wait_ge(sVE, rl_free[pp])
            s.dma(sy, xrl[pp][:], s.x_d[t * 128:(t + 1) * 128, :])
            x_rec = s.dma(sy, grl[pp][:], s.gsp[t * 128:(t + 1) * 128, :])
            banks = [p6[4 * pp + i] for i in range(4)]
            for i in range(4):
                if bank6_free[4 * pp + i] is not None:
                    te.wait_ge(sVE, bank6_free[4 * pp + i])
            last = None
            for c in range(NCH):
                a_ap = s.attnT[:, c * TOK + t * 128: c * TOK + (t + 1) * 128]
                for ds in range(4):
                    last = te.matmul(banks[ds][:], a_ap,
                                     s.wo_s[:, c * D + ds * 512: c * D + (ds + 1) * 512],
                                     start=(c == 0), stop=(c == NCH - 1))
            last.then_inc(sTE, 1)
            t_rec = cTE.inc()
            ve.wait_ge(sTE, t_rec)
            ve.wait_ge(sSY, x_rec)
            if o_free[pp] is not None:
                ve.wait_ge(sSY, o_free[pp])
            lastve = None
            for ds in range(4):
                sl = slice(ds * 512, (ds + 1) * 512)
                ve.scalar_tensor_tensor(ostg[pp][:, sl], banks[ds][:], 1.0,
                                        s.c_bo[:, sl], ALU.bypass, ALU.add)
                ve.tensor_mul(ostg[pp][:, sl], ostg[pp][:, sl], grl[pp][:, sl])
                lastve = ve.tensor_add(ostg[pp][:, sl], ostg[pp][:, sl], xrl[pp][:, sl])
            lastve.then_inc(sVE, 1)
            v_rec = cVE.inc()
            rl_free[pp] = v_rec
            for i in range(4):
                bank6_free[4 * pp + i] = v_rec
            sy.wait_ge(sVE, v_rec)
            o_free[pp] = s.dma(sy, s.out_d[t * 128:(t + 1) * 128, :], ostg[pp][:])

    def finish(self):
        s = self
        s.sy.wait_ge(s.sSY, s.cSY.v)
        s.sy.wait_ge(s.sGP, s.cGP.v)
        s.sy.wait_ge(s.sTE, s.cTE.v)
        s.sy.wait_ge(s.sVE, s.cVE.v)
        s.sy.wait_ge(s.sSE, s.cSE.v)


def _host_consts():
    slopes = np.array([1.0 / 2 ** (k / 2) for k in range(H)], np.float32)
    alibi = np.zeros((128, 512), np.float32)
    j = np.arange(128, dtype=np.float32)
    for h in range(H):
        for kb in range(32):
            alibi[:, h * 32 + kb] = slopes[h] * (128 * kb + j) - SHIFT
    masks = np.zeros((128, 2048), np.float32)
    q = np.arange(512, dtype=np.float32)
    for v in range(4):
        masks[:, v * 512:(v + 1) * 512] = np.where(
            j[:, None] + 128 * v <= q[None, :], 0.0, -1e30)
    return alibi, masks, np.eye(128, dtype=ml_dtypes.bfloat16)


def make_in_maps(L0_attn_x, L0_attn_Wqkv, L0_attn_bqkv, L0_attn_Wo, L0_attn_bo,
                 L0_ffn_w13, L0_ffn_w2, L0_ffn_norm_w):
    x = np.asarray(L0_attn_x, np.float32)
    wqkv = np.asarray(L0_attn_Wqkv, np.float32)
    bqkvT = np.ascontiguousarray(np.asarray(L0_attn_bqkv, np.float32).reshape(48, 128).T)
    wo = np.ascontiguousarray(np.asarray(L0_attn_Wo, np.float32))
    bo_rep = np.ascontiguousarray(
        np.broadcast_to(np.asarray(L0_attn_bo, np.float32), (128, D)))
    nw_rep = np.ascontiguousarray(
        np.broadcast_to(np.asarray(L0_ffn_norm_w, np.float32), (128, HID)))
    wq6 = wqkv.reshape(NCH, 128, QKVC)
    def _qtile(base, K, ngrp):
        a = wq6[:, :, base:base + ngrp * K].reshape(NCH, 128, ngrp, K)
        return np.ascontiguousarray(a.transpose(2, 1, 0, 3).reshape(ngrp, 128, NCH * K))
    wqkp = _qtile(D, 256, 8)
    wqvp = _qtile(2 * D, 512, 4)
    wqqp = _qtile(0, 256, 8)
    w13t = np.asarray(L0_ffn_w13, np.float32).T
    a = w13t.reshape(2, 8, 128, 2, 11, 512)
    w13p = np.ascontiguousarray(
        a.transpose(4, 3, 0, 2, 1, 5).reshape(22, 2, 128, 4096))
    w2t = np.asarray(L0_ffn_w2, np.float32).T
    w2p = np.zeros((4, 6, 128, 4096), np.float32)
    for hg in range(6):
        nhc = min(8, NHCH - hg * 8)
        blk = w2t[hg * 8 * 128:(hg * 8 + nhc) * 128, :]
        blk = blk.reshape(nhc, 128, 4, 512).transpose(2, 1, 0, 3)
        w2p[:, hg, :, 0:nhc * 512] = blk.reshape(4, 128, nhc * 512)
    w2p = np.ascontiguousarray(w2p)
    alibi, masks, ident = _host_consts()
    slopes = np.array([1.0 / 2 ** (k / 2) for k in range(H)], np.float32)
    qv = np.arange(512, dtype=np.float32)
    in_maps = []
    for core in range(8):
        b, r = core // 4, core % 4
        aq = np.zeros((32, 512), np.float32)
        for h in range(H):
            for qs01 in range(2):
                chunk = r if qs01 == 0 else 7 - r
                aq[2 * h + qs01] = -slopes[h] * (chunk * 512 + qv)
        aq16 = aq.astype(ml_dtypes.bfloat16)
        x_own = np.ascontiguousarray(np.concatenate(
            [x[b, 512 * r:512 * (r + 1)], x[b, 512 * (7 - r):512 * (8 - r)]], axis=0))
        in_maps.append({
            "rid": np.array([[r]], np.int32), "x": x_own,
            "wqkp": wqkp, "wqvp": wqvp, "wqqp": wqqp,
            "bqkvT": bqkvT, "wo": wo, "bo_rep": bo_rep, "w13p": w13p,
            "w2p": w2p, "nw_rep": nw_rep, "alibi": alibi, "masks": masks,
            "ident": ident, "aq": aq16,
        })
    return in_maps


def kernel(L0_attn_x, L0_attn_Wqkv, L0_attn_bqkv, L0_attn_Wo, L0_attn_bo,
           L0_ffn_w13, L0_ffn_w2, L0_ffn_norm_w):
    in_maps = make_in_maps(L0_attn_x, L0_attn_Wqkv, L0_attn_bqkv, L0_attn_Wo,
                           L0_attn_bo, L0_ffn_w13, L0_ffn_w2, L0_ffn_norm_w)
    nc = MK().build()
    res = run_bass_kernel_spmd(nc, in_maps, core_ids=list(range(8)))
    out = np.empty((B, S, D), np.float32)
    for core in range(8):
        b, r = core // 4, core % 4
        o = res.results[core]["out"]
        out[b, 512 * r:512 * (r + 1)] = o[:512]
        out[b, 512 * (7 - r):512 * (8 - r)] = o[512:]
    return out


# version 10
# speedup vs baseline: 1.1856x; 1.1856x over previous
import math

import numpy as np
import ml_dtypes
import concourse.bass as bass
import concourse.mybir as mybir
from concourse.bass_utils import run_bass_kernel_spmd

F32 = mybir.dt.float32
BF16 = mybir.dt.bfloat16
I32 = mybir.dt.int32
AF = mybir.ActivationFunctionType
ALU = mybir.AluOpType
AX = mybir.AxisListType

B, S, D, H, DH = 2, 4096, 2048, 16, 128
QKVC = 3 * D
HID, FF = 5632, 11264
TOK, NT = 1024, 8
NCH = D // 128
NHCH = HID // 128
QSCALE = 1.0 / math.sqrt(DH)
MAGIC = float(np.float32(12582912.0))
SHIFT = 12.0
EPS = 1e-5
GROUPS = [[0, 1, 2, 3], [4, 5, 6, 7]]

TAPS = []
BUILD_UPTO = 99
DO_COLLECTIVES = True


def _crh(ck):
    return (ck, 0) if ck < 4 else (7 - ck, 1)



def _pck(dram_ap, p=128):
    return dram_ap.rearrange("(c p) k -> p c k", p=p)


def _sck(sbuf_ap, c):
    return sbuf_ap.rearrange("p (c k) -> p c k", c=c)



def _ap_span(ap):
    try:
        name = ap.tensor.name
    except Exception:
        name = str(ap.tensor)
    off = ap.offset
    reach = 1
    for step, cnt in ap.ap:
        reach += abs(step) * (cnt - 1)
    return (name, off, off + reach)


class VDrain:

    WINDOW = 5

    def __init__(self, ve):
        self._ve = ve
        self._hist = []

    def _chk(self, ins, outs):
        spans = [_ap_span(a) for a in ins + outs if a is not None]
        hit = False
        for (n, lo, hi) in spans:
            for (n2, lo2, hi2) in self._hist:
                if n == n2 and lo < hi2 and lo2 < hi:
                    hit = True
                    break
            if hit:
                break
        if hit:
            self._ve.drain()
            self._hist = []

    def _rec(self, outs):
        self._hist.extend(_ap_span(a) for a in outs if a is not None)
        self._hist = self._hist[-self.WINDOW:]

    def _emit(self, fn, ins, outs, *args, **kw):
        self._chk(ins, outs)
        r = fn(*args, **kw)
        self._rec(outs)
        return r

    def tensor_copy(self, out, in_):
        return self._emit(self._ve.tensor_copy, [in_], [out], out, in_)

    def tensor_add(self, out, a, b):
        return self._emit(self._ve.tensor_add, [a, b], [out], out, a, b)

    def tensor_sub(self, out, a, b):
        return self._emit(self._ve.tensor_sub, [a, b], [out], out, a, b)

    def tensor_mul(self, out, a, b):
        return self._emit(self._ve.tensor_mul, [a, b], [out], out, a, b)

    def tensor_scalar_mul(self, out, a, sc):
        ins = [a] + ([sc] if not isinstance(sc, float) else [])
        return self._emit(self._ve.tensor_scalar_mul, ins, [out], out, a, sc)

    def tensor_scalar_add(self, out, a, sc):
        ins = [a] + ([sc] if not isinstance(sc, float) else [])
        return self._emit(self._ve.tensor_scalar_add, ins, [out], out, a, sc)

    def tensor_scalar_max(self, out, a, sc):
        ins = [a] + ([sc] if not isinstance(sc, float) else [])
        return self._emit(self._ve.tensor_scalar_max, ins, [out], out, a, sc)

    def reciprocal(self, out, in_):
        return self._emit(self._ve.reciprocal, [in_], [out], out, in_)

    def reduce_sum(self, out, in_, axis, **kw):
        return self._emit(self._ve.reduce_sum, [in_], [out], out, in_, axis=axis, **kw)

    def reduce_max(self, out, in_, axis, **kw):
        return self._emit(self._ve.reduce_max, [in_], [out], out, in_, axis=axis, **kw)

    def scalar_tensor_tensor(self, out, in0, sc, in1, op0, op1):
        ins = [in0, in1] + ([sc] if not isinstance(sc, float) else [])
        return self._emit(self._ve.scalar_tensor_tensor, ins, [out],
                          out, in0, sc, in1, op0, op1)

    def memset(self, ap, v):
        return self._emit(self._ve.memset, [], [ap], ap, v)

    def wait_ge(self, sem, v):
        return self._ve.wait_ge(sem, v)

    def drain(self):
        self._hist = []
        return self._ve.drain()

    @property
    def engine(self):
        return self._ve.engine


class Ctr:
    __slots__ = ("v",)

    def __init__(self):
        self.v = 0

    def inc(self, n=1):
        self.v += n
        return self.v


class MK:
    def __init__(self):
        nc = bass.Bass("TRN2", target_bir_lowering=False,
                       dynamic_dma_scratch_size=16384)
        self.nc = nc
        self.stack = []
        self.taps = {}
        self.rid = nc.dram_tensor("rid", [1, 1], I32, kind="ExternalInput").ap()
        self.x_d = nc.dram_tensor("x", [TOK, D], F32, kind="ExternalInput").ap()
        self.wqkp_d = nc.dram_tensor("wqkp", [8, 128, 4096], F32, kind="ExternalInput").ap()
        self.wqvp_d = nc.dram_tensor("wqvp", [4, 128, 8192], F32, kind="ExternalInput").ap()
        self.wqqp_d = nc.dram_tensor("wqqp", [8, 128, 4096], F32, kind="ExternalInput").ap()
        self.bqkv_d = nc.dram_tensor("bqkvT", [128, 48], F32, kind="ExternalInput").ap()
        self.wo_d = nc.dram_tensor("wo", [D, D], F32, kind="ExternalInput").ap()
        self.bo_d = nc.dram_tensor("bo_rep", [128, D], F32, kind="ExternalInput").ap()
        self.w13p_d = nc.dram_tensor("w13p", [22, 2, 128, 4096], F32, kind="ExternalInput").ap()
        self.w2p_d = nc.dram_tensor("w2p", [4, 6, 128, 4096], F32, kind="ExternalInput").ap()
        self.nw_d = nc.dram_tensor("nw_rep", [128, HID], F32, kind="ExternalInput").ap()
        self.alibi_d = nc.dram_tensor("alibi", [128, 512], F32, kind="ExternalInput").ap()
        self.masks_d = nc.dram_tensor("masks", [128, 2048], F32, kind="ExternalInput").ap()
        self.id_d = nc.dram_tensor("ident", [128, 128], BF16, kind="ExternalInput").ap()
        self.aq_d = nc.dram_tensor("aq", [32, 512], BF16, kind="ExternalInput").ap()
        self.out_d = nc.dram_tensor("out", [TOK, D], F32, kind="ExternalOutput").ap()

        self.kvin = nc.dram_tensor("kvin", [2 * D, TOK], BF16, kind="Internal").ap()
        self.kvout = nc.dram_tensor("kvout", [8 * D, TOK], BF16, kind="Internal").ap()
        self.kin = self.kvin[0:D, :]
        self.vin = self.kvin[D:2 * D, :].rearrange("(t two) k -> t (two k)", two=2)
        self.qts = nc.dram_tensor("qts", [D, TOK], BF16, kind="Internal").ap()
        self.xqs = nc.dram_tensor("xqs", [D, TOK], BF16, kind="Internal").ap()
        self.gsp = nc.dram_tensor("gsp", [TOK, D], F32, kind="Internal").ap()
        for name, shape, dt in TAPS:
            self.taps[name.replace("tap_", "")] = nc.dram_tensor(
                name, shape, dt, kind="ExternalOutput").ap()

        self.te, self.ve, self.se = nc.tensor, VDrain(nc.vector), nc.scalar
        self.sy, self.gp = nc.sync, nc.gpsimd

        self.sSY = nc.semaphore("sSY").__enter__()
        self.sGP = nc.semaphore("sGP").__enter__()
        self.sCC = nc.semaphore("sCC").__enter__()
        self.sTE = nc.semaphore("sTE").__enter__()
        self.sVE = nc.semaphore("sVE").__enter__()
        self.sSE = nc.semaphore("sSE").__enter__()
        self.cSY, self.cGP, self.cCC = Ctr(), Ctr(), Ctr()
        self.cTE, self.cVE, self.cSE = Ctr(), Ctr(), Ctr()

    def sb(self, name, shape, dtype):
        g = self.nc.sbuf_tensor(name, shape, dtype)
        t = g.__enter__()
        self.stack.append((g, name))
        return t

    def ps(self, name, shape, dtype):
        g = self.nc.psum_tensor(name, shape, dtype)
        t = g.__enter__()
        self.stack.append((g, name))
        return t

    def free(self, *names):
        for name in names:
            g, n = self.stack.pop()
            assert n == name, f"free order: got {n}, want {name}"
            g.__exit__(None, None, None)

    def phase_barrier(self):
        s = self
        targets = [(s.sSY, s.cSY.v), (s.sTE, s.cTE.v),
                   (s.sVE, s.cVE.v), (s.sSE, s.cSE.v)]
        for eng in (s.te, s.ve, s.se, s.sy, s.gp):
            for sem, v in targets:
                if v:
                    eng.wait_ge(sem, v)

    def dma(self, eng, out, in_):
        if eng is self.sy:
            eng.dma_start(out=out, in_=in_).then_inc(self.sSY, 16)
            return self.cSY.inc(16)
        eng.dma_start(out=out, in_=in_).then_inc(self.sGP, 16)
        return self.cGP.inc(16)

    def build(self):
        self.p0_consts()
        if BUILD_UPTO >= 1:
            self.p1_layernorm()
        if BUILD_UPTO >= 2:
            self.p2_qkv()
        if BUILD_UPTO >= 4:
            self.p4_ffn()
        if BUILD_UPTO >= 5:
            self.p5_attention()
        if BUILD_UPTO >= 6:
            self.p6_combine()
        self.finish()
        return self.nc

    def p0_consts(self):
        s = self
        s.c_nw = s.sb("c_nw", [128, HID], F32)
        s.c_id = s.sb("c_id", [128, 128], BF16)
        s.c_bqkv = s.sb("c_bqkv", [128, 48], F32)
        s.c_bqs = s.sb("c_bqs", [128, 16], F32)
        s.c_ones = s.sb("c_ones", [128, 1], BF16)
        s.rid_s = s.sb("rid_s", [1, 1], I32)
        s.sent_s = s.sb("sent_s", [1, 1], I32)
        s.sxr = s.sb("sxr", [128, NT], F32)
        s.s2r = s.sb("s2r", [128, NT], F32)
        s.stats = s.sb("stats", [128, 96], F32)

        s.dma(s.sy, s.c_nw[:], s.nw_d[:, :])
        s.dma(s.sy, s.c_id[:], s.id_d[:, :])
        s.dma(s.sy, s.c_bqkv[:], s.bqkv_d[:, :])
        s.rid_rec = s.dma(s.sy, s.rid_s[:], s.rid[:, :])
        s.p0_loads = s.cSY.v

        s.pc_w13 = {}

        s.ve.wait_ge(s.sSY, s.p0_loads)
        s.ve.memset(s.c_ones[:], 1.0)
        s.ve.tensor_scalar_mul(s.c_bqs[:], s.c_bqkv[:, 0:16], QSCALE).then_inc(s.sVE, 1)
        s.p0_ve = s.cVE.inc()

    def p1_layernorm(self):
        s = self
        te, ve, se, sy = s.te, s.ve, s.se, s.sy
        sTE, sVE, sSE, sSY = s.sTE, s.sVE, s.sSE, s.sSY
        cTE, cVE, cSE = s.cTE, s.cVE, s.cSE
        st = s.stats

        s.xnT = s.sb("xnT", [128, NCH * TOK], BF16)
        x_all = s.sb("x_all", [128, NT * D], F32)
        dump = s.sb("dump", [128, D], F32)
        xnf = [s.sb(f"xnf{i}", [128, D], F32) for i in range(2)]
        xnb = [s.sb(f"xnb{i}", [128, D], BF16) for i in range(2)]
        xqb = [s.sb(f"xqb{i}", [128, D], BF16) for i in range(2)]
        xqstg = [s.sb(f"xqstg{i}", [128, 512], BF16) for i in range(2)]
        tpb = [s.ps(f"tpb{i}", [128, 1024], BF16) for i in range(2)]

        x_recs = [s.dma(sy, x_all[:, t * D:(t + 1) * D],
                        s.x_d[t * 128:(t + 1) * 128, :]) for t in range(NT)]
        lastve = lastse = None
        for t in range(NT):
            xt = x_all[:, t * D:(t + 1) * D]
            ve.wait_ge(sSY, x_recs[t])
            lastve = ve.reduce_sum(st[:, t:t + 1], xt, axis=AX.X)
            se.wait_ge(sSY, x_recs[t])
            lastse = se.activation(out=dump[:], in_=xt, func=AF.Square,
                                   accum_out=st[:, 8 + t:9 + t])
        lastve.then_inc(sVE, 1)
        v_sums = cVE.inc()
        lastse.then_inc(sSE, 1)
        s_sq = cSE.inc()

        SU, SQ, ME, VA, RV, IV, NB = 0, 8, 16, 24, 32, 40, 48
        SM, SC, RS = 56, 64, 72
        ve.wait_ge(sSE, s_sq)
        ve.tensor_scalar_mul(st[:, ME:ME + 8], st[:, SU:SU + 8], 1.0 / D)
        ve.tensor_scalar_mul(st[:, VA:VA + 8], st[:, SQ:SQ + 8], 1.0 / D)
        ve.tensor_mul(st[:, NB:NB + 8], st[:, ME:ME + 8], st[:, ME:ME + 8])
        ve.tensor_sub(st[:, VA:VA + 8], st[:, VA:VA + 8], st[:, NB:NB + 8])
        ve.tensor_scalar_add(st[:, VA:VA + 8], st[:, VA:VA + 8], EPS)
        ve.reciprocal(st[:, RV:RV + 8], st[:, VA:VA + 8]).then_inc(sVE, 1)
        v_rv = cVE.inc()
        se.wait_ge(sVE, v_rv)
        se.activation(out=st[:, IV:IV + 8], in_=st[:, RV:RV + 8],
                      func=AF.Sqrt).then_inc(sSE, 1)
        s_inv = cSE.inc()
        ve.wait_ge(sSE, s_inv)
        ve.tensor_scalar_mul(st[:, NB:NB + 8], st[:, ME:ME + 8], -1.0)
        ve.tensor_mul(st[:, NB:NB + 8], st[:, NB:NB + 8],
                      st[:, IV:IV + 8]).then_inc(sVE, 1)
        v_nb = cVE.inc()

        stg_free = [None, None]
        tp_free = [None, None]
        dump_free = None
        batch_i = 0
        xnf_free = [None, None]
        xnb_free = [None, None]
        xqb_free = [None, None]
        smax_rec = {}
        rnd_rec = {}
        for t in range(NT):
            p = t % 2
            se.wait_ge(sVE, v_nb)
            if xnf_free[p] is not None:
                se.wait_ge(sVE, xnf_free[p])
            se.activation(out=xnf[p][:], in_=x_all[:, t * D:(t + 1) * D],
                          func=AF.Identity, scale=st[:, IV + t:IV + t + 1],
                          bias=st[:, NB + t:NB + t + 1]).then_inc(sSE, 1)
            s_xnf = cSE.inc()
            ve.wait_ge(sSE, s_xnf)
            if xnb_free[p] is not None:
                ve.wait_ge(sTE, xnb_free[p])
            ve.tensor_copy(xnb[p][:], xnf[p][:])
            ve.reduce_max(st[:, SM + t:SM + t + 1], xnf[p][:], axis=AX.X,
                          apply_absolute_value=True)
            ve.tensor_scalar_max(st[:, SC + t:SC + t + 1],
                                 st[:, SM + t:SM + t + 1], 1e-5)
            ve.reciprocal(st[:, RS + t:RS + t + 1], st[:, SC + t:SC + t + 1])
            ve.tensor_scalar_mul(st[:, RS + t:RS + t + 1],
                                 st[:, RS + t:RS + t + 1], 127.0)
            ve.tensor_scalar_mul(s.sxr[:, t:t + 1], st[:, SC + t:SC + t + 1],
                                 1.0 / 127.0).then_inc(sVE, 1)
            v_qs = cVE.inc()
            smax_rec[t] = v_qs
            se.wait_ge(sVE, v_qs)
            if dump_free is not None:
                se.wait_ge(sVE, dump_free)
            se.activation(out=dump[:], in_=xnf[p][:], func=AF.Copy,
                          scale=st[:, RS + t:RS + t + 1], bias=MAGIC).then_inc(sSE, 1)
            s_rnd = cSE.inc()
            ve.wait_ge(sSE, s_rnd)
            if xqb_free[p] is not None:
                ve.wait_ge(sTE, xqb_free[p])
            ve.tensor_scalar_add(xqb[p][:], dump[:], -MAGIC).then_inc(sVE, 1)
            v_xq = cVE.inc()
            dump_free = v_xq
            xnf_free[p] = v_xq

            te.wait_ge(sVE, v_xq)
            for src_buf, is_xq in ((xnb[p], False), (xqb[p], True)):
                for b0 in range(0, NCH, 4):
                    bank = batch_i % 2
                    batch_i += 1
                    if tp_free[bank] is not None:
                        te.wait_ge(sVE, tp_free[bank])
                    last = None
                    for j in range(4):
                        cix = b0 + j
                        last = te.transpose(tpb[bank][:, j * 128:(j + 1) * 128],
                                            src_buf[:, cix * 128:(cix + 1) * 128],
                                            s.c_id[:])
                    last.then_inc(sTE, 1)
                    t_b = cTE.inc()
                    ve.wait_ge(sTE, t_b)
                    if is_xq:
                        par = bank
                        if stg_free[par] is not None:
                            ve.wait_ge(sSY, stg_free[par])
                        ve.tensor_copy(xqstg[par][:], tpb[bank][:, 0:512]).then_inc(sVE, 1)
                        v_e = cVE.inc()
                        tp_free[bank] = v_e
                        sy.wait_ge(sVE, v_e)
                        dst = _pck(s.xqs[b0 * 128:(b0 + 4) * 128,
                                         t * 128:(t + 1) * 128])
                        stg_free[par] = s.dma(sy, dst, _sck(xqstg[par][:], 4))
                    else:
                        last = None
                        for j in range(4):
                            cix = b0 + j
                            last = ve.tensor_copy(
                                s.xnT[:, cix * TOK + t * 128: cix * TOK + (t + 1) * 128],
                                tpb[bank][:, j * 128:(j + 1) * 128])
                        last.then_inc(sVE, 1)
                        tp_free[bank] = cVE.inc()
                xnb_free[p] = cTE.v if not is_xq else xnb_free[p]
                xqb_free[p] = cTE.v if is_xq else xqb_free[p]

        if "xnT" in s.taps:
            sy.wait_ge(sVE, cVE.v)
            s.dma(sy, s.taps["xnT"][:, :], s.xnT[:])
        if "sxr" in s.taps:
            sy.wait_ge(sVE, cVE.v)
            s.dma(sy, s.taps["sxr"][:, :], s.sxr[:])
        if "xqs" in s.taps:
            sy.wait_ge(sSY, s.cSY.v)
            s.dma(sy, s.taps["xqs"][:, :], s.xqs[:, :])
        s.p1 = dict(te=cTE.v, ve=cVE.v, se=cSE.v, sy=s.cSY.v)
        s.free("tpb1", "tpb0", "xqstg1", "xqstg0", "xqb1", "xqb0", "xnb1", "xnb0",
               "xnf1", "xnf0", "dump", "x_all")

    def p2_qkv(self):
        s = self
        s.phase_barrier()
        te, ve, se, sy, gp = s.te, s.ve, s.se, s.sy, s.gp
        sTE, sVE, sSE, sSY, sGP = s.sTE, s.sVE, s.sSE, s.sSY, s.sGP
        cTE, cVE, cSE = s.cTE, s.cVE, s.cSE

        wbuf = [s.sb(f"wbuf{i}", [128, NCH * 512], BF16) for i in range(2)]
        wf32 = [s.sb(f"wf32{i}", [128, 4096], F32) for i in range(2)]
        kstg = [s.sb(f"kstg{i}", [128, TOK], BF16) for i in range(2)]
        vstg = [s.sb(f"vstg{i}", [128, 512], BF16) for i in range(2)]
        pb = [s.ps(f"pb{i}", [128, 512], F32) for i in range(8)]

        te.wait_ge(sVE, s.p1["ve"])

        wbuf_free = [None, None]
        kstg_free = [None, None]
        vstg_free = [None, None]
        bank_free = [None] * 8
        kin_last = vin_last = 0

        wf32_free = [None, None]

        def stream_w(src_f32_blk, wdst, par, ncols_half):
            rec = None
            for hh, blk in enumerate(src_f32_blk):
                fpar = (par * 2 + hh) % 2
                if wf32_free[fpar] is not None:
                    sy.wait_ge(sVE, wf32_free[fpar])
                d_rec = s.dma(sy, wf32[fpar][:], blk)
                ve.wait_ge(sSY, d_rec)
                if wbuf_free[par] is not None:
                    ve.wait_ge(sTE, wbuf_free[par])
                ve.tensor_copy(wdst[:, hh * 4096:(hh + 1) * 4096],
                               wf32[fpar][:]).then_inc(sVE, 1)
                rec = cVE.inc()
                wf32_free[fpar] = rec
            return rec

        for grp in range(8):
            par = grp % 2
            w_rec = stream_w([s.wqkp_d[grp]], wbuf[par], par, 1)
            te.wait_ge(sVE, w_rec)
            banks = [pb[4 * par + i] for i in range(4)]
            for i in range(4):
                if bank_free[4 * par + i] is not None:
                    sem, v = bank_free[4 * par + i]
                    te.wait_ge(sem, v)
            last = None
            for c in range(NCH):
                for m in range(2):
                    w_ap = wbuf[par][:, c * 256 + m * 128: c * 256 + (m + 1) * 128]
                    for sp in range(2):
                        last = te.matmul(
                            banks[2 * m + sp][:], w_ap,
                            s.xnT[:, c * TOK + sp * 512: c * TOK + (sp + 1) * 512],
                            start=(c == 0), stop=(c == NCH - 1))
            last.then_inc(sTE, 1)
            t_grp = cTE.inc()
            wbuf_free[par] = t_grp
            for m in range(2):
                mk = 2 * grp + m
                se.wait_ge(sTE, t_grp)
                if kstg_free[m] is not None:
                    se.wait_ge(sSY, kstg_free[m])
                lastse = None
                for sp in range(2):
                    lastse = se.activation(
                        out=kstg[m][:, sp * 512:(sp + 1) * 512],
                        in_=banks[2 * m + sp][:], func=AF.Identity,
                        bias=s.c_bqkv[:, 16 + mk:17 + mk], scale=1.0)
                lastse.then_inc(sSE, 1)
                s_ev = cSE.inc()
                for sp in range(2):
                    bank_free[4 * par + 2 * m + sp] = (sSE, s_ev)
                sy.wait_ge(sSE, s_ev)
                rec = s.dma(sy, s.kin[mk * 128:(mk + 1) * 128, :], kstg[m][:])
                kstg_free[m] = rec
                kin_last = rec

        for vs in range(4):
            par = vs % 2
            w_rec = stream_w([s.wqvp_d[vs, :, 0:4096], s.wqvp_d[vs, :, 4096:8192]],
                             wbuf[par], par, 2)
            te.wait_ge(sVE, w_rec)
            for tg in range(2):
                banks = [pb[4 * tg + i] for i in range(4)]
                for i in range(4):
                    if bank_free[4 * tg + i] is not None:
                        sem, v = bank_free[4 * tg + i]
                        te.wait_ge(sem, v)
                last = None
                for c in range(NCH):
                    for ti in range(4):
                        t = 4 * tg + ti
                        last = te.matmul(
                            banks[ti][:],
                            s.xnT[:, c * TOK + t * 128: c * TOK + (t + 1) * 128],
                            wbuf[par][:, c * 512:(c + 1) * 512],
                            start=(c == 0), stop=(c == NCH - 1))
                last.then_inc(sTE, 1)
                t_grp = cTE.inc()
                if tg == 1:
                    wbuf_free[par] = t_grp
                for ti in range(4):
                    t = 4 * tg + ti
                    sp = t % 2
                    ve.wait_ge(sTE, t_grp)
                    if vstg_free[sp] is not None:
                        ve.wait_ge(sSY, vstg_free[sp])
                    ve.tensor_copy(vstg[sp][:], banks[ti][:]).then_inc(sVE, 1)
                    v_ev = cVE.inc()
                    bank_free[4 * tg + ti] = (sVE, v_ev)
                    sy.wait_ge(sVE, v_ev)
                    rec = s.dma(sy, s.vin[t * 128:(t + 1) * 128, vs * 512:(vs + 1) * 512],
                                vstg[sp][:])
                    vstg_free[sp] = rec
                    vin_last = rec

        if DO_COLLECTIVES:
            gp.wait_ge(sSY, s.cSY.v)
            gp.wait_ge(sVE, s.cVE.v)
            gp.collective_compute("AllGather", ALU.bypass, replica_groups=GROUPS,
                                  ins=[s.kvin[:, :]], outs=[s.kvout[:, :]]).then_inc(s.sCC, 1)
            s.cc_k = s.cc_v = s.cCC.inc()

        if DO_COLLECTIVES:
            sy.wait_ge(s.sCC, s.cc_v)
        se.wait_ge(sVE, s.p0_ve)
        for grp in range(8):
            par = grp % 2
            w_rec = stream_w([s.wqqp_d[grp]], wbuf[par], par, 1)
            te.wait_ge(sVE, w_rec)
            banks = [pb[4 * par + i] for i in range(4)]
            for i in range(4):
                if bank_free[4 * par + i] is not None:
                    sem, v = bank_free[4 * par + i]
                    te.wait_ge(sem, v)
            last = None
            for c in range(NCH):
                for m in range(2):
                    w_ap = wbuf[par][:, c * 256 + m * 128: c * 256 + (m + 1) * 128]
                    for sp in range(2):
                        last = te.matmul(
                            banks[2 * m + sp][:], w_ap,
                            s.xnT[:, c * TOK + sp * 512: c * TOK + (sp + 1) * 512],
                            start=(c == 0), stop=(c == NCH - 1))
            last.then_inc(sTE, 1)
            t_grp = cTE.inc()
            wbuf_free[par] = t_grp
            for m in range(2):
                mq = 2 * grp + m
                se.wait_ge(sTE, t_grp)
                if kstg_free[m] is not None:
                    se.wait_ge(sSY, kstg_free[m])
                lastse = None
                for sp in range(2):
                    lastse = se.activation(
                        out=kstg[m][:, sp * 512:(sp + 1) * 512],
                        in_=banks[2 * m + sp][:], func=AF.Identity,
                        bias=s.c_bqs[:, mq:mq + 1], scale=QSCALE)
                lastse.then_inc(sSE, 1)
                s_ev = cSE.inc()
                for sp in range(2):
                    bank_free[4 * par + 2 * m + sp] = (sSE, s_ev)
                sy.wait_ge(sSE, s_ev)
                rec = s.dma(sy, s.qts[mq * 128:(mq + 1) * 128, :], kstg[m][:])
                kstg_free[m] = rec

        for nm, buf in (("kin", s.kin), ("vin", s.vin), ("qts", s.qts)):
            if nm in s.taps:
                sy.wait_ge(sSY, s.cSY.v)
                s.dma(sy, s.taps[nm][:, :], buf[:, :])

        s.p2 = dict(te=cTE.v, ve=cVE.v, se=cSE.v, sy=s.cSY.v, gp=s.cGP.v)
        s.free(*[f"pb{i}" for i in reversed(range(8))])
        s.free("vstg1", "vstg0", "kstg1", "kstg0",
               "wf321", "wf320", "wbuf1", "wbuf0", "xnT")

    def p4_ffn(self):
        s = self
        s.phase_barrier()
        te, ve, se, sy, gp = s.te, s.ve, s.se, s.sy, s.gp
        sTE, sVE, sSE, sSY, sGP = s.sTE, s.sVE, s.sSE, s.sSY, s.sGP
        cTE, cVE, cSE = s.cTE, s.cVE, s.cSE
        if s.cCC.v:
            gp.wait_ge(s.sCC, s.cCC.v)

        wsb = [s.sb(f"wsb{i}", [128, 8 * 512], BF16) for i in range(2)]
        xqG = s.sb("xqG", [128, NCH * 512], BF16)
        x1h = s.sb("x1h", [128, 4 * 512], F32)
        tmp2 = [s.sb(f"tmp2{i}", [128, 512], F32) for i in range(2)]
        gstg = tmp2

        wsb_free = [None, None]
        gstg_free = [None, None]
        x1h_free = None
        half_i = 0

        def cast_block(dst_bf, src_f32_blk, ncols, guard):
            if guard is not None:
                gp.wait_ge(guard[0], guard[1])
            s.dma(gp, dst_bf[:, 0:ncols], src_f32_blk[:, 0:ncols])
            return s.dma(gp, s.sent_s[:], s.rid[:, :])

        def stream_w13(pass_i, hh):
            nonlocal half_i
            par = half_i % 2
            half_i += 1
            guard = (sTE, wsb_free[par]) if wsb_free[par] is not None else None
            rec = cast_block(wsb[par][:, 0:8 * 512], s.w13p_d[pass_i, hh],
                             8 * 512, guard)
            return par, rec

        for G in range(2):
            iqT = s.sb(f"iqT_{G}", [128, 4 * HID], BF16)
            gated = s.sb(f"gated_{G}", [128, 4 * HID], F32)
            iq_tok = s.sb(f"iq_tok_{G}", [128, 512], BF16)

            sy.wait_ge(s.sTE, s.cTE.v)
            src = _pck(s.xqs[:, G * 512:(G + 1) * 512])
            xq_rec = s.dma(sy, _sck(xqG[:], NCH), src)

            pbF = [s.ps(f"pbF{G}_{i}", [128, 512], F32) for i in range(8)]
            bankF_free = [None] * 8
            te.wait_ge(sSY, xq_rec)
            te.wait_ge(sSE, cSE.v)
            te.wait_ge(sVE, cVE.v)
            se.wait_ge(sSY, s.cSY.v)
            ve.wait_ge(sSY, s.cSY.v)
            for j in range(11):
                for part in range(2):
                    pass_i = 2 * j + part
                    halves = [stream_w13(pass_i, hh) for hh in range(2)]
                    banks = [pbF[4 * part + i] for i in range(4)]
                    for i in range(4):
                        if bankF_free[4 * part + i] is not None:
                            sem, v = bankF_free[4 * part + i]
                            te.wait_ge(sem, v)
                    last = None
                    for c in range(NCH):
                        hh = c // 8
                        par, rec = halves[hh]
                        if c % 8 == 0:
                            te.wait_ge(sGP, rec)
                        w_ap = wsb[par][:, (c % 8) * 512:(c % 8 + 1) * 512]
                        for ti in range(4):
                            last = te.matmul(
                                banks[ti][:],
                                xqG[:, c * 512 + ti * 128: c * 512 + (ti + 1) * 128],
                                w_ap, start=(c == 0), stop=(c == NCH - 1))
                        if c == 7:
                            last.then_inc(sTE, 1)
                            wsb_free[halves[0][0]] = cTE.inc()
                    last.then_inc(sTE, 1)
                    t_pass = cTE.inc()
                    wsb_free[halves[1][0]] = t_pass
                    if part == 0:
                        se.wait_ge(sTE, t_pass)
                        if x1h_free is not None:
                            se.wait_ge(sVE, x1h_free)
                        lastse = None
                        for ti in range(4):
                            lastse = se.activation(out=x1h[:, ti * 512:(ti + 1) * 512],
                                                   in_=banks[ti][:], func=AF.Relu)
                        lastse.then_inc(sSE, 1)
                        s_x1 = cSE.inc()
                        for ti in range(4):
                            bankF_free[ti] = (sSE, s_x1)
                    else:
                        ve.wait_ge(sTE, t_pass)
                        ve.wait_ge(sSE, s_x1)
                        lastve = None
                        for ti in range(4):
                            tb = tmp2[ti % 2]
                            ve.tensor_mul(tb[:], x1h[:, ti * 512:(ti + 1) * 512],
                                          x1h[:, ti * 512:(ti + 1) * 512])
                            lastve = ve.tensor_mul(
                                gated[:, ti * HID + j * 512: ti * HID + (j + 1) * 512],
                                tb[:], banks[ti][:])
                        lastve.then_inc(sVE, 1)
                        v_gat = cVE.inc()
                        x1h_free = v_gat
                        for ti in range(4):
                            bankF_free[4 + ti] = (sVE, v_gat)
            f1a_ve = cVE.v
            s.free(*[f"pbF{G}_{i}" for i in reversed(range(8))])

            tpb = [s.ps(f"tpbF{G}_{i}", [128, 1024], BF16) for i in range(2)]
            tp_free = [None, None]
            batch_i = 0
            tmp2_safe = f1a_ve
            for ti in range(4):
                t = 4 * G + ti
                gt = gated[:, ti * HID:(ti + 1) * HID]
                se.wait_ge(sVE, tmp2_safe)
                lastse = None
                for sl in range(11):
                    lastse = se.activation(out=tmp2[sl % 2][:],
                                           in_=gt[:, sl * 512:(sl + 1) * 512],
                                           func=AF.Square,
                                           accum_out=s.stats[:, 12 + sl:13 + sl])
                lastse.then_inc(sSE, 1)
                s_ssq = cSE.inc()
                ve.wait_ge(sSE, s_ssq)
                ve.reduce_sum(s.stats[:, 24:25], s.stats[:, 12:23], axis=AX.X)
                ve.tensor_mul(gt[:], gt[:], s.c_nw[:])
                ve.reduce_max(s.stats[:, 25:26], gt[:], axis=AX.X,
                              apply_absolute_value=True)
                ve.tensor_mul(s.stats[:, 26:27], s.sxr[:, t:t + 1], s.sxr[:, t:t + 1])
                ve.tensor_mul(s.stats[:, 26:27], s.stats[:, 26:27], s.sxr[:, t:t + 1])
                ve.tensor_mul(s.stats[:, 27:28], s.stats[:, 26:27], s.stats[:, 26:27])
                ve.tensor_mul(s.stats[:, 27:28], s.stats[:, 27:28], s.stats[:, 24:25])
                ve.tensor_scalar_mul(s.stats[:, 27:28], s.stats[:, 27:28], 1.0 / HID)
                ve.tensor_scalar_add(s.stats[:, 27:28], s.stats[:, 27:28], EPS)
                ve.reciprocal(s.stats[:, 28:29], s.stats[:, 27:28]).then_inc(sVE, 1)
                v_pre = cVE.inc()
                se.wait_ge(sVE, v_pre)
                se.activation(out=s.stats[:, 29:30], in_=s.stats[:, 28:29],
                              func=AF.Sqrt).then_inc(sSE, 1)
                s_rs = cSE.inc()
                ve.wait_ge(sSE, s_rs)
                ve.tensor_mul(s.stats[:, 29:30], s.stats[:, 29:30], s.stats[:, 26:27])
                ve.tensor_mul(s.stats[:, 30:31], s.stats[:, 29:30], s.stats[:, 25:26])
                ve.tensor_scalar_max(s.stats[:, 30:31], s.stats[:, 30:31], 1e-5)
                ve.reciprocal(s.stats[:, 31:32], s.stats[:, 30:31])
                ve.tensor_mul(s.stats[:, 31:32], s.stats[:, 31:32], s.stats[:, 29:30])
                ve.tensor_scalar_mul(s.stats[:, 31:32], s.stats[:, 31:32], 127.0)
                ve.tensor_scalar_mul(s.s2r[:, t:t + 1], s.stats[:, 30:31],
                                     1.0 / 127.0).then_inc(sVE, 1)
                v_cs = cVE.inc()
                prev_sub = None
                iq_free = None
                for sl in range(11):
                    tb = tmp2[sl % 2]
                    se.wait_ge(sVE, v_cs if prev_sub is None else prev_sub)
                    se.activation(out=tb[:], in_=gt[:, sl * 512:(sl + 1) * 512],
                                  func=AF.Copy, scale=s.stats[:, 31:32],
                                  bias=MAGIC).then_inc(sSE, 1)
                    s_r = cSE.inc()
                    ve.wait_ge(sSE, s_r)
                    if iq_free is not None:
                        ve.wait_ge(sTE, iq_free)
                    ve.tensor_scalar_add(iq_tok[:], tb[:], -MAGIC).then_inc(sVE, 1)
                    prev_sub = cVE.inc()
                    bank = batch_i % 2
                    batch_i += 1
                    if tp_free[bank] is not None:
                        te.wait_ge(sVE, tp_free[bank])
                    te.wait_ge(sVE, prev_sub)
                    last = None
                    for jj in range(4):
                        last = te.transpose(tpb[bank][:, jj * 128:(jj + 1) * 128],
                                            iq_tok[:, jj * 128:(jj + 1) * 128],
                                            s.c_id[:])
                    last.then_inc(sTE, 1)
                    t_b = cTE.inc()
                    iq_free = t_b
                    ve.wait_ge(sTE, t_b)
                    lastve = None
                    for jj in range(4):
                        hc = sl * 4 + jj
                        lastve = ve.tensor_copy(
                            iqT[:, ti * HID + hc * 128: ti * HID + (hc + 1) * 128],
                            tpb[bank][:, jj * 128:(jj + 1) * 128])
                    lastve.then_inc(sVE, 1)
                    tp_free[bank] = cVE.inc()
                tmp2_safe = prev_sub
            f1b_ve = cVE.v
            s.free(f"tpbF{G}_1", f"tpbF{G}_0")

            pb2 = [s.ps(f"pb2{G}_{i}", [128, 512], F32) for i in range(8)]
            bank2_free = [None] * 8
            te.wait_ge(sVE, f1b_ve)
            hgs = [(hg * 8, min(8, NHCH - hg * 8)) for hg in range(6)]
            for ds in range(4):
                banks = [pb2[4 * (ds % 2) + i] for i in range(4)]
                for i in range(4):
                    if bank2_free[4 * (ds % 2) + i] is not None:
                        sem, v = bank2_free[4 * (ds % 2) + i]
                        te.wait_ge(sem, v)
                last = None
                for hgi, (hc0, nhc) in enumerate(hgs):
                    par = half_i % 2
                    half_i += 1
                    guard = (sTE, wsb_free[par]) if wsb_free[par] is not None else None
                    w_rec = cast_block(wsb[par][:, 0:nhc * 512],
                                       s.w2p_d[ds, hgi, :, 0:nhc * 512],
                                       nhc * 512, guard)
                    te.wait_ge(sGP, w_rec)
                    for hcj in range(nhc):
                        hc = hc0 + hcj
                        for ti in range(4):
                            last = te.matmul(
                                banks[ti][:],
                                iqT[:, ti * HID + hc * 128: ti * HID + (hc + 1) * 128],
                                wsb[par][:, hcj * 512:(hcj + 1) * 512],
                                start=(hc == 0), stop=(hc == NHCH - 1))
                    last.then_inc(sTE, 1)
                    wsb_free[par] = cTE.inc()
                t_ds = cTE.v
                for ti in range(4):
                    t = 4 * G + ti
                    pp = ti % 2
                    se.wait_ge(sTE, t_ds)
                    if gstg_free[pp] is not None:
                        se.wait_ge(sSY, gstg_free[pp])
                    se.activation(out=gstg[pp][:], in_=banks[ti][:], func=AF.Silu,
                                  scale=s.s2r[:, t:t + 1]).then_inc(sSE, 1)
                    s_g = cSE.inc()
                    bank2_free[4 * (ds % 2) + ti] = (sSE, s_g)
                    sy.wait_ge(sSE, s_g)
                    rec = s.dma(sy, s.gsp[t * 128:(t + 1) * 128, ds * 512:(ds + 1) * 512],
                                gstg[pp][:])
                    gstg_free[pp] = rec
            s.free(*[f"pb2{G}_{i}" for i in reversed(range(8))])
            s.free(f"iq_tok_{G}", f"gated_{G}", f"iqT_{G}")

        if "gsp" in s.taps:
            sy.wait_ge(sSY, s.cSY.v)
            s.dma(sy, s.taps["gsp"][:, :], s.gsp[:, :])
        s.p4 = dict(te=cTE.v, ve=cVE.v, se=cSE.v, sy=s.cSY.v, gp=s.cGP.v)
        s.free("tmp21", "tmp20", "x1h", "xqG", "wsb1", "wsb0")

    def p5_attention(self):
        s = self
        s.phase_barrier()
        te, ve, se, sy, gp = s.te, s.ve, s.se, s.sy, s.gp
        sTE, sVE, sSE, sSY, sGP, sCC = s.sTE, s.sVE, s.sSE, s.sSY, s.sGP, s.sCC
        cTE, cVE, cSE = s.cTE, s.cVE, s.cSE

        s.c_onesr = s.sb("c_onesr", [1, 128], F32)
        s.c_onesr16 = s.sb("c_onesr16", [1, 128], BF16)
        ve.memset(s.c_onesr[:], 1.0)
        ve.memset(s.c_onesr16[:], 1.0).then_inc(sVE, 1)
        s.onesr_rec = cVE.inc()
        s.wo_s = s.sb("wo_s", [128, NCH * D], BF16)
        s.c_bo = s.sb("c_bo", [128, D], F32)
        s.attnT = s.sb("attnT", [128, NCH * TOK], BF16)
        s.c_masks = s.sb("c_masks", [128, 2048], F32)
        s.c_alibi = s.sb("c_alibi", [128, 512], F32)
        s.dma(sy, s.c_bo[:], s.bo_d[:, :])
        s.dma(sy, s.c_alibi[:], s.alibi_d[:, :])
        mask_rec = s.dma(sy, s.c_masks[:], s.masks_d[:, :])

        wo32 = [s.sb(f"wo32{i}", [128, D], F32) for i in range(2)]
        wo_rec = 0
        wo32_free = [None, None]
        for c in range(NCH):
            fp = c % 2
            if wo32_free[fp] is not None:
                sy.wait_ge(sSE, wo32_free[fp])
            sy.dma_start(out=wo32[fp][:],
                         in_=s.wo_d[c * 128:(c + 1) * 128, :]).then_inc(s.sGP, 16)
            s.cGP.inc(16)
            se.wait_ge(s.sGP, s.cGP.v)
            se.activation(out=s.wo_s[:, c * D:(c + 1) * D], in_=wo32[fp][:],
                          func=AF.Copy).then_inc(sSE, 1)
            wo_rec = cSE.inc()
            wo32_free[fp] = wo_rec
        s.wo_rec = wo_rec

        aqrow = [s.sb(f"aqrow{i}", [1, 512], BF16) for i in range(4)]
        kbuf = [s.sb(f"kbuf{i}", [128, 4096], BF16) for i in range(2)]
        vbuf = [s.sb(f"vbuf{i}", [128, 4096], BF16) for i in range(2)]
        qbuf = [s.sb(f"qbuf{i}", [128, TOK], BF16) for i in range(2)]
        phat = s.sb("phat", [128, 4 * 512], BF16)
        Lrow = s.sb("Lrow", [128, 512], F32)
        bcs = s.sb("bcs", [128, 512], F32)
        atmp = s.sb("atmp", [128, 512], F32)
        zb = [s.ps(f"zb{i}", [128, 512], F32) for i in range(2)]
        pv = [s.ps(f"pv{i}", [128, 512], F32) for i in range(2)]
        lb = [s.ps(f"lb{i}", [128, 512], F32) for i in range(2)]
        bcb = s.ps("bcb", [128, 512], F32)
        te.wait_ge(s.sSE, s.cSE.v)
        te.wait_ge(s.sVE, s.cVE.v)

        kv_rec = []
        self._h_done_target = [None] * H
        te_base = cTE.v
        per_h_te = 2 * 36 + 2
        for h in range(H):
            if h >= 2:
                sy.wait_ge(sTE, te_base + (h - 1) * per_h_te)
            if h == 0:
                sy.wait_ge(sCC, s.cc_v)
            for qs01 in range(2):
                s.dma(sy, aqrow[(2 * h + qs01) % 4][:],
                      s.aq_d[2 * h + qs01:2 * h + qs01 + 1, :])
            rec = s.dma(sy, qbuf[h % 2][:], s.qts[h * 128:(h + 1) * 128, :])
            for ck in range(8):
                rank, half = _crh(ck)
                slab = 4096 * rank
                rec = s.dma(sy, kbuf[h % 2][:, ck * 512:(ck + 1) * 512],
                            s.kvout[slab + 128 * h: slab + 128 * (h + 1),
                                    512 * half: 512 * (half + 1)])
                vout_r = s.kvout[slab + 2048: slab + 4096, :].rearrange(
                    "(t two) k -> t (two k)", two=2)
                vsrc = _pck(vout_r[512 * half: 512 * (half + 1),
                                   h * 128:(h + 1) * 128])
                rec = s.dma(sy, _sck(vbuf[h % 2][:, ck * 512:(ck + 1) * 512], 4), vsrc)
            kv_rec.append(rec)

        ve.wait_ge(sSY, mask_rec)
        for eng in (te, se, ve):
            eng.wait_ge(sSY, s.rid_rec)
        idx = s.nc.values_load(s.rid_s[0:1, 0:1],
                               engines=[te.engine, se.engine, ve.engine],
                               min_val=0, max_val=3, skip_runtime_bounds_check=True)
        snap = (cTE.v, cVE.v, cSE.v)
        ends = []
        for r in s.nc.Switch(engines=[te, se, ve._ve], index=[idx, idx, idx], n=4):
            cTE.v, cVE.v, cSE.v = snap
            self._emit_attn_variant(r, kbuf, vbuf, qbuf, phat, Lrow, bcs, atmp,
                                    zb, pv, lb, bcb, kv_rec, aqrow)
            ends.append((cTE.v, cVE.v, cSE.v))
        assert all(e == ends[0] for e in ends), f"variant counts differ: {ends}"
        cTE.v, cVE.v, cSE.v = ends[0]
        assert cTE.v - te_base == H * per_h_te, (cTE.v - te_base, H * per_h_te)

        if "attnT" in s.taps:
            sy.wait_ge(sSE, cSE.v)
            s.dma(sy, s.taps["attnT"][:, :], s.attnT[:])
        if "wos" in s.taps:
            sy.wait_ge(sSE, s.wo_rec)
            s.dma(sy, s.taps["wos"][:, :], s.wo_s[:])
        for nm, buf in (("kbuf1", kbuf[1]), ("vbuf1", vbuf[1]), ("qbuf1", qbuf[1]),
                        ("lrow", Lrow), ("bcs", bcs), ("atmp", atmp),
                        ("phat", phat)):
            if nm in s.taps:
                sy.wait_ge(sSE, cSE.v)
                sy.wait_ge(sVE, cVE.v)
                s.dma(sy, s.taps[nm][:, :], buf[:])
        s.p5 = dict(te=cTE.v, ve=cVE.v, se=cSE.v, sy=s.cSY.v, gp=s.cGP.v)
        s.free("bcb", "lb1", "lb0", "pv1", "pv0", "zb1", "zb0", "atmp", "bcs",
               "Lrow", "phat", "qbuf1", "qbuf0", "vbuf1", "vbuf0", "kbuf1", "kbuf0",
               "aqrow3", "aqrow2", "aqrow1", "aqrow0", "wo321", "wo320")

    def _emit_attn_variant(self, r, kbuf, vbuf, qbuf, phat, Lrow, bcs, atmp,
                           zb, pv, lb, bcb, kv_rec, aqrow):
        s = self
        te, ve, se = s.te, s.ve, s.se
        sTE, sVE, sSE, sSY = s.sTE, s.sVE, s.sSE, s.sSY
        cTE, cVE, cSE = s.cTE, s.cVE, s.cSE

        pvl_free = [None, None]
        bcb_free = None
        atmp_free = None
        prev_exp_last = None
        for h in range(H):
            te.wait_ge(sSY, kv_rec[h])
            for qs01 in range(2):
                A = 4 * r + 4 if qs01 == 0 else 32 - 4 * r
                g2 = (2 * h + qs01) % 2
                exp_rec = {}
                pvl_rec = {}
                phat_rec = {}

                def emit_z(kb):
                    zi = kb % 2
                    if kb >= 2:
                        te.wait_ge(sSE, exp_rec[zi])
                    elif prev_exp_last is not None:
                        te.wait_ge(sSE, prev_exp_last)
                    te.matmul(zb[zi][:], s.c_onesr16[0:1, 0:128],
                              aqrow[(2 * h + qs01) % 4][0:1, :],
                              start=True, stop=False)
                    te.matmul(zb[zi][:], kbuf[h % 2][:, kb * 128:(kb + 1) * 128],
                              qbuf[h % 2][:, qs01 * 512:(qs01 + 1) * 512],
                              start=False, stop=True).then_inc(sTE, 1)
                    z_rec = cTE.inc()
                    if kb >= A - 4:
                        v = kb - (A - 4)
                        ve.wait_ge(sTE, z_rec)
                        ve.tensor_add(zb[zi][:], zb[zi][:],
                                      s.c_masks[:, v * 512:(v + 1) * 512]).then_inc(sVE, 1)
                        z_gate = (sVE, cVE.inc())
                    else:
                        z_gate = (sTE, z_rec)
                    se.wait_ge(z_gate[0], z_gate[1])
                    if kb - 4 in pvl_rec:
                        se.wait_ge(sTE, pvl_rec[kb - 4])
                    elif kb < 4 and pvl_free[1 - g2] is not None:
                        se.wait_ge(sVE, pvl_free[1 - g2])
                    se.activation(out=phat[:, (kb % 4) * 512:(kb % 4 + 1) * 512],
                                  in_=zb[zi][:], func=AF.Exp,
                                  bias=s.c_alibi[:, h * 32 + kb: h * 32 + kb + 1],
                                  scale=1.0).then_inc(sSE, 1)
                    exp_rec[zi] = cSE.inc()
                    phat_rec[kb] = (sSE, exp_rec[zi])

                emit_z(0)
                for kb in range(A):
                    if kb + 1 < A:
                        emit_z(kb + 1)
                    sem, rec = phat_rec[kb]
                    te.wait_ge(sem, rec)
                    if kb == 0 and pvl_free[g2] is not None:
                        te.wait_ge(sVE, pvl_free[g2])
                    te.matmul(pv[g2][:], vbuf[h % 2][:, kb * 128:(kb + 1) * 128],
                              phat[:, (kb % 4) * 512:(kb % 4 + 1) * 512],
                              start=(kb == 0), stop=(kb == A - 1))
                    te.matmul(lb[g2][0:1, :], s.c_ones[:, 0:1],
                              phat[:, (kb % 4) * 512:(kb % 4 + 1) * 512],
                              start=(kb == 0), stop=(kb == A - 1)).then_inc(sTE, 1)
                    pvl_rec[kb] = cTE.inc()
                ve.wait_ge(sTE, pvl_rec[A - 1])
                if atmp_free is not None:
                    ve.wait_ge(sSE, atmp_free)
                ve.tensor_copy(Lrow[0:1, 0:512], lb[g2][0:1, :])
                ve.reciprocal(Lrow[0:1, 0:512], Lrow[0:1, 0:512]).then_inc(sVE, 1)
                l_rec = cVE.inc()
                te.wait_ge(sVE, l_rec)
                te.matmul(bcb[:], s.c_onesr[0:1, :], Lrow[0:1, 0:512],
                          start=True, stop=True).then_inc(sTE, 1)
                bc_rec = cTE.inc()
                ve.wait_ge(sTE, bc_rec)
                ve.tensor_copy(bcs[:], bcb[:]).then_inc(sVE, 1)
                bcb_free = cVE.inc()
                ve.scalar_tensor_tensor(atmp[:], pv[g2][:], 1.0, bcs[:],
                                        ALU.bypass, ALU.mult).then_inc(sVE, 1)
                ev_rec = cVE.inc()
                pvl_free[g2] = ev_rec
                se.wait_ge(sVE, ev_rec)
                se.activation(out=s.attnT[:, h * TOK + qs01 * 512: h * TOK + (qs01 + 1) * 512],
                              in_=atmp[:], func=AF.Identity,
                              bias=s.c_bqkv[:, 32 + h:33 + h], scale=1.0).then_inc(sSE, 1)
                atmp_free = cSE.inc()
                prev_exp_last = exp_rec[(A - 1) % 2]

    def p6_combine(self):
        s = self
        s.phase_barrier()
        te, ve, se, sy = s.te, s.ve, s.se, s.sy
        sTE, sVE, sSE, sSY, sGP = s.sTE, s.sVE, s.sSE, s.sSY, s.sGP
        cTE, cVE, cSE = s.cTE, s.cVE, s.cSE

        xrl = [s.sb(f"xrl{i}", [128, D], F32) for i in range(2)]
        grl = [s.sb(f"grl{i}", [128, D], F32) for i in range(2)]
        ostg = [s.sb(f"ostg{i}", [128, D], F32) for i in range(2)]
        p6 = [s.ps(f"p6_{i}", [128, 512], F32) for i in range(8)]

        te.wait_ge(sSE, s.p5["se"])
        te.wait_ge(sVE, s.p5["ve"])
        te.wait_ge(sSE, s.wo_rec)
        rl_free = [None, None]
        o_free = [None, None]
        bank6_free = [None] * 8
        for t in range(NT):
            pp = t % 2
            if rl_free[pp] is not None:
                sy.wait_ge(sVE, rl_free[pp])
            s.dma(sy, xrl[pp][:], s.x_d[t * 128:(t + 1) * 128, :])
            x_rec = s.dma(sy, grl[pp][:], s.gsp[t * 128:(t + 1) * 128, :])
            banks = [p6[4 * pp + i] for i in range(4)]
            for i in range(4):
                if bank6_free[4 * pp + i] is not None:
                    te.wait_ge(sVE, bank6_free[4 * pp + i])
            last = None
            for c in range(NCH):
                a_ap = s.attnT[:, c * TOK + t * 128: c * TOK + (t + 1) * 128]
                for ds in range(4):
                    last = te.matmul(banks[ds][:], a_ap,
                                     s.wo_s[:, c * D + ds * 512: c * D + (ds + 1) * 512],
                                     start=(c == 0), stop=(c == NCH - 1))
            last.then_inc(sTE, 1)
            t_rec = cTE.inc()
            ve.wait_ge(sTE, t_rec)
            ve.wait_ge(sSY, x_rec)
            if o_free[pp] is not None:
                ve.wait_ge(sSY, o_free[pp])
            lastve = None
            for ds in range(4):
                sl = slice(ds * 512, (ds + 1) * 512)
                ve.scalar_tensor_tensor(ostg[pp][:, sl], banks[ds][:], 1.0,
                                        s.c_bo[:, sl], ALU.bypass, ALU.add)
                ve.tensor_mul(ostg[pp][:, sl], ostg[pp][:, sl], grl[pp][:, sl])
                lastve = ve.tensor_add(ostg[pp][:, sl], ostg[pp][:, sl], xrl[pp][:, sl])
            lastve.then_inc(sVE, 1)
            v_rec = cVE.inc()
            rl_free[pp] = v_rec
            for i in range(4):
                bank6_free[4 * pp + i] = v_rec
            sy.wait_ge(sVE, v_rec)
            o_free[pp] = s.dma(sy, s.out_d[t * 128:(t + 1) * 128, :], ostg[pp][:])

    def finish(self):
        s = self
        s.sy.wait_ge(s.sSY, s.cSY.v)
        s.sy.wait_ge(s.sGP, s.cGP.v)
        s.sy.wait_ge(s.sTE, s.cTE.v)
        s.sy.wait_ge(s.sVE, s.cVE.v)
        s.sy.wait_ge(s.sSE, s.cSE.v)


def _host_consts():
    slopes = np.array([1.0 / 2 ** (k / 2) for k in range(H)], np.float32)
    alibi = np.zeros((128, 512), np.float32)
    j = np.arange(128, dtype=np.float32)
    for h in range(H):
        for kb in range(32):
            alibi[:, h * 32 + kb] = slopes[h] * (128 * kb + j) - SHIFT
    masks = np.zeros((128, 2048), np.float32)
    q = np.arange(512, dtype=np.float32)
    for v in range(4):
        masks[:, v * 512:(v + 1) * 512] = np.where(
            j[:, None] + 128 * v <= q[None, :], 0.0, -1e30)
    return alibi, masks, np.eye(128, dtype=ml_dtypes.bfloat16)


def make_in_maps(L0_attn_x, L0_attn_Wqkv, L0_attn_bqkv, L0_attn_Wo, L0_attn_bo,
                 L0_ffn_w13, L0_ffn_w2, L0_ffn_norm_w):
    x = np.asarray(L0_attn_x, np.float32)
    wqkv = np.asarray(L0_attn_Wqkv, np.float32)
    bqkvT = np.ascontiguousarray(np.asarray(L0_attn_bqkv, np.float32).reshape(48, 128).T)
    wo = np.ascontiguousarray(np.asarray(L0_attn_Wo, np.float32))
    bo_rep = np.ascontiguousarray(
        np.broadcast_to(np.asarray(L0_attn_bo, np.float32), (128, D)))
    nw_rep = np.ascontiguousarray(
        np.broadcast_to(np.asarray(L0_ffn_norm_w, np.float32), (128, HID)))
    wq6 = wqkv.reshape(NCH, 128, QKVC)
    def _qtile(base, K, ngrp):
        a = wq6[:, :, base:base + ngrp * K].reshape(NCH, 128, ngrp, K)
        return np.ascontiguousarray(a.transpose(2, 1, 0, 3).reshape(ngrp, 128, NCH * K))
    wqkp = _qtile(D, 256, 8)
    wqvp = _qtile(2 * D, 512, 4)
    wqqp = _qtile(0, 256, 8)
    w13t = np.asarray(L0_ffn_w13, np.float32).T
    a = w13t.reshape(2, 8, 128, 2, 11, 512)
    w13p = np.ascontiguousarray(
        a.transpose(4, 3, 0, 2, 1, 5).reshape(22, 2, 128, 4096))
    w2t = np.asarray(L0_ffn_w2, np.float32).T
    w2p = np.zeros((4, 6, 128, 4096), np.float32)
    for hg in range(6):
        nhc = min(8, NHCH - hg * 8)
        blk = w2t[hg * 8 * 128:(hg * 8 + nhc) * 128, :]
        blk = blk.reshape(nhc, 128, 4, 512).transpose(2, 1, 0, 3)
        w2p[:, hg, :, 0:nhc * 512] = blk.reshape(4, 128, nhc * 512)
    w2p = np.ascontiguousarray(w2p)
    alibi, masks, ident = _host_consts()
    slopes = np.array([1.0 / 2 ** (k / 2) for k in range(H)], np.float32)
    qv = np.arange(512, dtype=np.float32)
    in_maps = []
    for core in range(8):
        b, r = core // 4, core % 4
        aq = np.zeros((32, 512), np.float32)
        for h in range(H):
            for qs01 in range(2):
                chunk = r if qs01 == 0 else 7 - r
                aq[2 * h + qs01] = -slopes[h] * (chunk * 512 + qv)
        aq16 = aq.astype(ml_dtypes.bfloat16)
        x_own = np.ascontiguousarray(np.concatenate(
            [x[b, 512 * r:512 * (r + 1)], x[b, 512 * (7 - r):512 * (8 - r)]], axis=0))
        in_maps.append({
            "rid": np.array([[r]], np.int32), "x": x_own,
            "wqkp": wqkp, "wqvp": wqvp, "wqqp": wqqp,
            "bqkvT": bqkvT, "wo": wo, "bo_rep": bo_rep, "w13p": w13p,
            "w2p": w2p, "nw_rep": nw_rep, "alibi": alibi, "masks": masks,
            "ident": ident, "aq": aq16,
        })
    return in_maps


def kernel(L0_attn_x, L0_attn_Wqkv, L0_attn_bqkv, L0_attn_Wo, L0_attn_bo,
           L0_ffn_w13, L0_ffn_w2, L0_ffn_norm_w):
    in_maps = make_in_maps(L0_attn_x, L0_attn_Wqkv, L0_attn_bqkv, L0_attn_Wo,
                           L0_attn_bo, L0_ffn_w13, L0_ffn_w2, L0_ffn_norm_w)
    nc = MK().build()
    res = run_bass_kernel_spmd(nc, in_maps, core_ids=list(range(8)))
    out = np.empty((B, S, D), np.float32)
    for core in range(8):
        b, r = core // 4, core % 4
        o = res.results[core]["out"]
        out[b, 512 * r:512 * (r + 1)] = o[:512]
        out[b, 512 * (7 - r):512 * (8 - r)] = o[512:]
    return out


# version 12
# speedup vs baseline: 1.2788x; 1.0786x over previous
import math

import numpy as np
import ml_dtypes
import concourse.bass as bass
import concourse.mybir as mybir
from concourse.bass_utils import run_bass_kernel_spmd

F32 = mybir.dt.float32
BF16 = mybir.dt.bfloat16
I32 = mybir.dt.int32
AF = mybir.ActivationFunctionType
ALU = mybir.AluOpType
AX = mybir.AxisListType

B, S, D, H, DH = 2, 4096, 2048, 16, 128
QKVC = 3 * D
HID, FF = 5632, 11264
TOK, NT = 1024, 8
NCH = D // 128
NHCH = HID // 128
QSCALE = 1.0 / math.sqrt(DH)
MAGIC = float(np.float32(12582912.0))
SHIFT = 12.0
EPS = 1e-5
GROUPS = [[0, 1, 2, 3], [4, 5, 6, 7]]

TAPS = []
BUILD_UPTO = 99
DO_COLLECTIVES = True


def _crh(ck):
    return (ck, 0) if ck < 4 else (7 - ck, 1)



def _pck(dram_ap, p=128):
    return dram_ap.rearrange("(c p) k -> p c k", p=p)


def _sck(sbuf_ap, c):
    return sbuf_ap.rearrange("p (c k) -> p c k", c=c)



def _ap_span(ap):
    try:
        name = ap.tensor.name
    except Exception:
        name = str(ap.tensor)
    off = ap.offset
    reach = 1
    for step, cnt in ap.ap:
        reach += abs(step) * (cnt - 1)
    return (name, off, off + reach)


class VDrain:

    WINDOW = 5

    def __init__(self, ve):
        self._ve = ve
        self._hist = []

    def _chk(self, ins, outs):
        spans = [_ap_span(a) for a in ins + outs if a is not None]
        hit = False
        for (n, lo, hi) in spans:
            for (n2, lo2, hi2) in self._hist:
                if n == n2 and lo < hi2 and lo2 < hi:
                    hit = True
                    break
            if hit:
                break
        if hit:
            self._ve.drain()
            self._hist = []

    def _rec(self, outs):
        self._hist.extend(_ap_span(a) for a in outs if a is not None)
        self._hist = self._hist[-self.WINDOW:]

    def _emit(self, fn, ins, outs, *args, **kw):
        self._chk(ins, outs)
        r = fn(*args, **kw)
        self._rec(outs)
        return r

    def tensor_copy(self, out, in_):
        return self._emit(self._ve.tensor_copy, [in_], [out], out, in_)

    def tensor_add(self, out, a, b):
        return self._emit(self._ve.tensor_add, [a, b], [out], out, a, b)

    def tensor_sub(self, out, a, b):
        return self._emit(self._ve.tensor_sub, [a, b], [out], out, a, b)

    def tensor_mul(self, out, a, b):
        return self._emit(self._ve.tensor_mul, [a, b], [out], out, a, b)

    def tensor_scalar_mul(self, out, a, sc):
        ins = [a] + ([sc] if not isinstance(sc, float) else [])
        return self._emit(self._ve.tensor_scalar_mul, ins, [out], out, a, sc)

    def tensor_scalar_add(self, out, a, sc):
        ins = [a] + ([sc] if not isinstance(sc, float) else [])
        return self._emit(self._ve.tensor_scalar_add, ins, [out], out, a, sc)

    def tensor_scalar_max(self, out, a, sc):
        ins = [a] + ([sc] if not isinstance(sc, float) else [])
        return self._emit(self._ve.tensor_scalar_max, ins, [out], out, a, sc)

    def reciprocal(self, out, in_):
        return self._emit(self._ve.reciprocal, [in_], [out], out, in_)

    def reduce_sum(self, out, in_, axis, **kw):
        return self._emit(self._ve.reduce_sum, [in_], [out], out, in_, axis=axis, **kw)

    def reduce_max(self, out, in_, axis, **kw):
        return self._emit(self._ve.reduce_max, [in_], [out], out, in_, axis=axis, **kw)

    def scalar_tensor_tensor(self, out, in0, sc, in1, op0, op1):
        ins = [in0, in1] + ([sc] if not isinstance(sc, float) else [])
        return self._emit(self._ve.scalar_tensor_tensor, ins, [out],
                          out, in0, sc, in1, op0, op1)

    def memset(self, ap, v):
        return self._emit(self._ve.memset, [], [ap], ap, v)

    def wait_ge(self, sem, v):
        return self._ve.wait_ge(sem, v)

    def drain(self):
        self._hist = []
        return self._ve.drain()

    @property
    def engine(self):
        return self._ve.engine


class Ctr:
    __slots__ = ("v",)

    def __init__(self):
        self.v = 0

    def inc(self, n=1):
        self.v += n
        return self.v


class MK:
    def __init__(self):
        nc = bass.Bass("TRN2", target_bir_lowering=False)
        self.nc = nc
        self.stack = []
        self.taps = {}
        self.rid = nc.dram_tensor("rid", [1, 1], I32, kind="ExternalInput").ap()
        self.x_d = nc.dram_tensor("x", [TOK, D], F32, kind="ExternalInput").ap()
        self.wqkv_d = nc.dram_tensor("wqkv", [D, QKVC], F32, kind="ExternalInput").ap()
        self.bqkv_d = nc.dram_tensor("bqkvT", [128, 48], F32, kind="ExternalInput").ap()
        self.wo_d = nc.dram_tensor("wo", [D, D], F32, kind="ExternalInput").ap()
        self.bo_d = nc.dram_tensor("bo_rep", [128, D], F32, kind="ExternalInput").ap()
        self.w13_d = nc.dram_tensor("w13t", [D, FF], F32, kind="ExternalInput").ap()
        self.w2_d = nc.dram_tensor("w2t", [HID, D], F32, kind="ExternalInput").ap()
        self.nw_d = nc.dram_tensor("nw_rep", [128, HID], F32, kind="ExternalInput").ap()
        self.alibi_d = nc.dram_tensor("alibi", [128, 512], F32, kind="ExternalInput").ap()
        self.masks_d = nc.dram_tensor("masks", [128, 2048], F32, kind="ExternalInput").ap()
        self.id_d = nc.dram_tensor("ident", [128, 128], BF16, kind="ExternalInput").ap()
        self.aq_d = nc.dram_tensor("aq", [32, 512], BF16, kind="ExternalInput").ap()
        self.out_d = nc.dram_tensor("out", [TOK, D], F32, kind="ExternalOutput").ap()

        self.kvin = nc.dram_tensor("kvin", [2 * D, TOK], BF16, kind="Internal").ap()
        self.kvout = nc.dram_tensor("kvout", [8 * D, TOK], BF16, kind="Internal").ap()
        self.kin = self.kvin[0:D, :]
        self.vin = self.kvin[D:2 * D, :].rearrange("(a two) c -> a (two c)", two=2)
        self.qts = nc.dram_tensor("qts", [D, TOK], BF16, kind="Internal").ap()
        self.xqs = nc.dram_tensor("xqs", [D, TOK], BF16, kind="Internal").ap()
        self.gsp = nc.dram_tensor("gsp", [TOK, D], F32, kind="Internal").ap()
        for name, shape, dt in TAPS:
            self.taps[name.replace("tap_", "")] = nc.dram_tensor(
                name, shape, dt, kind="ExternalOutput").ap()

        self.te, self.ve, self.se = nc.tensor, VDrain(nc.vector), nc.scalar
        self.sy, self.gp = nc.sync, nc.gpsimd

        self.sSY = nc.semaphore("sSY").__enter__()
        self.sGP = nc.semaphore("sGP").__enter__()
        self.sCC = nc.semaphore("sCC").__enter__()
        self.sTE = nc.semaphore("sTE").__enter__()
        self.sVE = nc.semaphore("sVE").__enter__()
        self.sSE = nc.semaphore("sSE").__enter__()
        self.cSY, self.cGP, self.cCC = Ctr(), Ctr(), Ctr()
        self.cTE, self.cVE, self.cSE = Ctr(), Ctr(), Ctr()

    def sb(self, name, shape, dtype):
        g = self.nc.sbuf_tensor(name, shape, dtype)
        t = g.__enter__()
        self.stack.append((g, name))
        return t

    def ps(self, name, shape, dtype):
        g = self.nc.psum_tensor(name, shape, dtype)
        t = g.__enter__()
        self.stack.append((g, name))
        return t

    def free(self, *names):
        for name in names:
            g, n = self.stack.pop()
            assert n == name, f"free order: got {n}, want {name}"
            g.__exit__(None, None, None)

    def phase_barrier(self):
        s = self
        targets = [(s.sSY, s.cSY.v), (s.sGP, s.cGP.v), (s.sTE, s.cTE.v),
                   (s.sVE, s.cVE.v), (s.sSE, s.cSE.v)]
        for eng in (s.te, s.ve, s.se, s.sy, s.gp):
            for sem, v in targets:
                if v:
                    eng.wait_ge(sem, v)

    def dma(self, eng, out, in_):
        if eng is self.sy:
            eng.dma_start(out=out, in_=in_).then_inc(self.sSY, 16)
            return self.cSY.inc(16)
        eng.dma_start(out=out, in_=in_).then_inc(self.sGP, 16)
        return self.cGP.inc(16)

    def build(self):
        self.p0_consts()
        if BUILD_UPTO >= 1:
            self.p1_layernorm()
        if BUILD_UPTO >= 2:
            self.p2_qkv()
        if BUILD_UPTO >= 4:
            self.p4_ffn()
        if BUILD_UPTO >= 5:
            self.p5_attention()
        if BUILD_UPTO >= 6:
            self.p6_combine()
        self.finish()
        return self.nc

    def p0_consts(self):
        s = self
        s.c_nw = s.sb("c_nw", [128, HID], F32)
        s.c_id = s.sb("c_id", [128, 128], BF16)
        s.c_bqkv = s.sb("c_bqkv", [128, 48], F32)
        s.c_bqs = s.sb("c_bqs", [128, 16], F32)
        s.c_ones = s.sb("c_ones", [128, 1], BF16)
        s.c_onesr = s.sb("c_onesr", [1, 128], F32)
        s.c_onesr16 = s.sb("c_onesr16", [1, 128], BF16)
        s.rid_s = s.sb("rid_s", [1, 1], I32)
        s.sxr = s.sb("sxr", [128, NT], F32)
        s.s2r = s.sb("s2r", [128, NT], F32)
        s.stats = s.sb("stats", [128, 32], F32)

        s.dma(s.sy, s.c_nw[:], s.nw_d[:, :])
        s.dma(s.sy, s.c_id[:], s.id_d[:, :])
        s.dma(s.sy, s.c_bqkv[:], s.bqkv_d[:, :])
        s.rid_rec = s.dma(s.sy, s.rid_s[:], s.rid[:, :])
        s.p0_loads = s.cSY.v

        s.ve.wait_ge(s.sSY, s.p0_loads)
        s.ve.memset(s.c_ones[:], 1.0)
        s.ve.memset(s.c_onesr[:], 1.0)
        s.ve.memset(s.c_onesr16[:], 1.0)
        s.ve.tensor_scalar_mul(s.c_bqs[:], s.c_bqkv[:, 0:16], QSCALE).then_inc(s.sVE, 1)
        s.p0_ve = s.cVE.inc()

    def p1_layernorm(self):
        s = self
        te, ve, se, sy = s.te, s.ve, s.se, s.sy
        sTE, sVE, sSE, sSY = s.sTE, s.sVE, s.sSE, s.sSY
        cTE, cVE, cSE = s.cTE, s.cVE, s.cSE

        s.xnT = s.sb("xnT", [128, NCH * TOK], BF16)
        xb = [s.sb(f"xb{i}", [128, D], F32) for i in range(2)]
        dump = s.sb("dump", [128, D], F32)
        xnf = s.sb("xnf", [128, D], F32)
        xnb = s.sb("xnb", [128, D], BF16)
        xqb = s.sb("xqb", [128, D], BF16)
        xqstg = [s.sb(f"xqstg{i}", [128, 512], BF16) for i in range(2)]
        tpb = [s.ps(f"tpb{i}", [128, 1024], BF16) for i in range(2)]

        x_last = {}
        stg_free = [None, None]
        dump_free = None
        tp_free = [None, None]
        batch_i = 0

        for t in range(NT):
            buf = xb[t % 2]
            if t - 2 in x_last:
                for sem, v in x_last[t - 2]:
                    sy.wait_ge(sem, v)
            x_rec = s.dma(sy, buf[:], s.x_d[t * 128:(t + 1) * 128, :])

            ve.wait_ge(sSY, x_rec)
            ve.reduce_sum(s.stats[:, 0:1], buf[:], axis=AX.X).then_inc(sVE, 1)
            cVE.inc()
            se.wait_ge(sSY, x_rec)
            if dump_free is not None:
                se.wait_ge(sVE, dump_free)
            se.activation(out=dump[:], in_=buf[:], func=AF.Square,
                          accum_out=s.stats[:, 1:2]).then_inc(sSE, 1)
            s_sq = cSE.inc()
            ve.wait_ge(sSE, s_sq)
            ve.tensor_scalar_mul(s.stats[:, 2:3], s.stats[:, 0:1], 1.0 / D)
            ve.tensor_mul(s.stats[:, 3:4], s.stats[:, 2:3], s.stats[:, 2:3])
            ve.tensor_scalar_mul(s.stats[:, 4:5], s.stats[:, 1:2], 1.0 / D)
            ve.tensor_sub(s.stats[:, 4:5], s.stats[:, 4:5], s.stats[:, 3:4])
            ve.tensor_scalar_add(s.stats[:, 4:5], s.stats[:, 4:5], EPS)
            ve.reciprocal(s.stats[:, 5:6], s.stats[:, 4:5]).then_inc(sVE, 1)
            v_rv = cVE.inc()
            se.wait_ge(sVE, v_rv)
            se.activation(out=s.stats[:, 6:7], in_=s.stats[:, 5:6],
                          func=AF.Sqrt).then_inc(sSE, 1)
            s_inv = cSE.inc()
            ve.wait_ge(sSE, s_inv)
            ve.tensor_scalar_mul(s.stats[:, 7:8], s.stats[:, 2:3], -1.0)
            ve.tensor_mul(s.stats[:, 7:8], s.stats[:, 7:8],
                          s.stats[:, 6:7]).then_inc(sVE, 1)
            v_nb = cVE.inc()
            se.wait_ge(sVE, v_nb)
            se.activation(out=xnf[:], in_=buf[:], func=AF.Identity,
                          scale=s.stats[:, 6:7], bias=s.stats[:, 7:8]).then_inc(sSE, 1)
            s_xnf = cSE.inc()

            ve.wait_ge(sSE, s_xnf)
            ve.tensor_copy(xnb[:], xnf[:])
            ve.reduce_max(s.stats[:, 8:9], xnf[:], axis=AX.X, apply_absolute_value=True)
            ve.tensor_scalar_max(s.stats[:, 9:10], s.stats[:, 8:9], 1e-5)
            ve.reciprocal(s.stats[:, 10:11], s.stats[:, 9:10])
            ve.tensor_scalar_mul(s.stats[:, 11:12], s.stats[:, 10:11], 127.0)
            ve.tensor_scalar_mul(s.sxr[:, t:t + 1], s.stats[:, 9:10],
                                 1.0 / 127.0).then_inc(sVE, 1)
            v_qs = cVE.inc()
            se.wait_ge(sVE, v_qs)
            se.activation(out=dump[:], in_=xnf[:], func=AF.Copy,
                          scale=s.stats[:, 11:12], bias=MAGIC).then_inc(sSE, 1)
            s_rnd = cSE.inc()
            ve.wait_ge(sSE, s_rnd)
            ve.tensor_scalar_add(xqb[:], dump[:], -MAGIC).then_inc(sVE, 1)
            v_xq = cVE.inc()
            dump_free = v_xq
            x_last[t] = ((sSE, s_rnd), (sVE, v_qs))

            te.wait_ge(sVE, v_xq)
            for src, is_xq in ((xnb, False), (xqb, True)):
                for b0 in range(0, NCH, 4):
                    bank = batch_i % 2
                    batch_i += 1
                    if tp_free[bank] is not None:
                        te.wait_ge(sVE, tp_free[bank])
                    last = None
                    for j in range(4):
                        cix = b0 + j
                        last = te.transpose(tpb[bank][:, j * 128:(j + 1) * 128],
                                            src[:, cix * 128:(cix + 1) * 128], s.c_id[:])
                    last.then_inc(sTE, 1)
                    t_b = cTE.inc()
                    ve.wait_ge(sTE, t_b)
                    if is_xq:
                        par = bank
                        if stg_free[par] is not None:
                            ve.wait_ge(sSY, stg_free[par])
                        ve.tensor_copy(xqstg[par][:], tpb[bank][:, 0:512]).then_inc(sVE, 1)
                        v_e = cVE.inc()
                        tp_free[bank] = v_e
                        sy.wait_ge(sVE, v_e)
                        dst = _pck(s.xqs[b0 * 128:(b0 + 4) * 128,
                                         t * 128:(t + 1) * 128])
                        stg_free[par] = s.dma(sy, dst, _sck(xqstg[par][:], 4))
                    else:
                        last = None
                        for j in range(4):
                            cix = b0 + j
                            last = ve.tensor_copy(
                                s.xnT[:, cix * TOK + t * 128: cix * TOK + (t + 1) * 128],
                                tpb[bank][:, j * 128:(j + 1) * 128])
                        last.then_inc(sVE, 1)
                        tp_free[bank] = cVE.inc()

        if "xnf" in s.taps:
            sy.wait_ge(sVE, cVE.v)
            sy.wait_ge(sSE, cSE.v)
            s.dma(sy, s.taps["xnf"][:, :], xnf[:])
        if "stats" in s.taps:
            sy.wait_ge(sVE, cVE.v)
            sy.wait_ge(sSE, cSE.v)
            s.dma(sy, s.taps["stats"][:, :], s.stats[:])
        if "xlast" in s.taps:
            sy.wait_ge(sVE, cVE.v)
            sy.wait_ge(sSE, cSE.v)
            s.dma(sy, s.taps["xlast"][:, :], xb[(NT - 1) % 2][:])
        if "xnT" in s.taps:
            sy.wait_ge(sVE, cVE.v)
            s.dma(sy, s.taps["xnT"][:, :], s.xnT[:])
        if "sxr" in s.taps:
            sy.wait_ge(sVE, cVE.v)
            s.dma(sy, s.taps["sxr"][:, :], s.sxr[:])
        if "xqs" in s.taps:
            sy.wait_ge(sSY, s.cSY.v)
            s.dma(sy, s.taps["xqs"][:, :], s.xqs[:, :])
        s.p1 = dict(te=cTE.v, ve=cVE.v, se=cSE.v, sy=s.cSY.v)
        s.free("tpb1", "tpb0", "xqstg1", "xqstg0", "xqb", "xnb", "xnf", "dump",
               "xb1", "xb0")

    def p2_qkv(self):
        s = self
        s.phase_barrier()
        te, ve, se, sy, gp = s.te, s.ve, s.se, s.sy, s.gp
        sTE, sVE, sSE, sSY, sGP = s.sTE, s.sVE, s.sSE, s.sSY, s.sGP
        cTE, cVE, cSE = s.cTE, s.cVE, s.cSE

        wbuf = [s.sb(f"wbuf{i}", [128, NCH * 512], BF16) for i in range(2)]
        kstg = [s.sb(f"kstg{i}", [128, TOK], BF16) for i in range(2)]
        vstg = [s.sb(f"vstg{i}", [128, 512], BF16) for i in range(2)]
        pb = [s.ps(f"pb{i}", [128, 512], F32) for i in range(8)]

        te.wait_ge(sVE, s.p1["ve"])

        wbuf_free = [None, None]
        kstg_free = [None, None]
        vstg_free = [None, None]
        bank_free = [None] * 8
        kin_last = vin_last = 0

        def stream_w(cols0, ncols, par):
            if wbuf_free[par] is not None:
                gp.wait_ge(sTE, wbuf_free[par])
            src = _pck(s.wqkv_d[:, cols0:cols0 + ncols])
            return s.dma(gp, _sck(wbuf[par][:, 0:NCH * ncols], NCH), src)

        def qk_pass(colbase, bias_col, scale, dst_dram, stg, stg_free):
            nonlocal kin_last
            last_rec = None
            for grp in range(8):
                par = grp % 2
                w_rec = stream_w(colbase + grp * 256, 256, par)
                te.wait_ge(sGP, w_rec)
                banks = [pb[4 * par + i] for i in range(4)]
                for i in range(4):
                    if bank_free[4 * par + i] is not None:
                        sem, v = bank_free[4 * par + i]
                        te.wait_ge(sem, v)
                last = None
                for c in range(NCH):
                    for m in range(2):
                        w_ap = wbuf[par][:, c * 256 + m * 128: c * 256 + (m + 1) * 128]
                        for sp in range(2):
                            last = te.matmul(
                                banks[2 * m + sp][:], w_ap,
                                s.xnT[:, c * TOK + sp * 512: c * TOK + (sp + 1) * 512],
                                start=(c == 0), stop=(c == NCH - 1))
                last.then_inc(sTE, 1)
                t_grp = cTE.inc()
                wbuf_free[par] = t_grp
                for m in range(2):
                    mk = 2 * grp + m
                    se.wait_ge(sTE, t_grp)
                    if stg_free[m] is not None:
                        se.wait_ge(sSY, stg_free[m])
                    lastse = None
                    for sp in range(2):
                        lastse = se.activation(
                            out=stg[m][:, sp * 512:(sp + 1) * 512],
                            in_=banks[2 * m + sp][:], func=AF.Identity,
                            bias=bias_col(mk), scale=scale)
                    lastse.then_inc(sSE, 1)
                    s_ev = cSE.inc()
                    for sp in range(2):
                        bank_free[4 * par + 2 * m + sp] = (sSE, s_ev)
                    sy.wait_ge(sSE, s_ev)
                    rec = s.dma(sy, dst_dram[mk * 128:(mk + 1) * 128, :], stg[m][:])
                    stg_free[m] = rec
                    last_rec = rec
            return last_rec

        kin_last = qk_pass(D, lambda mk: s.c_bqkv[:, 16 + mk:17 + mk], 1.0,
                           s.kin, kstg, kstg_free)

        for vs in range(4):
            par = vs % 2
            w_rec = stream_w(2 * D + vs * 512, 512, par)
            te.wait_ge(sGP, w_rec)
            for tg in range(2):
                banks = [pb[4 * tg + i] for i in range(4)]
                for i in range(4):
                    if bank_free[4 * tg + i] is not None:
                        sem, v = bank_free[4 * tg + i]
                        te.wait_ge(sem, v)
                last = None
                for c in range(NCH):
                    for ti in range(4):
                        t = 4 * tg + ti
                        last = te.matmul(
                            banks[ti][:],
                            s.xnT[:, c * TOK + t * 128: c * TOK + (t + 1) * 128],
                            wbuf[par][:, c * 512:(c + 1) * 512],
                            start=(c == 0), stop=(c == NCH - 1))
                last.then_inc(sTE, 1)
                t_grp = cTE.inc()
                if tg == 1:
                    wbuf_free[par] = t_grp
                for ti in range(4):
                    t = 4 * tg + ti
                    sp = t % 2
                    ve.wait_ge(sTE, t_grp)
                    if vstg_free[sp] is not None:
                        ve.wait_ge(sSY, vstg_free[sp])
                    ve.tensor_copy(vstg[sp][:], banks[ti][:]).then_inc(sVE, 1)
                    v_ev = cVE.inc()
                    bank_free[4 * tg + ti] = (sVE, v_ev)
                    sy.wait_ge(sVE, v_ev)
                    rec = s.dma(sy, s.vin[t * 128:(t + 1) * 128, vs * 512:(vs + 1) * 512],
                                vstg[sp][:])
                    vstg_free[sp] = rec
                    vin_last = rec
        se.wait_ge(sVE, s.p0_ve)
        qk_pass(0, lambda mq: s.c_bqs[:, mq:mq + 1], QSCALE, s.qts, kstg, kstg_free)

        if DO_COLLECTIVES:
            gp.wait_ge(sGP, s.cGP.v)
            gp.wait_ge(sSY, kin_last)
            gp.wait_ge(sSY, vin_last)
            gp.collective_compute("AllGather", ALU.bypass, replica_groups=GROUPS,
                                  ins=[s.kvin[:, :]],
                                  outs=[s.kvout[:, :]]).then_inc(s.sCC, 1)
            s.cc_k = s.cc_v = s.cCC.inc()

        for nm, buf in (("kin", s.kin), ("vin", s.vin), ("qts", s.qts)):
            if nm in s.taps:
                sy.wait_ge(sSY, s.cSY.v)
                s.dma(sy, s.taps[nm][:, :], buf[:, :])


        s.p2 = dict(te=cTE.v, ve=cVE.v, se=cSE.v, sy=s.cSY.v, gp=s.cGP.v)
        s.free(*[f"pb{i}" for i in reversed(range(8))])
        s.free("vstg1", "vstg0", "kstg1", "kstg0", "wbuf1", "wbuf0", "xnT")

    def p4_ffn(self):
        s = self
        s.phase_barrier()
        te, ve, se, sy, gp = s.te, s.ve, s.se, s.sy, s.gp
        sTE, sVE, sSE, sSY, sGP = s.sTE, s.sVE, s.sSE, s.sSY, s.sGP
        cTE, cVE, cSE = s.cTE, s.cVE, s.cSE

        wsb = [s.sb(f"wsb{i}", [128, 8 * 512], BF16) for i in range(2)]
        xqG = s.sb("xqG", [128, NCH * 512], BF16)
        x1h = s.sb("x1h", [128, 4 * 512], F32)
        tmp2 = [s.sb(f"tmp2{i}", [128, 512], F32) for i in range(2)]
        gstg = [s.sb(f"gstg{i}", [128, 512], F32) for i in range(2)]

        wsb_free = [None, None]
        gstg_free = [None, None]
        x1h_free = None
        half_i = 0
        if DO_COLLECTIVES:
            gp.wait_ge(s.sCC, s.cCC.v)

        def stream_w13(colbase, hh):
            nonlocal half_i
            par = half_i % 2
            half_i += 1
            if wsb_free[par] is not None:
                gp.wait_ge(sTE, wsb_free[par])
            srcw = _pck(s.w13_d[hh * 1024:(hh + 1) * 1024, colbase:colbase + 512])
            rec = s.dma(gp, _sck(wsb[par][:, 0:8 * 512], 8), srcw)
            return par, rec

        for G in range(2):
            iqT = s.sb(f"iqT_{G}", [128, 4 * HID], BF16)
            gated = s.sb(f"gated_{G}", [128, 4 * HID], F32)
            iq_tok = s.sb(f"iq_tok_{G}", [128, 512], BF16)

            sy.wait_ge(s.sTE, s.cTE.v)
            src = _pck(s.xqs[:, G * 512:(G + 1) * 512])
            xq_rec = s.dma(sy, _sck(xqG[:], NCH), src)

            pbF = [s.ps(f"pbF{G}_{i}", [128, 512], F32) for i in range(8)]
            bankF_free = [None] * 8
            te.wait_ge(sSY, xq_rec)
            te.wait_ge(sSE, cSE.v)
            te.wait_ge(sVE, cVE.v)
            for j in range(11):
                for part in range(2):
                    colbase = j * 512 if part == 0 else HID + j * 512
                    halves = [stream_w13(colbase, hh) for hh in range(2)]
                    banks = [pbF[4 * part + i] for i in range(4)]
                    for i in range(4):
                        if bankF_free[4 * part + i] is not None:
                            sem, v = bankF_free[4 * part + i]
                            te.wait_ge(sem, v)
                    last = None
                    for c in range(NCH):
                        hh = c // 8
                        par, rec = halves[hh]
                        if c % 8 == 0:
                            te.wait_ge(sGP, rec)
                        w_ap = wsb[par][:, (c % 8) * 512:(c % 8 + 1) * 512]
                        for ti in range(4):
                            last = te.matmul(
                                banks[ti][:],
                                xqG[:, c * 512 + ti * 128: c * 512 + (ti + 1) * 128],
                                w_ap, start=(c == 0), stop=(c == NCH - 1))
                        if c == 7:
                            last.then_inc(sTE, 1)
                            wsb_free[halves[0][0]] = cTE.inc()
                    last.then_inc(sTE, 1)
                    t_pass = cTE.inc()
                    wsb_free[halves[1][0]] = t_pass
                    if part == 0:
                        se.wait_ge(sTE, t_pass)
                        if x1h_free is not None:
                            se.wait_ge(sVE, x1h_free)
                        lastse = None
                        for ti in range(4):
                            lastse = se.activation(out=x1h[:, ti * 512:(ti + 1) * 512],
                                                   in_=banks[ti][:], func=AF.Relu)
                        lastse.then_inc(sSE, 1)
                        s_x1 = cSE.inc()
                        for ti in range(4):
                            bankF_free[ti] = (sSE, s_x1)
                    else:
                        ve.wait_ge(sTE, t_pass)
                        ve.wait_ge(sSE, s_x1)
                        lastve = None
                        for ti in range(4):
                            tb = tmp2[ti % 2]
                            ve.tensor_mul(tb[:], x1h[:, ti * 512:(ti + 1) * 512],
                                          x1h[:, ti * 512:(ti + 1) * 512])
                            lastve = ve.tensor_mul(
                                gated[:, ti * HID + j * 512: ti * HID + (j + 1) * 512],
                                tb[:], banks[ti][:])
                        lastve.then_inc(sVE, 1)
                        v_gat = cVE.inc()
                        x1h_free = v_gat
                        for ti in range(4):
                            bankF_free[4 + ti] = (sVE, v_gat)
            f1a_ve = cVE.v
            s.free(*[f"pbF{G}_{i}" for i in reversed(range(8))])

            tpb = [s.ps(f"tpbF{G}_{i}", [128, 1024], BF16) for i in range(2)]
            tp_free = [None, None]
            batch_i = 0
            tmp2_safe = f1a_ve
            for ti in range(4):
                t = 4 * G + ti
                gt = gated[:, ti * HID:(ti + 1) * HID]
                se.wait_ge(sVE, tmp2_safe)
                lastse = None
                for sl in range(11):
                    lastse = se.activation(out=tmp2[sl % 2][:],
                                           in_=gt[:, sl * 512:(sl + 1) * 512],
                                           func=AF.Square,
                                           accum_out=s.stats[:, 12 + sl:13 + sl])
                lastse.then_inc(sSE, 1)
                s_ssq = cSE.inc()
                ve.wait_ge(sSE, s_ssq)
                ve.reduce_sum(s.stats[:, 24:25], s.stats[:, 12:23], axis=AX.X)
                ve.tensor_mul(gt[:], gt[:], s.c_nw[:])
                ve.reduce_max(s.stats[:, 25:26], gt[:], axis=AX.X,
                              apply_absolute_value=True)
                ve.tensor_mul(s.stats[:, 26:27], s.sxr[:, t:t + 1], s.sxr[:, t:t + 1])
                ve.tensor_mul(s.stats[:, 26:27], s.stats[:, 26:27], s.sxr[:, t:t + 1])
                ve.tensor_mul(s.stats[:, 27:28], s.stats[:, 26:27], s.stats[:, 26:27])
                ve.tensor_mul(s.stats[:, 27:28], s.stats[:, 27:28], s.stats[:, 24:25])
                ve.tensor_scalar_mul(s.stats[:, 27:28], s.stats[:, 27:28], 1.0 / HID)
                ve.tensor_scalar_add(s.stats[:, 27:28], s.stats[:, 27:28], EPS)
                ve.reciprocal(s.stats[:, 28:29], s.stats[:, 27:28]).then_inc(sVE, 1)
                v_pre = cVE.inc()
                se.wait_ge(sVE, v_pre)
                se.activation(out=s.stats[:, 29:30], in_=s.stats[:, 28:29],
                              func=AF.Sqrt).then_inc(sSE, 1)
                s_rs = cSE.inc()
                ve.wait_ge(sSE, s_rs)
                ve.tensor_mul(s.stats[:, 29:30], s.stats[:, 29:30], s.stats[:, 26:27])
                ve.tensor_mul(s.stats[:, 30:31], s.stats[:, 29:30], s.stats[:, 25:26])
                ve.tensor_scalar_max(s.stats[:, 30:31], s.stats[:, 30:31], 1e-5)
                ve.reciprocal(s.stats[:, 31:32], s.stats[:, 30:31])
                ve.tensor_mul(s.stats[:, 31:32], s.stats[:, 31:32], s.stats[:, 29:30])
                ve.tensor_scalar_mul(s.stats[:, 31:32], s.stats[:, 31:32], 127.0)
                ve.tensor_scalar_mul(s.s2r[:, t:t + 1], s.stats[:, 30:31],
                                     1.0 / 127.0).then_inc(sVE, 1)
                v_cs = cVE.inc()
                prev_sub = None
                iq_free = None
                for sl in range(11):
                    tb = tmp2[sl % 2]
                    se.wait_ge(sVE, v_cs if prev_sub is None else prev_sub)
                    se.activation(out=tb[:], in_=gt[:, sl * 512:(sl + 1) * 512],
                                  func=AF.Copy, scale=s.stats[:, 31:32],
                                  bias=MAGIC).then_inc(sSE, 1)
                    s_r = cSE.inc()
                    ve.wait_ge(sSE, s_r)
                    if iq_free is not None:
                        ve.wait_ge(sTE, iq_free)
                    ve.tensor_scalar_add(iq_tok[:], tb[:], -MAGIC).then_inc(sVE, 1)
                    prev_sub = cVE.inc()
                    bank = batch_i % 2
                    batch_i += 1
                    if tp_free[bank] is not None:
                        te.wait_ge(sVE, tp_free[bank])
                    te.wait_ge(sVE, prev_sub)
                    last = None
                    for jj in range(4):
                        last = te.transpose(tpb[bank][:, jj * 128:(jj + 1) * 128],
                                            iq_tok[:, jj * 128:(jj + 1) * 128],
                                            s.c_id[:])
                    last.then_inc(sTE, 1)
                    t_b = cTE.inc()
                    iq_free = t_b
                    ve.wait_ge(sTE, t_b)
                    lastve = None
                    for jj in range(4):
                        hc = sl * 4 + jj
                        lastve = ve.tensor_copy(
                            iqT[:, ti * HID + hc * 128: ti * HID + (hc + 1) * 128],
                            tpb[bank][:, jj * 128:(jj + 1) * 128])
                    lastve.then_inc(sVE, 1)
                    tp_free[bank] = cVE.inc()
                tmp2_safe = prev_sub
            f1b_ve = cVE.v
            s.free(f"tpbF{G}_1", f"tpbF{G}_0")

            pb2 = [s.ps(f"pb2{G}_{i}", [128, 512], F32) for i in range(8)]
            bank2_free = [None] * 8
            te.wait_ge(sVE, f1b_ve)
            hgs = [(hg * 8, min(8, NHCH - hg * 8)) for hg in range(6)]
            for ds in range(4):
                banks = [pb2[4 * (ds % 2) + i] for i in range(4)]
                for i in range(4):
                    if bank2_free[4 * (ds % 2) + i] is not None:
                        sem, v = bank2_free[4 * (ds % 2) + i]
                        te.wait_ge(sem, v)
                last = None
                for hc0, nhc in hgs:
                    par = half_i % 2
                    half_i += 1
                    if wsb_free[par] is not None:
                        gp.wait_ge(sTE, wsb_free[par])
                    srcw = _pck(s.w2_d[hc0 * 128:(hc0 + nhc) * 128,
                                       ds * 512:(ds + 1) * 512])
                    w_rec = s.dma(gp, _sck(wsb[par][:, 0:nhc * 512], nhc), srcw)
                    te.wait_ge(sGP, w_rec)
                    for hcj in range(nhc):
                        hc = hc0 + hcj
                        for ti in range(4):
                            last = te.matmul(
                                banks[ti][:],
                                iqT[:, ti * HID + hc * 128: ti * HID + (hc + 1) * 128],
                                wsb[par][:, hcj * 512:(hcj + 1) * 512],
                                start=(hc == 0), stop=(hc == NHCH - 1))
                    last.then_inc(sTE, 1)
                    wsb_free[par] = cTE.inc()
                t_ds = cTE.v
                for ti in range(4):
                    t = 4 * G + ti
                    pp = ti % 2
                    se.wait_ge(sTE, t_ds)
                    if gstg_free[pp] is not None:
                        se.wait_ge(sSY, gstg_free[pp])
                    se.activation(out=gstg[pp][:], in_=banks[ti][:], func=AF.Silu,
                                  scale=s.s2r[:, t:t + 1]).then_inc(sSE, 1)
                    s_g = cSE.inc()
                    bank2_free[4 * (ds % 2) + ti] = (sSE, s_g)
                    sy.wait_ge(sSE, s_g)
                    rec = s.dma(sy, s.gsp[t * 128:(t + 1) * 128, ds * 512:(ds + 1) * 512],
                                gstg[pp][:])
                    gstg_free[pp] = rec
            s.free(*[f"pb2{G}_{i}" for i in reversed(range(8))])
            s.free(f"iq_tok_{G}", f"gated_{G}", f"iqT_{G}")

        if "gsp" in s.taps:
            sy.wait_ge(sSY, s.cSY.v)
            s.dma(sy, s.taps["gsp"][:, :], s.gsp[:, :])
        s.p4 = dict(te=cTE.v, ve=cVE.v, se=cSE.v, sy=s.cSY.v, gp=s.cGP.v)
        s.free("gstg1", "gstg0", "tmp21", "tmp20", "x1h", "xqG", "wsb1", "wsb0")

    def p5_attention(self):
        s = self
        s.phase_barrier()
        te, ve, se, sy, gp = s.te, s.ve, s.se, s.sy, s.gp
        sTE, sVE, sSE, sSY, sGP, sCC = s.sTE, s.sVE, s.sSE, s.sSY, s.sGP, s.sCC
        cTE, cVE, cSE = s.cTE, s.cVE, s.cSE

        s.wo_s = s.sb("wo_s", [128, NCH * D], BF16)
        s.c_bo = s.sb("c_bo", [128, D], F32)
        s.attnT = s.sb("attnT", [128, NCH * TOK], BF16)
        s.c_masks = s.sb("c_masks", [128, 2048], F32)
        s.c_alibi = s.sb("c_alibi", [128, 512], F32)
        s.dma(sy, s.c_bo[:], s.bo_d[:, :])
        s.dma(sy, s.c_alibi[:], s.alibi_d[:, :])
        mask_rec = s.dma(sy, s.c_masks[:], s.masks_d[:, :])

        wo_recs = []
        for c in range(NCH):
            wo_recs.append(s.dma(gp, s.wo_s[:, c * D:(c + 1) * D],
                                 s.wo_d[c * 128:(c + 1) * 128, :]))
        s.wo_rec = wo_recs[-1]

        aqrow = [s.sb(f"aqrow{i}", [1, 512], BF16) for i in range(4)]
        kbuf = [s.sb(f"kbuf{i}", [128, 4096], BF16) for i in range(2)]
        vbuf = [s.sb(f"vbuf{i}", [128, 4096], BF16) for i in range(2)]
        qbuf = [s.sb(f"qbuf{i}", [128, TOK], BF16) for i in range(2)]
        phat = s.sb("phat", [128, 4 * 512], BF16)
        Lrow = s.sb("Lrow", [128, 512], F32)
        bcs = s.sb("bcs", [128, 512], F32)
        atmp = s.sb("atmp", [128, 512], F32)
        zb = [s.ps(f"zb{i}", [128, 512], F32) for i in range(2)]
        pv = [s.ps(f"pv{i}", [128, 512], F32) for i in range(2)]
        lb = [s.ps(f"lb{i}", [128, 512], F32) for i in range(2)]
        bcb = s.ps("bcb", [128, 512], F32)
        te.wait_ge(s.sSE, s.cSE.v)
        te.wait_ge(s.sVE, s.cVE.v)

        kv_rec = []
        self._h_done_target = [None] * H
        te_base = cTE.v
        per_h_te = 2 * 36 + 2
        for h in range(H):
            if h >= 2:
                sy.wait_ge(sTE, te_base + (h - 1) * per_h_te)
            if h == 0:
                sy.wait_ge(sCC, s.cc_v)
            for qs01 in range(2):
                s.dma(sy, aqrow[(2 * h + qs01) % 4][:],
                      s.aq_d[2 * h + qs01:2 * h + qs01 + 1, :])
            rec = s.dma(sy, qbuf[h % 2][:], s.qts[h * 128:(h + 1) * 128, :])
            for ck in range(8):
                rank, half = _crh(ck)
                rec = s.dma(sy, kbuf[h % 2][:, ck * 512:(ck + 1) * 512],
                            s.kvout[4096 * rank + h * 128: 4096 * rank + (h + 1) * 128,
                                    512 * half: 512 * (half + 1)])
                voutR = s.kvout[4096 * rank + 2048: 4096 * rank + 4096, :].rearrange(
                    "(a two) c -> a (two c)", two=2)
                vsrc = _pck(voutR[512 * half: 512 * (half + 1),
                                  h * 128:(h + 1) * 128])
                rec = s.dma(sy, _sck(vbuf[h % 2][:, ck * 512:(ck + 1) * 512], 4), vsrc)
            kv_rec.append(rec)

        ve.wait_ge(sSY, mask_rec)
        for eng in (te, se, ve):
            eng.wait_ge(sSY, s.rid_rec)
        idx = s.nc.values_load(s.rid_s[0:1, 0:1],
                               engines=[te.engine, se.engine, ve.engine],
                               min_val=0, max_val=3, skip_runtime_bounds_check=True)
        snap = (cTE.v, cVE.v, cSE.v)
        ends = []
        for r in s.nc.Switch(engines=[te, se, ve._ve], index=[idx, idx, idx], n=4):
            cTE.v, cVE.v, cSE.v = snap
            self._emit_attn_variant(r, kbuf, vbuf, qbuf, phat, Lrow, bcs, atmp,
                                    zb, pv, lb, bcb, kv_rec, aqrow)
            ends.append((cTE.v, cVE.v, cSE.v))
        assert all(e == ends[0] for e in ends), f"variant counts differ: {ends}"
        cTE.v, cVE.v, cSE.v = ends[0]
        assert cTE.v - te_base == H * per_h_te, (cTE.v - te_base, H * per_h_te)

        if "attnT" in s.taps:
            sy.wait_ge(sSE, cSE.v)
            s.dma(sy, s.taps["attnT"][:, :], s.attnT[:])
        for nm, buf in (("kbuf1", kbuf[1]), ("vbuf1", vbuf[1]), ("qbuf1", qbuf[1]),
                        ("lrow", Lrow), ("bcs", bcs), ("atmp", atmp),
                        ("phat", phat)):
            if nm in s.taps:
                sy.wait_ge(sSE, cSE.v)
                sy.wait_ge(sVE, cVE.v)
                s.dma(sy, s.taps[nm][:, :], buf[:])
        s.p5 = dict(te=cTE.v, ve=cVE.v, se=cSE.v, sy=s.cSY.v, gp=s.cGP.v)
        s.free("bcb", "lb1", "lb0", "pv1", "pv0", "zb1", "zb0", "atmp", "bcs",
               "Lrow", "phat", "qbuf1", "qbuf0", "vbuf1", "vbuf0", "kbuf1", "kbuf0")

    def _emit_attn_variant(self, r, kbuf, vbuf, qbuf, phat, Lrow, bcs, atmp,
                           zb, pv, lb, bcb, kv_rec, aqrow):
        s = self
        te, ve, se = s.te, s.ve, s.se
        sTE, sVE, sSE, sSY = s.sTE, s.sVE, s.sSE, s.sSY
        cTE, cVE, cSE = s.cTE, s.cVE, s.cSE

        pvl_free = [None, None]
        bcb_free = None
        atmp_free = None
        prev_exp_last = None
        for h in range(H):
            te.wait_ge(sSY, kv_rec[h])
            for qs01 in range(2):
                A = 4 * r + 4 if qs01 == 0 else 32 - 4 * r
                g2 = (2 * h + qs01) % 2
                exp_rec = {}
                pvl_rec = {}
                phat_rec = {}

                def emit_z(kb):
                    zi = kb % 2
                    if kb >= 2:
                        te.wait_ge(sSE, exp_rec[zi])
                    elif prev_exp_last is not None:
                        te.wait_ge(sSE, prev_exp_last)
                    te.matmul(zb[zi][:], s.c_onesr16[0:1, 0:128],
                              aqrow[(2 * h + qs01) % 4][0:1, :],
                              start=True, stop=False)
                    te.matmul(zb[zi][:], kbuf[h % 2][:, kb * 128:(kb + 1) * 128],
                              qbuf[h % 2][:, qs01 * 512:(qs01 + 1) * 512],
                              start=False, stop=True).then_inc(sTE, 1)
                    z_rec = cTE.inc()
                    if kb >= A - 4:
                        v = kb - (A - 4)
                        ve.wait_ge(sTE, z_rec)
                        ve.tensor_add(zb[zi][:], zb[zi][:],
                                      s.c_masks[:, v * 512:(v + 1) * 512]).then_inc(sVE, 1)
                        z_gate = (sVE, cVE.inc())
                    else:
                        z_gate = (sTE, z_rec)
                    se.wait_ge(z_gate[0], z_gate[1])
                    if kb - 4 in pvl_rec:
                        se.wait_ge(sTE, pvl_rec[kb - 4])
                    elif kb < 4 and pvl_free[1 - g2] is not None:
                        se.wait_ge(sVE, pvl_free[1 - g2])
                    se.activation(out=phat[:, (kb % 4) * 512:(kb % 4 + 1) * 512],
                                  in_=zb[zi][:], func=AF.Exp,
                                  bias=s.c_alibi[:, h * 32 + kb: h * 32 + kb + 1],
                                  scale=1.0).then_inc(sSE, 1)
                    exp_rec[zi] = cSE.inc()
                    phat_rec[kb] = (sSE, exp_rec[zi])

                emit_z(0)
                for kb in range(A):
                    if kb + 1 < A:
                        emit_z(kb + 1)
                    sem, rec = phat_rec[kb]
                    te.wait_ge(sem, rec)
                    if kb == 0 and pvl_free[g2] is not None:
                        te.wait_ge(sVE, pvl_free[g2])
                    te.matmul(pv[g2][:], vbuf[h % 2][:, kb * 128:(kb + 1) * 128],
                              phat[:, (kb % 4) * 512:(kb % 4 + 1) * 512],
                              start=(kb == 0), stop=(kb == A - 1))
                    te.matmul(lb[g2][0:1, :], s.c_ones[:, 0:1],
                              phat[:, (kb % 4) * 512:(kb % 4 + 1) * 512],
                              start=(kb == 0), stop=(kb == A - 1)).then_inc(sTE, 1)
                    pvl_rec[kb] = cTE.inc()
                ve.wait_ge(sTE, pvl_rec[A - 1])
                if atmp_free is not None:
                    ve.wait_ge(sSE, atmp_free)
                ve.tensor_copy(Lrow[0:1, 0:512], lb[g2][0:1, :])
                ve.reciprocal(Lrow[0:1, 0:512], Lrow[0:1, 0:512]).then_inc(sVE, 1)
                l_rec = cVE.inc()
                te.wait_ge(sVE, l_rec)
                te.matmul(bcb[:], s.c_onesr[0:1, :], Lrow[0:1, 0:512],
                          start=True, stop=True).then_inc(sTE, 1)
                bc_rec = cTE.inc()
                ve.wait_ge(sTE, bc_rec)
                ve.tensor_copy(bcs[:], bcb[:]).then_inc(sVE, 1)
                bcb_free = cVE.inc()
                ve.scalar_tensor_tensor(atmp[:], pv[g2][:], 1.0, bcs[:],
                                        ALU.bypass, ALU.mult).then_inc(sVE, 1)
                ev_rec = cVE.inc()
                pvl_free[g2] = ev_rec
                se.wait_ge(sVE, ev_rec)
                se.activation(out=s.attnT[:, h * TOK + qs01 * 512: h * TOK + (qs01 + 1) * 512],
                              in_=atmp[:], func=AF.Identity,
                              bias=s.c_bqkv[:, 32 + h:33 + h], scale=1.0).then_inc(sSE, 1)
                atmp_free = cSE.inc()
                prev_exp_last = exp_rec[(A - 1) % 2]

    def p6_combine(self):
        s = self
        s.phase_barrier()
        te, ve, se, sy = s.te, s.ve, s.se, s.sy
        sTE, sVE, sSE, sSY, sGP = s.sTE, s.sVE, s.sSE, s.sSY, s.sGP
        cTE, cVE, cSE = s.cTE, s.cVE, s.cSE

        xrl = [s.sb(f"xrl{i}", [128, D], F32) for i in range(2)]
        grl = [s.sb(f"grl{i}", [128, D], F32) for i in range(2)]
        ostg = [s.sb(f"ostg{i}", [128, D], F32) for i in range(2)]
        p6 = [s.ps(f"p6_{i}", [128, 512], F32) for i in range(8)]

        te.wait_ge(sSE, s.p5["se"])
        te.wait_ge(sGP, s.wo_rec)
        rl_free = [None, None]
        o_free = [None, None]
        bank6_free = [None] * 8
        for t in range(NT):
            pp = t % 2
            if rl_free[pp] is not None:
                sy.wait_ge(sVE, rl_free[pp])
            s.dma(sy, xrl[pp][:], s.x_d[t * 128:(t + 1) * 128, :])
            x_rec = s.dma(sy, grl[pp][:], s.gsp[t * 128:(t + 1) * 128, :])
            banks = [p6[4 * pp + i] for i in range(4)]
            for i in range(4):
                if bank6_free[4 * pp + i] is not None:
                    te.wait_ge(sVE, bank6_free[4 * pp + i])
            last = None
            for c in range(NCH):
                a_ap = s.attnT[:, c * TOK + t * 128: c * TOK + (t + 1) * 128]
                for ds in range(4):
                    last = te.matmul(banks[ds][:], a_ap,
                                     s.wo_s[:, c * D + ds * 512: c * D + (ds + 1) * 512],
                                     start=(c == 0), stop=(c == NCH - 1))
            last.then_inc(sTE, 1)
            t_rec = cTE.inc()
            ve.wait_ge(sTE, t_rec)
            ve.wait_ge(sSY, x_rec)
            if o_free[pp] is not None:
                ve.wait_ge(sSY, o_free[pp])
            lastve = None
            for ds in range(4):
                sl = slice(ds * 512, (ds + 1) * 512)
                ve.scalar_tensor_tensor(ostg[pp][:, sl], banks[ds][:], 1.0,
                                        s.c_bo[:, sl], ALU.bypass, ALU.add)
                ve.tensor_mul(ostg[pp][:, sl], ostg[pp][:, sl], grl[pp][:, sl])
                lastve = ve.tensor_add(ostg[pp][:, sl], ostg[pp][:, sl], xrl[pp][:, sl])
            lastve.then_inc(sVE, 1)
            v_rec = cVE.inc()
            rl_free[pp] = v_rec
            for i in range(4):
                bank6_free[4 * pp + i] = v_rec
            sy.wait_ge(sVE, v_rec)
            o_free[pp] = s.dma(sy, s.out_d[t * 128:(t + 1) * 128, :], ostg[pp][:])

    def finish(self):
        s = self
        s.sy.wait_ge(s.sSY, s.cSY.v)
        s.sy.wait_ge(s.sGP, s.cGP.v)
        s.sy.wait_ge(s.sTE, s.cTE.v)
        s.sy.wait_ge(s.sVE, s.cVE.v)
        s.sy.wait_ge(s.sSE, s.cSE.v)


def _host_consts():
    slopes = np.array([1.0 / 2 ** (k / 2) for k in range(H)], np.float32)
    alibi = np.zeros((128, 512), np.float32)
    j = np.arange(128, dtype=np.float32)
    for h in range(H):
        for kb in range(32):
            alibi[:, h * 32 + kb] = slopes[h] * (128 * kb + j) - SHIFT
    masks = np.zeros((128, 2048), np.float32)
    q = np.arange(512, dtype=np.float32)
    for v in range(4):
        masks[:, v * 512:(v + 1) * 512] = np.where(
            j[:, None] + 128 * v <= q[None, :], 0.0, -1e30)
    return alibi, masks, np.eye(128, dtype=ml_dtypes.bfloat16)


def make_in_maps(L0_attn_x, L0_attn_Wqkv, L0_attn_bqkv, L0_attn_Wo, L0_attn_bo,
                 L0_ffn_w13, L0_ffn_w2, L0_ffn_norm_w):
    x = np.asarray(L0_attn_x, np.float32)
    wqkv = np.ascontiguousarray(np.asarray(L0_attn_Wqkv, np.float32))
    bqkvT = np.ascontiguousarray(np.asarray(L0_attn_bqkv, np.float32).reshape(48, 128).T)
    wo = np.ascontiguousarray(np.asarray(L0_attn_Wo, np.float32))
    bo_rep = np.ascontiguousarray(
        np.broadcast_to(np.asarray(L0_attn_bo, np.float32), (128, D)))
    w13t = np.ascontiguousarray(np.asarray(L0_ffn_w13, np.float32).T)
    w2t = np.ascontiguousarray(np.asarray(L0_ffn_w2, np.float32).T)
    nw_rep = np.ascontiguousarray(
        np.broadcast_to(np.asarray(L0_ffn_norm_w, np.float32), (128, HID)))
    alibi, masks, ident = _host_consts()
    slopes = np.array([1.0 / 2 ** (k / 2) for k in range(H)], np.float32)
    qv = np.arange(512, dtype=np.float32)
    in_maps = []
    for core in range(8):
        b, r = core // 4, core % 4
        aq = np.zeros((32, 512), np.float32)
        for h in range(H):
            for qs01 in range(2):
                chunk = r if qs01 == 0 else 7 - r
                aq[2 * h + qs01] = -slopes[h] * (chunk * 512 + qv)
        aq16 = aq.astype(ml_dtypes.bfloat16)
        x_own = np.ascontiguousarray(np.concatenate(
            [x[b, 512 * r:512 * (r + 1)], x[b, 512 * (7 - r):512 * (8 - r)]], axis=0))
        in_maps.append({
            "rid": np.array([[r]], np.int32), "x": x_own, "wqkv": wqkv,
            "bqkvT": bqkvT, "wo": wo, "bo_rep": bo_rep, "w13t": w13t,
            "w2t": w2t, "nw_rep": nw_rep, "alibi": alibi, "masks": masks,
            "ident": ident, "aq": aq16,
        })
    return in_maps


def kernel(L0_attn_x, L0_attn_Wqkv, L0_attn_bqkv, L0_attn_Wo, L0_attn_bo,
           L0_ffn_w13, L0_ffn_w2, L0_ffn_norm_w):
    in_maps = make_in_maps(L0_attn_x, L0_attn_Wqkv, L0_attn_bqkv, L0_attn_Wo,
                           L0_attn_bo, L0_ffn_w13, L0_ffn_w2, L0_ffn_norm_w)
    nc = MK().build()
    res = run_bass_kernel_spmd(nc, in_maps, core_ids=list(range(8)))
    out = np.empty((B, S, D), np.float32)
    for core in range(8):
        b, r = core // 4, core % 4
        o = res.results[core]["out"]
        out[b, 512 * r:512 * (r + 1)] = o[:512]
        out[b, 512 * (7 - r):512 * (8 - r)] = o[512:]
    return out


# version 13 (final)
# speedup vs baseline: 1.3094x; 1.0240x over previous
import math

import numpy as np
import ml_dtypes
import concourse.bass as bass
import concourse.mybir as mybir
from concourse.bass_utils import run_bass_kernel_spmd

F32 = mybir.dt.float32
BF16 = mybir.dt.bfloat16
I32 = mybir.dt.int32
AF = mybir.ActivationFunctionType
ALU = mybir.AluOpType
AX = mybir.AxisListType

B, S, D, H, DH = 2, 4096, 2048, 16, 128
QKVC = 3 * D
HID, FF = 5632, 11264
TOK, NT = 1024, 8
NCH = D // 128
NHCH = HID // 128
QSCALE = 1.0 / math.sqrt(DH)
MAGIC = float(np.float32(12582912.0))
SHIFT = 12.0
EPS = 1e-5
GROUPS = [[0, 1, 2, 3], [4, 5, 6, 7]]

TAPS = []
BUILD_UPTO = 99
DO_COLLECTIVES = True


def _crh(ck):
    return (ck, 0) if ck < 4 else (7 - ck, 1)



def _pck(dram_ap, p=128):
    return dram_ap.rearrange("(c p) k -> p c k", p=p)


def _sck(sbuf_ap, c):
    return sbuf_ap.rearrange("p (c k) -> p c k", c=c)



def _ap_span(ap):
    try:
        name = ap.tensor.name
    except Exception:
        name = str(ap.tensor)
    off = ap.offset
    reach = 1
    for step, cnt in ap.ap:
        reach += abs(step) * (cnt - 1)
    return (name, off, off + reach)


class VDrain:

    WINDOW = 5

    def __init__(self, ve):
        self._ve = ve
        self._hist = []

    def _chk(self, ins, outs):
        spans = [_ap_span(a) for a in ins + outs if a is not None]
        hit = False
        for (n, lo, hi) in spans:
            for (n2, lo2, hi2) in self._hist:
                if n == n2 and lo < hi2 and lo2 < hi:
                    hit = True
                    break
            if hit:
                break
        if hit:
            self._ve.drain()
            self._hist = []

    def _rec(self, outs):
        self._hist.extend(_ap_span(a) for a in outs if a is not None)
        self._hist = self._hist[-self.WINDOW:]

    def _emit(self, fn, ins, outs, *args, **kw):
        self._chk(ins, outs)
        r = fn(*args, **kw)
        self._rec(outs)
        return r

    def tensor_copy(self, out, in_):
        return self._emit(self._ve.tensor_copy, [in_], [out], out, in_)

    def tensor_add(self, out, a, b):
        return self._emit(self._ve.tensor_add, [a, b], [out], out, a, b)

    def tensor_sub(self, out, a, b):
        return self._emit(self._ve.tensor_sub, [a, b], [out], out, a, b)

    def tensor_mul(self, out, a, b):
        return self._emit(self._ve.tensor_mul, [a, b], [out], out, a, b)

    def tensor_scalar_mul(self, out, a, sc):
        ins = [a] + ([sc] if not isinstance(sc, float) else [])
        return self._emit(self._ve.tensor_scalar_mul, ins, [out], out, a, sc)

    def tensor_scalar_add(self, out, a, sc):
        ins = [a] + ([sc] if not isinstance(sc, float) else [])
        return self._emit(self._ve.tensor_scalar_add, ins, [out], out, a, sc)

    def tensor_scalar_max(self, out, a, sc):
        ins = [a] + ([sc] if not isinstance(sc, float) else [])
        return self._emit(self._ve.tensor_scalar_max, ins, [out], out, a, sc)

    def reciprocal(self, out, in_):
        return self._emit(self._ve.reciprocal, [in_], [out], out, in_)

    def reduce_sum(self, out, in_, axis, **kw):
        return self._emit(self._ve.reduce_sum, [in_], [out], out, in_, axis=axis, **kw)

    def reduce_max(self, out, in_, axis, **kw):
        return self._emit(self._ve.reduce_max, [in_], [out], out, in_, axis=axis, **kw)

    def scalar_tensor_tensor(self, out, in0, sc, in1, op0, op1):
        ins = [in0, in1] + ([sc] if not isinstance(sc, float) else [])
        return self._emit(self._ve.scalar_tensor_tensor, ins, [out],
                          out, in0, sc, in1, op0, op1)

    def memset(self, ap, v):
        return self._emit(self._ve.memset, [], [ap], ap, v)

    def wait_ge(self, sem, v):
        return self._ve.wait_ge(sem, v)

    def drain(self):
        self._hist = []
        return self._ve.drain()

    @property
    def engine(self):
        return self._ve.engine


class Ctr:
    __slots__ = ("v",)

    def __init__(self):
        self.v = 0

    def inc(self, n=1):
        self.v += n
        return self.v


class MK:
    def __init__(self):
        nc = bass.Bass("TRN2", target_bir_lowering=False)
        self.nc = nc
        self.stack = []
        self.taps = {}
        self.rid = nc.dram_tensor("rid", [1, 1], I32, kind="ExternalInput").ap()
        self.x_d = nc.dram_tensor("x", [TOK, D], F32, kind="ExternalInput").ap()
        self.wqkv_d = nc.dram_tensor("wqkv", [D, QKVC], F32, kind="ExternalInput").ap()
        self.bqkv_d = nc.dram_tensor("bqkvT", [128, 48], F32, kind="ExternalInput").ap()
        self.wo_d = nc.dram_tensor("wo", [D, D], F32, kind="ExternalInput").ap()
        self.bo_d = nc.dram_tensor("bo_rep", [128, D], F32, kind="ExternalInput").ap()
        self.w13_d = nc.dram_tensor("w13t", [D, FF], F32, kind="ExternalInput").ap()
        self.w2_d = nc.dram_tensor("w2t", [HID, D], F32, kind="ExternalInput").ap()
        self.nw_d = nc.dram_tensor("nw_rep", [128, HID], F32, kind="ExternalInput").ap()
        self.alibi_d = nc.dram_tensor("alibi", [128, 512], F32, kind="ExternalInput").ap()
        self.masks_d = nc.dram_tensor("masks", [128, 2048], F32, kind="ExternalInput").ap()
        self.id_d = nc.dram_tensor("ident", [128, 128], BF16, kind="ExternalInput").ap()
        self.aq_d = nc.dram_tensor("aq", [32, 512], BF16, kind="ExternalInput").ap()
        self.out_d = nc.dram_tensor("out", [TOK, D], F32, kind="ExternalOutput").ap()

        self.kvin = nc.dram_tensor("kvin", [2 * D, TOK], BF16, kind="Internal").ap()
        self.kvout = nc.dram_tensor("kvout", [8 * D, TOK], BF16, kind="Internal").ap()
        self.kin = self.kvin[0:D, :]
        self.vin = self.kvin[D:2 * D, :].rearrange("(a two) c -> a (two c)", two=2)
        self.qts = nc.dram_tensor("qts", [D, TOK], BF16, kind="Internal").ap()
        self.xqs = nc.dram_tensor("xqs", [D, TOK], BF16, kind="Internal").ap()
        self.gsp = nc.dram_tensor("gsp", [TOK, D], F32, kind="Internal").ap()
        for name, shape, dt in TAPS:
            self.taps[name.replace("tap_", "")] = nc.dram_tensor(
                name, shape, dt, kind="ExternalOutput").ap()

        self.te, self.ve, self.se = nc.tensor, VDrain(nc.vector), nc.scalar
        self.sy, self.gp = nc.sync, nc.gpsimd

        self.sSY = nc.semaphore("sSY").__enter__()
        self.sGP = nc.semaphore("sGP").__enter__()
        self.sCC = nc.semaphore("sCC").__enter__()
        self.sTE = nc.semaphore("sTE").__enter__()
        self.sVE = nc.semaphore("sVE").__enter__()
        self.sSE = nc.semaphore("sSE").__enter__()
        self.cSY, self.cGP, self.cCC = Ctr(), Ctr(), Ctr()
        self.cTE, self.cVE, self.cSE = Ctr(), Ctr(), Ctr()

    def sb(self, name, shape, dtype):
        g = self.nc.sbuf_tensor(name, shape, dtype)
        t = g.__enter__()
        self.stack.append((g, name))
        return t

    def ps(self, name, shape, dtype):
        g = self.nc.psum_tensor(name, shape, dtype)
        t = g.__enter__()
        self.stack.append((g, name))
        return t

    def free(self, *names):
        for name in names:
            g, n = self.stack.pop()
            assert n == name, f"free order: got {n}, want {name}"
            g.__exit__(None, None, None)

    def phase_barrier(self):
        s = self
        targets = [(s.sSY, s.cSY.v), (s.sGP, s.cGP.v), (s.sTE, s.cTE.v),
                   (s.sVE, s.cVE.v), (s.sSE, s.cSE.v)]
        for eng in (s.te, s.ve, s.se, s.sy, s.gp):
            for sem, v in targets:
                if v:
                    eng.wait_ge(sem, v)

    def dma(self, eng, out, in_):
        if eng is self.sy:
            eng.dma_start(out=out, in_=in_).then_inc(self.sSY, 16)
            return self.cSY.inc(16)
        eng.dma_start(out=out, in_=in_).then_inc(self.sGP, 16)
        return self.cGP.inc(16)

    def build(self):
        self.p0_consts()
        if BUILD_UPTO >= 1:
            self.p1_layernorm()
        if BUILD_UPTO >= 2:
            self.p2_qkv()
        if BUILD_UPTO >= 4:
            self.p4_ffn()
        if BUILD_UPTO >= 5:
            self.p5_attention()
        if BUILD_UPTO >= 6:
            self.p6_combine()
        self.finish()
        return self.nc

    def p0_consts(self):
        s = self
        s.c_nw = s.sb("c_nw", [128, HID], F32)
        s.c_id = s.sb("c_id", [128, 128], BF16)
        s.c_bqkv = s.sb("c_bqkv", [128, 48], F32)
        s.c_bqs = s.sb("c_bqs", [128, 16], F32)
        s.c_ones = s.sb("c_ones", [128, 1], BF16)
        s.rid_s = s.sb("rid_s", [1, 1], I32)
        s.sxr = s.sb("sxr", [128, NT], F32)
        s.s2r = s.sb("s2r", [128, NT], F32)
        s.stats = s.sb("stats", [128, 32], F32)

        s.dma(s.sy, s.c_nw[:], s.nw_d[:, :])
        s.dma(s.sy, s.c_id[:], s.id_d[:, :])
        s.dma(s.sy, s.c_bqkv[:], s.bqkv_d[:, :])
        s.rid_rec = s.dma(s.sy, s.rid_s[:], s.rid[:, :])
        s.p0_loads = s.cSY.v

        s.ve.wait_ge(s.sSY, s.p0_loads)
        s.ve.memset(s.c_ones[:], 1.0)
        s.ve.tensor_scalar_mul(s.c_bqs[:], s.c_bqkv[:, 0:16], QSCALE).then_inc(s.sVE, 1)
        s.p0_ve = s.cVE.inc()

    def p1_layernorm(self):
        s = self
        te, ve, se, sy = s.te, s.ve, s.se, s.sy
        sTE, sVE, sSE, sSY = s.sTE, s.sVE, s.sSE, s.sSY
        cTE, cVE, cSE = s.cTE, s.cVE, s.cSE

        s.xnT = s.sb("xnT", [128, NCH * TOK], BF16)
        xb = [s.sb(f"xb{i}", [128, D], F32) for i in range(2)]
        dump = s.sb("dump", [128, D], F32)
        xnf = s.sb("xnf", [128, D], F32)
        xnb = s.sb("xnb", [128, D], BF16)
        xqb = s.sb("xqb", [128, D], BF16)
        xqstg = [s.sb(f"xqstg{i}", [128, 512], BF16) for i in range(2)]
        tpb = [s.ps(f"tpb{i}", [128, 1024], BF16) for i in range(2)]

        x_last = {}
        stg_free = [None, None]
        dump_free = None
        tp_free = [None, None]
        batch_i = 0

        for t in range(NT):
            buf = xb[t % 2]
            if t - 2 in x_last:
                for sem, v in x_last[t - 2]:
                    sy.wait_ge(sem, v)
            x_rec = s.dma(sy, buf[:], s.x_d[t * 128:(t + 1) * 128, :])

            ve.wait_ge(sSY, x_rec)
            ve.reduce_sum(s.stats[:, 0:1], buf[:], axis=AX.X).then_inc(sVE, 1)
            cVE.inc()
            se.wait_ge(sSY, x_rec)
            if dump_free is not None:
                se.wait_ge(sVE, dump_free)
            se.activation(out=dump[:], in_=buf[:], func=AF.Square,
                          accum_out=s.stats[:, 1:2]).then_inc(sSE, 1)
            s_sq = cSE.inc()
            ve.wait_ge(sSE, s_sq)
            ve.tensor_scalar_mul(s.stats[:, 2:3], s.stats[:, 0:1], 1.0 / D)
            ve.tensor_mul(s.stats[:, 3:4], s.stats[:, 2:3], s.stats[:, 2:3])
            ve.tensor_scalar_mul(s.stats[:, 4:5], s.stats[:, 1:2], 1.0 / D)
            ve.tensor_sub(s.stats[:, 4:5], s.stats[:, 4:5], s.stats[:, 3:4])
            ve.tensor_scalar_add(s.stats[:, 4:5], s.stats[:, 4:5], EPS)
            ve.reciprocal(s.stats[:, 5:6], s.stats[:, 4:5]).then_inc(sVE, 1)
            v_rv = cVE.inc()
            se.wait_ge(sVE, v_rv)
            se.activation(out=s.stats[:, 6:7], in_=s.stats[:, 5:6],
                          func=AF.Sqrt).then_inc(sSE, 1)
            s_inv = cSE.inc()
            ve.wait_ge(sSE, s_inv)
            ve.tensor_scalar_mul(s.stats[:, 7:8], s.stats[:, 2:3], -1.0)
            ve.tensor_mul(s.stats[:, 7:8], s.stats[:, 7:8],
                          s.stats[:, 6:7]).then_inc(sVE, 1)
            v_nb = cVE.inc()
            se.wait_ge(sVE, v_nb)
            se.activation(out=xnf[:], in_=buf[:], func=AF.Identity,
                          scale=s.stats[:, 6:7], bias=s.stats[:, 7:8]).then_inc(sSE, 1)
            s_xnf = cSE.inc()

            ve.wait_ge(sSE, s_xnf)
            ve.tensor_copy(xnb[:], xnf[:])
            ve.reduce_max(s.stats[:, 8:9], xnf[:], axis=AX.X, apply_absolute_value=True)
            ve.tensor_scalar_max(s.stats[:, 9:10], s.stats[:, 8:9], 1e-5)
            ve.reciprocal(s.stats[:, 10:11], s.stats[:, 9:10])
            ve.tensor_scalar_mul(s.stats[:, 11:12], s.stats[:, 10:11], 127.0)
            ve.tensor_scalar_mul(s.sxr[:, t:t + 1], s.stats[:, 9:10],
                                 1.0 / 127.0).then_inc(sVE, 1)
            v_qs = cVE.inc()
            se.wait_ge(sVE, v_qs)
            se.activation(out=dump[:], in_=xnf[:], func=AF.Copy,
                          scale=s.stats[:, 11:12], bias=MAGIC).then_inc(sSE, 1)
            s_rnd = cSE.inc()
            ve.wait_ge(sSE, s_rnd)
            ve.tensor_scalar_add(xqb[:], dump[:], -MAGIC).then_inc(sVE, 1)
            v_xq = cVE.inc()
            dump_free = v_xq
            x_last[t] = ((sSE, s_rnd), (sVE, v_qs))

            te.wait_ge(sVE, v_xq)
            for src, is_xq in ((xnb, False), (xqb, True)):
                for b0 in range(0, NCH, 4):
                    bank = batch_i % 2
                    batch_i += 1
                    if tp_free[bank] is not None:
                        te.wait_ge(sVE, tp_free[bank])
                    last = None
                    for j in range(4):
                        cix = b0 + j
                        last = te.transpose(tpb[bank][:, j * 128:(j + 1) * 128],
                                            src[:, cix * 128:(cix + 1) * 128], s.c_id[:])
                    last.then_inc(sTE, 1)
                    t_b = cTE.inc()
                    ve.wait_ge(sTE, t_b)
                    if is_xq:
                        par = bank
                        if stg_free[par] is not None:
                            ve.wait_ge(sSY, stg_free[par])
                        ve.tensor_copy(xqstg[par][:], tpb[bank][:, 0:512]).then_inc(sVE, 1)
                        v_e = cVE.inc()
                        tp_free[bank] = v_e
                        sy.wait_ge(sVE, v_e)
                        dst = _pck(s.xqs[b0 * 128:(b0 + 4) * 128,
                                         t * 128:(t + 1) * 128])
                        stg_free[par] = s.dma(sy, dst, _sck(xqstg[par][:], 4))
                    else:
                        last = None
                        for j in range(4):
                            cix = b0 + j
                            last = ve.tensor_copy(
                                s.xnT[:, cix * TOK + t * 128: cix * TOK + (t + 1) * 128],
                                tpb[bank][:, j * 128:(j + 1) * 128])
                        last.then_inc(sVE, 1)
                        tp_free[bank] = cVE.inc()

        if "xnf" in s.taps:
            sy.wait_ge(sVE, cVE.v)
            sy.wait_ge(sSE, cSE.v)
            s.dma(sy, s.taps["xnf"][:, :], xnf[:])
        if "stats" in s.taps:
            sy.wait_ge(sVE, cVE.v)
            sy.wait_ge(sSE, cSE.v)
            s.dma(sy, s.taps["stats"][:, :], s.stats[:])
        if "xlast" in s.taps:
            sy.wait_ge(sVE, cVE.v)
            sy.wait_ge(sSE, cSE.v)
            s.dma(sy, s.taps["xlast"][:, :], xb[(NT - 1) % 2][:])
        if "xnT" in s.taps:
            sy.wait_ge(sVE, cVE.v)
            s.dma(sy, s.taps["xnT"][:, :], s.xnT[:])
        if "sxr" in s.taps:
            sy.wait_ge(sVE, cVE.v)
            s.dma(sy, s.taps["sxr"][:, :], s.sxr[:])
        if "xqs" in s.taps:
            sy.wait_ge(sSY, s.cSY.v)
            s.dma(sy, s.taps["xqs"][:, :], s.xqs[:, :])
        s.p1 = dict(te=cTE.v, ve=cVE.v, se=cSE.v, sy=s.cSY.v)
        s.free("tpb1", "tpb0", "xqstg1", "xqstg0", "xqb", "xnb", "xnf", "dump",
               "xb1", "xb0")

    def p2_qkv(self):
        s = self
        s.phase_barrier()
        te, ve, se, sy, gp = s.te, s.ve, s.se, s.sy, s.gp
        sTE, sVE, sSE, sSY, sGP = s.sTE, s.sVE, s.sSE, s.sSY, s.sGP
        cTE, cVE, cSE = s.cTE, s.cVE, s.cSE

        wbuf = [s.sb(f"wbuf{i}", [128, NCH * 512], BF16) for i in range(2)]
        kstg = [s.sb(f"kstg{i}", [128, TOK], BF16) for i in range(2)]
        vstg = [s.sb(f"vstg{i}", [128, 512], BF16) for i in range(2)]
        pb = [s.ps(f"pb{i}", [128, 512], F32) for i in range(8)]

        te.wait_ge(sVE, s.p1["ve"])

        wbuf_free = [None, None]
        kstg_free = [None, None]
        vstg_free = [None, None]
        bank_free = [None] * 8
        kin_last = vin_last = 0

        def stream_w(cols0, ncols, par):
            if wbuf_free[par] is not None:
                gp.wait_ge(sTE, wbuf_free[par])
            src = _pck(s.wqkv_d[:, cols0:cols0 + ncols])
            return s.dma(gp, _sck(wbuf[par][:, 0:NCH * ncols], NCH), src)

        def qk_pass(colbase, bias_col, scale, dst_dram, stg, stg_free):
            nonlocal kin_last
            last_rec = None
            for grp in range(8):
                par = grp % 2
                w_rec = stream_w(colbase + grp * 256, 256, par)
                te.wait_ge(sGP, w_rec)
                banks = [pb[4 * par + i] for i in range(4)]
                for i in range(4):
                    if bank_free[4 * par + i] is not None:
                        sem, v = bank_free[4 * par + i]
                        te.wait_ge(sem, v)
                last = None
                for c in range(NCH):
                    for m in range(2):
                        w_ap = wbuf[par][:, c * 256 + m * 128: c * 256 + (m + 1) * 128]
                        for sp in range(2):
                            last = te.matmul(
                                banks[2 * m + sp][:], w_ap,
                                s.xnT[:, c * TOK + sp * 512: c * TOK + (sp + 1) * 512],
                                start=(c == 0), stop=(c == NCH - 1))
                last.then_inc(sTE, 1)
                t_grp = cTE.inc()
                wbuf_free[par] = t_grp
                for m in range(2):
                    mk = 2 * grp + m
                    se.wait_ge(sTE, t_grp)
                    if stg_free[m] is not None:
                        se.wait_ge(sSY, stg_free[m])
                    lastse = None
                    for sp in range(2):
                        lastse = se.activation(
                            out=stg[m][:, sp * 512:(sp + 1) * 512],
                            in_=banks[2 * m + sp][:], func=AF.Identity,
                            bias=bias_col(mk), scale=scale)
                    lastse.then_inc(sSE, 1)
                    s_ev = cSE.inc()
                    for sp in range(2):
                        bank_free[4 * par + 2 * m + sp] = (sSE, s_ev)
                    sy.wait_ge(sSE, s_ev)
                    rec = s.dma(sy, dst_dram[mk * 128:(mk + 1) * 128, :], stg[m][:])
                    stg_free[m] = rec
                    last_rec = rec
            return last_rec

        kin_last = qk_pass(D, lambda mk: s.c_bqkv[:, 16 + mk:17 + mk], 1.0,
                           s.kin, kstg, kstg_free)

        for vs in range(4):
            par = vs % 2
            w_rec = stream_w(2 * D + vs * 512, 512, par)
            te.wait_ge(sGP, w_rec)
            for tg in range(2):
                banks = [pb[4 * tg + i] for i in range(4)]
                for i in range(4):
                    if bank_free[4 * tg + i] is not None:
                        sem, v = bank_free[4 * tg + i]
                        te.wait_ge(sem, v)
                last = None
                for c in range(NCH):
                    for ti in range(4):
                        t = 4 * tg + ti
                        last = te.matmul(
                            banks[ti][:],
                            s.xnT[:, c * TOK + t * 128: c * TOK + (t + 1) * 128],
                            wbuf[par][:, c * 512:(c + 1) * 512],
                            start=(c == 0), stop=(c == NCH - 1))
                last.then_inc(sTE, 1)
                t_grp = cTE.inc()
                if tg == 1:
                    wbuf_free[par] = t_grp
                for ti in range(4):
                    t = 4 * tg + ti
                    sp = t % 2
                    ve.wait_ge(sTE, t_grp)
                    if vstg_free[sp] is not None:
                        ve.wait_ge(sSY, vstg_free[sp])
                    ve.tensor_copy(vstg[sp][:], banks[ti][:]).then_inc(sVE, 1)
                    v_ev = cVE.inc()
                    bank_free[4 * tg + ti] = (sVE, v_ev)
                    sy.wait_ge(sVE, v_ev)
                    rec = s.dma(sy, s.vin[t * 128:(t + 1) * 128, vs * 512:(vs + 1) * 512],
                                vstg[sp][:])
                    vstg_free[sp] = rec
                    vin_last = rec
        se.wait_ge(sVE, s.p0_ve)
        qk_pass(0, lambda mq: s.c_bqs[:, mq:mq + 1], QSCALE, s.qts, kstg, kstg_free)

        if DO_COLLECTIVES:
            gp.wait_ge(sGP, s.cGP.v)
            gp.wait_ge(sSY, kin_last)
            gp.wait_ge(sSY, vin_last)
            gp.collective_compute("AllGather", ALU.bypass, replica_groups=GROUPS,
                                  ins=[s.kvin[:, :]],
                                  outs=[s.kvout[:, :]]).then_inc(s.sCC, 1)
            s.cc_k = s.cc_v = s.cCC.inc()

        for nm, buf in (("kin", s.kin), ("vin", s.vin), ("qts", s.qts)):
            if nm in s.taps:
                sy.wait_ge(sSY, s.cSY.v)
                s.dma(sy, s.taps[nm][:, :], buf[:, :])


        s.p2 = dict(te=cTE.v, ve=cVE.v, se=cSE.v, sy=s.cSY.v, gp=s.cGP.v)
        s.free(*[f"pb{i}" for i in reversed(range(8))])
        s.free("vstg1", "vstg0", "kstg1", "kstg0", "wbuf1", "wbuf0", "xnT")

    def p4_ffn(self):
        s = self
        s.phase_barrier()
        te, ve, se, sy, gp = s.te, s.ve, s.se, s.sy, s.gp
        sTE, sVE, sSE, sSY, sGP = s.sTE, s.sVE, s.sSE, s.sSY, s.sGP
        cTE, cVE, cSE = s.cTE, s.cVE, s.cSE

        wsb = [s.sb(f"wsb{i}", [128, 8 * 512], BF16) for i in range(3)]
        xqG = s.sb("xqG", [128, NCH * 512], BF16)
        x1h = s.sb("x1h", [128, 4 * 512], F32)
        tmp2 = [s.sb(f"tmp2{i}", [128, 512], F32) for i in range(2)]
        gstg = tmp2

        wsb_free = [None, None, None]
        gstg_free = [None, None]
        x1h_free = None
        half_i = 0
        if DO_COLLECTIVES:
            gp.wait_ge(s.sCC, s.cCC.v)

        def stream_w13(colbase, hh):
            nonlocal half_i
            par = half_i % 3
            half_i += 1
            if wsb_free[par] is not None:
                gp.wait_ge(sTE, wsb_free[par])
            srcw = _pck(s.w13_d[hh * 1024:(hh + 1) * 1024, colbase:colbase + 512])
            rec = s.dma(gp, _sck(wsb[par][:, 0:8 * 512], 8), srcw)
            return par, rec

        for G in range(2):
            iqT = s.sb(f"iqT_{G}", [128, 4 * HID], BF16)
            gated = s.sb(f"gated_{G}", [128, 4 * HID], F32)
            iq_tok = s.sb(f"iq_tok_{G}", [128, 512], BF16)

            sy.wait_ge(s.sTE, s.cTE.v)
            src = _pck(s.xqs[:, G * 512:(G + 1) * 512])
            xq_rec = s.dma(sy, _sck(xqG[:], NCH), src)

            pbF = [s.ps(f"pbF{G}_{i}", [128, 512], F32) for i in range(8)]
            bankF_free = [None] * 8
            te.wait_ge(sSY, xq_rec)
            te.wait_ge(sSE, cSE.v)
            te.wait_ge(sVE, cVE.v)
            se.wait_ge(sSY, s.cSY.v)
            ve.wait_ge(sSY, s.cSY.v)
            for j in range(11):
                for part in range(2):
                    colbase = j * 512 if part == 0 else HID + j * 512
                    halves = [stream_w13(colbase, hh) for hh in range(2)]
                    banks = [pbF[4 * part + i] for i in range(4)]
                    for i in range(4):
                        if bankF_free[4 * part + i] is not None:
                            sem, v = bankF_free[4 * part + i]
                            te.wait_ge(sem, v)
                    last = None
                    for c in range(NCH):
                        hh = c // 8
                        par, rec = halves[hh]
                        if c % 8 == 0:
                            te.wait_ge(sGP, rec)
                        w_ap = wsb[par][:, (c % 8) * 512:(c % 8 + 1) * 512]
                        for ti in range(4):
                            last = te.matmul(
                                banks[ti][:],
                                xqG[:, c * 512 + ti * 128: c * 512 + (ti + 1) * 128],
                                w_ap, start=(c == 0), stop=(c == NCH - 1))
                        if c == 7:
                            last.then_inc(sTE, 1)
                            wsb_free[halves[0][0]] = cTE.inc()
                    last.then_inc(sTE, 1)
                    t_pass = cTE.inc()
                    wsb_free[halves[1][0]] = t_pass
                    if part == 0:
                        se.wait_ge(sTE, t_pass)
                        if x1h_free is not None:
                            se.wait_ge(sVE, x1h_free)
                        lastse = None
                        for ti in range(4):
                            lastse = se.activation(out=x1h[:, ti * 512:(ti + 1) * 512],
                                                   in_=banks[ti][:], func=AF.Relu)
                        lastse.then_inc(sSE, 1)
                        s_x1 = cSE.inc()
                        for ti in range(4):
                            bankF_free[ti] = (sSE, s_x1)
                    else:
                        ve.wait_ge(sTE, t_pass)
                        ve.wait_ge(sSE, s_x1)
                        lastve = None
                        for ti in range(4):
                            tb = tmp2[ti % 2]
                            ve.tensor_mul(tb[:], x1h[:, ti * 512:(ti + 1) * 512],
                                          x1h[:, ti * 512:(ti + 1) * 512])
                            lastve = ve.tensor_mul(
                                gated[:, ti * HID + j * 512: ti * HID + (j + 1) * 512],
                                tb[:], banks[ti][:])
                        lastve.then_inc(sVE, 1)
                        v_gat = cVE.inc()
                        x1h_free = v_gat
                        for ti in range(4):
                            bankF_free[4 + ti] = (sVE, v_gat)
            f1a_ve = cVE.v
            s.free(*[f"pbF{G}_{i}" for i in reversed(range(8))])

            tpb = [s.ps(f"tpbF{G}_{i}", [128, 1024], BF16) for i in range(2)]
            tp_free = [None, None]
            batch_i = 0
            tmp2_safe = f1a_ve
            for ti in range(4):
                t = 4 * G + ti
                gt = gated[:, ti * HID:(ti + 1) * HID]
                se.wait_ge(sVE, tmp2_safe)
                lastse = None
                for sl in range(11):
                    lastse = se.activation(out=tmp2[sl % 2][:],
                                           in_=gt[:, sl * 512:(sl + 1) * 512],
                                           func=AF.Square,
                                           accum_out=s.stats[:, 12 + sl:13 + sl])
                lastse.then_inc(sSE, 1)
                s_ssq = cSE.inc()
                ve.wait_ge(sSE, s_ssq)
                ve.reduce_sum(s.stats[:, 24:25], s.stats[:, 12:23], axis=AX.X)
                ve.tensor_mul(gt[:], gt[:], s.c_nw[:])
                ve.reduce_max(s.stats[:, 25:26], gt[:], axis=AX.X,
                              apply_absolute_value=True)
                ve.tensor_mul(s.stats[:, 26:27], s.sxr[:, t:t + 1], s.sxr[:, t:t + 1])
                ve.tensor_mul(s.stats[:, 26:27], s.stats[:, 26:27], s.sxr[:, t:t + 1])
                ve.tensor_mul(s.stats[:, 27:28], s.stats[:, 26:27], s.stats[:, 26:27])
                ve.tensor_mul(s.stats[:, 27:28], s.stats[:, 27:28], s.stats[:, 24:25])
                ve.tensor_scalar_mul(s.stats[:, 27:28], s.stats[:, 27:28], 1.0 / HID)
                ve.tensor_scalar_add(s.stats[:, 27:28], s.stats[:, 27:28], EPS)
                ve.reciprocal(s.stats[:, 28:29], s.stats[:, 27:28]).then_inc(sVE, 1)
                v_pre = cVE.inc()
                se.wait_ge(sVE, v_pre)
                se.activation(out=s.stats[:, 29:30], in_=s.stats[:, 28:29],
                              func=AF.Sqrt).then_inc(sSE, 1)
                s_rs = cSE.inc()
                ve.wait_ge(sSE, s_rs)
                ve.tensor_mul(s.stats[:, 29:30], s.stats[:, 29:30], s.stats[:, 26:27])
                ve.tensor_mul(s.stats[:, 30:31], s.stats[:, 29:30], s.stats[:, 25:26])
                ve.tensor_scalar_max(s.stats[:, 30:31], s.stats[:, 30:31], 1e-5)
                ve.reciprocal(s.stats[:, 31:32], s.stats[:, 30:31])
                ve.tensor_mul(s.stats[:, 31:32], s.stats[:, 31:32], s.stats[:, 29:30])
                ve.tensor_scalar_mul(s.stats[:, 31:32], s.stats[:, 31:32], 127.0)
                ve.tensor_scalar_mul(s.s2r[:, t:t + 1], s.stats[:, 30:31],
                                     1.0 / 127.0).then_inc(sVE, 1)
                v_cs = cVE.inc()
                prev_sub = None
                iq_free = None
                for sl in range(11):
                    tb = tmp2[sl % 2]
                    se.wait_ge(sVE, v_cs if prev_sub is None else prev_sub)
                    se.activation(out=tb[:], in_=gt[:, sl * 512:(sl + 1) * 512],
                                  func=AF.Copy, scale=s.stats[:, 31:32],
                                  bias=MAGIC).then_inc(sSE, 1)
                    s_r = cSE.inc()
                    ve.wait_ge(sSE, s_r)
                    if iq_free is not None:
                        ve.wait_ge(sTE, iq_free)
                    ve.tensor_scalar_add(iq_tok[:], tb[:], -MAGIC).then_inc(sVE, 1)
                    prev_sub = cVE.inc()
                    bank = batch_i % 2
                    batch_i += 1
                    if tp_free[bank] is not None:
                        te.wait_ge(sVE, tp_free[bank])
                    te.wait_ge(sVE, prev_sub)
                    last = None
                    for jj in range(4):
                        last = te.transpose(tpb[bank][:, jj * 128:(jj + 1) * 128],
                                            iq_tok[:, jj * 128:(jj + 1) * 128],
                                            s.c_id[:])
                    last.then_inc(sTE, 1)
                    t_b = cTE.inc()
                    iq_free = t_b
                    ve.wait_ge(sTE, t_b)
                    lastve = None
                    for jj in range(4):
                        hc = sl * 4 + jj
                        lastve = ve.tensor_copy(
                            iqT[:, ti * HID + hc * 128: ti * HID + (hc + 1) * 128],
                            tpb[bank][:, jj * 128:(jj + 1) * 128])
                    lastve.then_inc(sVE, 1)
                    tp_free[bank] = cVE.inc()
                tmp2_safe = prev_sub
            f1b_ve = cVE.v
            s.free(f"tpbF{G}_1", f"tpbF{G}_0")

            pb2 = [s.ps(f"pb2{G}_{i}", [128, 512], F32) for i in range(8)]
            bank2_free = [None] * 8
            te.wait_ge(sVE, f1b_ve)
            hgs = [(hg * 8, min(8, NHCH - hg * 8)) for hg in range(6)]
            for ds in range(4):
                banks = [pb2[4 * (ds % 2) + i] for i in range(4)]
                for i in range(4):
                    if bank2_free[4 * (ds % 2) + i] is not None:
                        sem, v = bank2_free[4 * (ds % 2) + i]
                        te.wait_ge(sem, v)
                last = None
                for hc0, nhc in hgs:
                    par = half_i % 3
                    half_i += 1
                    if wsb_free[par] is not None:
                        gp.wait_ge(sTE, wsb_free[par])
                    srcw = _pck(s.w2_d[hc0 * 128:(hc0 + nhc) * 128,
                                       ds * 512:(ds + 1) * 512])
                    w_rec = s.dma(gp, _sck(wsb[par][:, 0:nhc * 512], nhc), srcw)
                    te.wait_ge(sGP, w_rec)
                    for hcj in range(nhc):
                        hc = hc0 + hcj
                        for ti in range(4):
                            last = te.matmul(
                                banks[ti][:],
                                iqT[:, ti * HID + hc * 128: ti * HID + (hc + 1) * 128],
                                wsb[par][:, hcj * 512:(hcj + 1) * 512],
                                start=(hc == 0), stop=(hc == NHCH - 1))
                    last.then_inc(sTE, 1)
                    wsb_free[par] = cTE.inc()
                t_ds = cTE.v
                for ti in range(4):
                    t = 4 * G + ti
                    pp = ti % 2
                    se.wait_ge(sTE, t_ds)
                    if gstg_free[pp] is not None:
                        se.wait_ge(sSY, gstg_free[pp])
                    se.activation(out=gstg[pp][:], in_=banks[ti][:], func=AF.Silu,
                                  scale=s.s2r[:, t:t + 1]).then_inc(sSE, 1)
                    s_g = cSE.inc()
                    bank2_free[4 * (ds % 2) + ti] = (sSE, s_g)
                    sy.wait_ge(sSE, s_g)
                    rec = s.dma(sy, s.gsp[t * 128:(t + 1) * 128, ds * 512:(ds + 1) * 512],
                                gstg[pp][:])
                    gstg_free[pp] = rec
            s.free(*[f"pb2{G}_{i}" for i in reversed(range(8))])
            s.free(f"iq_tok_{G}", f"gated_{G}", f"iqT_{G}")

        if "gsp" in s.taps:
            sy.wait_ge(sSY, s.cSY.v)
            s.dma(sy, s.taps["gsp"][:, :], s.gsp[:, :])
        s.p4 = dict(te=cTE.v, ve=cVE.v, se=cSE.v, sy=s.cSY.v, gp=s.cGP.v)
        s.free("tmp21", "tmp20", "x1h", "xqG", "wsb2", "wsb1", "wsb0")

    def p5_attention(self):
        s = self
        s.phase_barrier()
        te, ve, se, sy, gp = s.te, s.ve, s.se, s.sy, s.gp
        sTE, sVE, sSE, sSY, sGP, sCC = s.sTE, s.sVE, s.sSE, s.sSY, s.sGP, s.sCC
        cTE, cVE, cSE = s.cTE, s.cVE, s.cSE

        s.c_onesr = s.sb("c_onesr", [1, 128], F32)
        s.c_onesr16 = s.sb("c_onesr16", [1, 128], BF16)
        s.ve.memset(s.c_onesr[:], 1.0)
        s.ve.memset(s.c_onesr16[:], 1.0).then_inc(s.sVE, 1)
        s.cVE.inc()
        s.wo_s = s.sb("wo_s", [128, NCH * D], BF16)
        s.c_bo = s.sb("c_bo", [128, D], F32)
        s.attnT = s.sb("attnT", [128, NCH * TOK], BF16)
        s.c_masks = s.sb("c_masks", [128, 2048], F32)
        s.c_alibi = s.sb("c_alibi", [128, 512], F32)
        s.dma(sy, s.c_bo[:], s.bo_d[:, :])
        s.dma(sy, s.c_alibi[:], s.alibi_d[:, :])
        mask_rec = s.dma(sy, s.c_masks[:], s.masks_d[:, :])

        wo_recs = []
        for c in range(NCH):
            wo_recs.append(s.dma(gp, s.wo_s[:, c * D:(c + 1) * D],
                                 s.wo_d[c * 128:(c + 1) * 128, :]))
        s.wo_rec = wo_recs[-1]

        aqrow = [s.sb(f"aqrow{i}", [1, 512], BF16) for i in range(4)]
        kbuf = [s.sb(f"kbuf{i}", [128, 4096], BF16) for i in range(2)]
        vbuf = [s.sb(f"vbuf{i}", [128, 4096], BF16) for i in range(2)]
        qbuf = [s.sb(f"qbuf{i}", [128, TOK], BF16) for i in range(2)]
        phat = s.sb("phat", [128, 4 * 512], BF16)
        Lrow = s.sb("Lrow", [128, 512], F32)
        bcs = s.sb("bcs", [128, 512], F32)
        atmp = s.sb("atmp", [128, 512], F32)
        zb = [s.ps(f"zb{i}", [128, 512], F32) for i in range(2)]
        pv = [s.ps(f"pv{i}", [128, 512], F32) for i in range(2)]
        lb = [s.ps(f"lb{i}", [128, 512], F32) for i in range(2)]
        bcb = s.ps("bcb", [128, 512], F32)
        te.wait_ge(s.sSE, s.cSE.v)
        te.wait_ge(s.sVE, s.cVE.v)

        kv_rec = []
        self._h_done_target = [None] * H
        te_base = cTE.v
        per_h_te = 2 * 36 + 2
        for h in range(H):
            if h >= 2:
                sy.wait_ge(sTE, te_base + (h - 1) * per_h_te)
            if h == 0:
                sy.wait_ge(sCC, s.cc_v)
            for qs01 in range(2):
                s.dma(sy, aqrow[(2 * h + qs01) % 4][:],
                      s.aq_d[2 * h + qs01:2 * h + qs01 + 1, :])
            rec = s.dma(sy, qbuf[h % 2][:], s.qts[h * 128:(h + 1) * 128, :])
            for ck in range(8):
                rank, half = _crh(ck)
                rec = s.dma(sy, kbuf[h % 2][:, ck * 512:(ck + 1) * 512],
                            s.kvout[4096 * rank + h * 128: 4096 * rank + (h + 1) * 128,
                                    512 * half: 512 * (half + 1)])
                voutR = s.kvout[4096 * rank + 2048: 4096 * rank + 4096, :].rearrange(
                    "(a two) c -> a (two c)", two=2)
                vsrc = _pck(voutR[512 * half: 512 * (half + 1),
                                  h * 128:(h + 1) * 128])
                rec = s.dma(sy, _sck(vbuf[h % 2][:, ck * 512:(ck + 1) * 512], 4), vsrc)
            kv_rec.append(rec)

        ve.wait_ge(sSY, mask_rec)
        for eng in (te, se, ve):
            eng.wait_ge(sSY, s.rid_rec)
        idx = s.nc.values_load(s.rid_s[0:1, 0:1],
                               engines=[te.engine, se.engine, ve.engine],
                               min_val=0, max_val=3, skip_runtime_bounds_check=True)
        snap = (cTE.v, cVE.v, cSE.v)
        ends = []
        for r in s.nc.Switch(engines=[te, se, ve._ve], index=[idx, idx, idx], n=4):
            cTE.v, cVE.v, cSE.v = snap
            self._emit_attn_variant(r, kbuf, vbuf, qbuf, phat, Lrow, bcs, atmp,
                                    zb, pv, lb, bcb, kv_rec, aqrow)
            ends.append((cTE.v, cVE.v, cSE.v))
        assert all(e == ends[0] for e in ends), f"variant counts differ: {ends}"
        cTE.v, cVE.v, cSE.v = ends[0]
        assert cTE.v - te_base == H * per_h_te, (cTE.v - te_base, H * per_h_te)

        if "attnT" in s.taps:
            sy.wait_ge(sSE, cSE.v)
            s.dma(sy, s.taps["attnT"][:, :], s.attnT[:])
        for nm, buf in (("kbuf1", kbuf[1]), ("vbuf1", vbuf[1]), ("qbuf1", qbuf[1]),
                        ("lrow", Lrow), ("bcs", bcs), ("atmp", atmp),
                        ("phat", phat)):
            if nm in s.taps:
                sy.wait_ge(sSE, cSE.v)
                sy.wait_ge(sVE, cVE.v)
                s.dma(sy, s.taps[nm][:, :], buf[:])
        s.p5 = dict(te=cTE.v, ve=cVE.v, se=cSE.v, sy=s.cSY.v, gp=s.cGP.v)
        s.free("bcb", "lb1", "lb0", "pv1", "pv0", "zb1", "zb0", "atmp", "bcs",
               "Lrow", "phat", "qbuf1", "qbuf0", "vbuf1", "vbuf0", "kbuf1", "kbuf0")

    def _emit_attn_variant(self, r, kbuf, vbuf, qbuf, phat, Lrow, bcs, atmp,
                           zb, pv, lb, bcb, kv_rec, aqrow):
        s = self
        te, ve, se = s.te, s.ve, s.se
        sTE, sVE, sSE, sSY = s.sTE, s.sVE, s.sSE, s.sSY
        cTE, cVE, cSE = s.cTE, s.cVE, s.cSE

        pvl_free = [None, None]
        bcb_free = None
        atmp_free = None
        prev_exp_last = None
        for h in range(H):
            te.wait_ge(sSY, kv_rec[h])
            for qs01 in range(2):
                A = 4 * r + 4 if qs01 == 0 else 32 - 4 * r
                g2 = (2 * h + qs01) % 2
                exp_rec = {}
                pvl_rec = {}
                phat_rec = {}

                def emit_z(kb):
                    zi = kb % 2
                    if kb >= 2:
                        te.wait_ge(sSE, exp_rec[zi])
                    elif prev_exp_last is not None:
                        te.wait_ge(sSE, prev_exp_last)
                    te.matmul(zb[zi][:], s.c_onesr16[0:1, 0:128],
                              aqrow[(2 * h + qs01) % 4][0:1, :],
                              start=True, stop=False)
                    te.matmul(zb[zi][:], kbuf[h % 2][:, kb * 128:(kb + 1) * 128],
                              qbuf[h % 2][:, qs01 * 512:(qs01 + 1) * 512],
                              start=False, stop=True).then_inc(sTE, 1)
                    z_rec = cTE.inc()
                    if kb >= A - 4:
                        v = kb - (A - 4)
                        ve.wait_ge(sTE, z_rec)
                        ve.tensor_add(zb[zi][:], zb[zi][:],
                                      s.c_masks[:, v * 512:(v + 1) * 512]).then_inc(sVE, 1)
                        z_gate = (sVE, cVE.inc())
                    else:
                        z_gate = (sTE, z_rec)
                    se.wait_ge(z_gate[0], z_gate[1])
                    if kb - 4 in pvl_rec:
                        se.wait_ge(sTE, pvl_rec[kb - 4])
                    elif kb < 4 and pvl_free[1 - g2] is not None:
                        se.wait_ge(sVE, pvl_free[1 - g2])
                    se.activation(out=phat[:, (kb % 4) * 512:(kb % 4 + 1) * 512],
                                  in_=zb[zi][:], func=AF.Exp,
                                  bias=s.c_alibi[:, h * 32 + kb: h * 32 + kb + 1],
                                  scale=1.0).then_inc(sSE, 1)
                    exp_rec[zi] = cSE.inc()
                    phat_rec[kb] = (sSE, exp_rec[zi])

                emit_z(0)
                for kb in range(A):
                    if kb + 1 < A:
                        emit_z(kb + 1)
                    sem, rec = phat_rec[kb]
                    te.wait_ge(sem, rec)
                    if kb == 0 and pvl_free[g2] is not None:
                        te.wait_ge(sVE, pvl_free[g2])
                    te.matmul(pv[g2][:], vbuf[h % 2][:, kb * 128:(kb + 1) * 128],
                              phat[:, (kb % 4) * 512:(kb % 4 + 1) * 512],
                              start=(kb == 0), stop=(kb == A - 1))
                    te.matmul(lb[g2][0:1, :], s.c_ones[:, 0:1],
                              phat[:, (kb % 4) * 512:(kb % 4 + 1) * 512],
                              start=(kb == 0), stop=(kb == A - 1)).then_inc(sTE, 1)
                    pvl_rec[kb] = cTE.inc()
                ve.wait_ge(sTE, pvl_rec[A - 1])
                if atmp_free is not None:
                    ve.wait_ge(sSE, atmp_free)
                ve.tensor_copy(Lrow[0:1, 0:512], lb[g2][0:1, :])
                ve.reciprocal(Lrow[0:1, 0:512], Lrow[0:1, 0:512]).then_inc(sVE, 1)
                l_rec = cVE.inc()
                te.wait_ge(sVE, l_rec)
                te.matmul(bcb[:], s.c_onesr[0:1, :], Lrow[0:1, 0:512],
                          start=True, stop=True).then_inc(sTE, 1)
                bc_rec = cTE.inc()
                ve.wait_ge(sTE, bc_rec)
                ve.tensor_copy(bcs[:], bcb[:]).then_inc(sVE, 1)
                bcb_free = cVE.inc()
                ve.scalar_tensor_tensor(atmp[:], pv[g2][:], 1.0, bcs[:],
                                        ALU.bypass, ALU.mult).then_inc(sVE, 1)
                ev_rec = cVE.inc()
                pvl_free[g2] = ev_rec
                se.wait_ge(sVE, ev_rec)
                se.activation(out=s.attnT[:, h * TOK + qs01 * 512: h * TOK + (qs01 + 1) * 512],
                              in_=atmp[:], func=AF.Identity,
                              bias=s.c_bqkv[:, 32 + h:33 + h], scale=1.0).then_inc(sSE, 1)
                atmp_free = cSE.inc()
                prev_exp_last = exp_rec[(A - 1) % 2]

    def p6_combine(self):
        s = self
        s.phase_barrier()
        te, ve, se, sy = s.te, s.ve, s.se, s.sy
        sTE, sVE, sSE, sSY, sGP = s.sTE, s.sVE, s.sSE, s.sSY, s.sGP
        cTE, cVE, cSE = s.cTE, s.cVE, s.cSE

        xrl = [s.sb(f"xrl{i}", [128, D], F32) for i in range(2)]
        grl = [s.sb(f"grl{i}", [128, D], F32) for i in range(2)]
        ostg = [s.sb(f"ostg{i}", [128, D], F32) for i in range(2)]
        p6 = [s.ps(f"p6_{i}", [128, 512], F32) for i in range(8)]

        te.wait_ge(sSE, s.p5["se"])
        te.wait_ge(sGP, s.wo_rec)
        rl_free = [None, None]
        o_free = [None, None]
        bank6_free = [None] * 8
        for t in range(NT):
            pp = t % 2
            if rl_free[pp] is not None:
                sy.wait_ge(sVE, rl_free[pp])
            s.dma(sy, xrl[pp][:], s.x_d[t * 128:(t + 1) * 128, :])
            x_rec = s.dma(sy, grl[pp][:], s.gsp[t * 128:(t + 1) * 128, :])
            banks = [p6[4 * pp + i] for i in range(4)]
            for i in range(4):
                if bank6_free[4 * pp + i] is not None:
                    te.wait_ge(sVE, bank6_free[4 * pp + i])
            last = None
            for c in range(NCH):
                a_ap = s.attnT[:, c * TOK + t * 128: c * TOK + (t + 1) * 128]
                for ds in range(4):
                    last = te.matmul(banks[ds][:], a_ap,
                                     s.wo_s[:, c * D + ds * 512: c * D + (ds + 1) * 512],
                                     start=(c == 0), stop=(c == NCH - 1))
            last.then_inc(sTE, 1)
            t_rec = cTE.inc()
            ve.wait_ge(sTE, t_rec)
            ve.wait_ge(sSY, x_rec)
            if o_free[pp] is not None:
                ve.wait_ge(sSY, o_free[pp])
            lastve = None
            for ds in range(4):
                sl = slice(ds * 512, (ds + 1) * 512)
                ve.scalar_tensor_tensor(ostg[pp][:, sl], banks[ds][:], 1.0,
                                        s.c_bo[:, sl], ALU.bypass, ALU.add)
                ve.tensor_mul(ostg[pp][:, sl], ostg[pp][:, sl], grl[pp][:, sl])
                lastve = ve.tensor_add(ostg[pp][:, sl], ostg[pp][:, sl], xrl[pp][:, sl])
            lastve.then_inc(sVE, 1)
            v_rec = cVE.inc()
            rl_free[pp] = v_rec
            for i in range(4):
                bank6_free[4 * pp + i] = v_rec
            sy.wait_ge(sVE, v_rec)
            o_free[pp] = s.dma(sy, s.out_d[t * 128:(t + 1) * 128, :], ostg[pp][:])

    def finish(self):
        s = self
        s.sy.wait_ge(s.sSY, s.cSY.v)
        s.sy.wait_ge(s.sGP, s.cGP.v)
        s.sy.wait_ge(s.sTE, s.cTE.v)
        s.sy.wait_ge(s.sVE, s.cVE.v)
        s.sy.wait_ge(s.sSE, s.cSE.v)


def _host_consts():
    slopes = np.array([1.0 / 2 ** (k / 2) for k in range(H)], np.float32)
    alibi = np.zeros((128, 512), np.float32)
    j = np.arange(128, dtype=np.float32)
    for h in range(H):
        for kb in range(32):
            alibi[:, h * 32 + kb] = slopes[h] * (128 * kb + j) - SHIFT
    masks = np.zeros((128, 2048), np.float32)
    q = np.arange(512, dtype=np.float32)
    for v in range(4):
        masks[:, v * 512:(v + 1) * 512] = np.where(
            j[:, None] + 128 * v <= q[None, :], 0.0, -1e30)
    return alibi, masks, np.eye(128, dtype=ml_dtypes.bfloat16)


def make_in_maps(L0_attn_x, L0_attn_Wqkv, L0_attn_bqkv, L0_attn_Wo, L0_attn_bo,
                 L0_ffn_w13, L0_ffn_w2, L0_ffn_norm_w):
    x = np.asarray(L0_attn_x, np.float32)
    wqkv = np.ascontiguousarray(np.asarray(L0_attn_Wqkv, np.float32))
    bqkvT = np.ascontiguousarray(np.asarray(L0_attn_bqkv, np.float32).reshape(48, 128).T)
    wo = np.ascontiguousarray(np.asarray(L0_attn_Wo, np.float32))
    bo_rep = np.ascontiguousarray(
        np.broadcast_to(np.asarray(L0_attn_bo, np.float32), (128, D)))
    w13t = np.ascontiguousarray(np.asarray(L0_ffn_w13, np.float32).T)
    w2t = np.ascontiguousarray(np.asarray(L0_ffn_w2, np.float32).T)
    nw_rep = np.ascontiguousarray(
        np.broadcast_to(np.asarray(L0_ffn_norm_w, np.float32), (128, HID)))
    alibi, masks, ident = _host_consts()
    slopes = np.array([1.0 / 2 ** (k / 2) for k in range(H)], np.float32)
    qv = np.arange(512, dtype=np.float32)
    in_maps = []
    for core in range(8):
        b, r = core // 4, core % 4
        aq = np.zeros((32, 512), np.float32)
        for h in range(H):
            for qs01 in range(2):
                chunk = r if qs01 == 0 else 7 - r
                aq[2 * h + qs01] = -slopes[h] * (chunk * 512 + qv)
        aq16 = aq.astype(ml_dtypes.bfloat16)
        x_own = np.ascontiguousarray(np.concatenate(
            [x[b, 512 * r:512 * (r + 1)], x[b, 512 * (7 - r):512 * (8 - r)]], axis=0))
        in_maps.append({
            "rid": np.array([[r]], np.int32), "x": x_own, "wqkv": wqkv,
            "bqkvT": bqkvT, "wo": wo, "bo_rep": bo_rep, "w13t": w13t,
            "w2t": w2t, "nw_rep": nw_rep, "alibi": alibi, "masks": masks,
            "ident": ident, "aq": aq16,
        })
    return in_maps


def kernel(L0_attn_x, L0_attn_Wqkv, L0_attn_bqkv, L0_attn_Wo, L0_attn_bo,
           L0_ffn_w13, L0_ffn_w2, L0_ffn_norm_w):
    in_maps = make_in_maps(L0_attn_x, L0_attn_Wqkv, L0_attn_bqkv, L0_attn_Wo,
                           L0_attn_bo, L0_ffn_w13, L0_ffn_w2, L0_ffn_norm_w)
    nc = MK().build()
    res = run_bass_kernel_spmd(nc, in_maps, core_ids=list(range(8)))
    out = np.empty((B, S, D), np.float32)
    for core in range(8):
        b, r = core // 4, core % 4
        o = res.results[core]["out"]
        out[b, 512 * r:512 * (r + 1)] = o[:512]
        out[b, 512 * (7 - r):512 * (8 - r)] = o[512:]
    return out
